# version 77
# baseline (speedup 1.0000x reference)
import numpy as np
import ml_dtypes
from contextlib import ExitStack

import concourse.bass as bass
import concourse.mybir as mybir
from concourse.bass_utils import run_bass_kernel_spmd

F32 = mybir.dt.float32
BF16 = mybir.dt.bfloat16
I32 = mybir.dt.int32
AF = mybir.ActivationFunctionType
ALU = mybir.AluOpType
AX = mybir.AxisListType

B, S, D = 16, 2048, 1024
NCORES = 8
SEQ_PC = B // NCORES
T = 128
TPS = S // T
DC = D // 128
HN, DH = 16, 64
HF = 4096
NEG = -1.0e9
RK = 640.5

_SB_BASE = 16640
_SB_TOP = 229376


class _Sem:
    __slots__ = ("name", "n", "h")

    def __init__(self, name):
        self.name = name
        self.n = 0
        self.h = None


def _off(ap):
    return ap.tensor.manual_sbuf_range[0]


def _build(nt, mode="full", cut=None):
    ntok = nt * T
    nc = bass.Bass("TRN2", target_bir_lowering=False, debug=False)

    xc = nc.dram_tensor("xc", [ntok, D], F32, kind="ExternalInput").ap()
    wqkv_d = nc.dram_tensor("wqkv", [D, 3 * D], F32, kind="ExternalInput").ap()
    wout_d = nc.dram_tensor("wout", [D, D], F32, kind="ExternalInput").ap()
    w1_d = nc.dram_tensor("w1", [HF, D], F32, kind="ExternalInput").ap()
    w2_d = nc.dram_tensor("w2", [D, HF], F32, kind="ExternalInput").ap()
    b1_d = nc.dram_tensor("b1", [HF], F32, kind="ExternalInput").ap()
    b2_d = nc.dram_tensor("b2", [D], F32, kind="ExternalInput").ap()
    identb_d = nc.dram_tensor("identb", [128, 128], BF16, kind="ExternalInput").ap()
    identf_d = nc.dram_tensor("identf", [128, 128], F32, kind="ExternalInput").ap()
    masks_d = nc.dram_tensor("masks", [128, 512], BF16, kind="ExternalInput").ap()
    y_d = nc.dram_tensor("y", [ntok, D], F32, kind="ExternalOutput").ap()
    x1_d = nc.dram_tensor("x1sp", [ntok, D], F32, kind="Internal").ap()
    w2t_d = nc.dram_tensor("w2td", [128, 32 * 1024], BF16, kind="Internal").ap()
    bbf_d = nc.dram_tensor("bbfd", [HF + D], BF16, kind="Internal").ap()

    cur = [_SB_BASE]

    def sb(name, p, n, dt, at=None):
        size = n * mybir.dt.size(dt)
        o = cur[0] if at is None else at
        o = (o + 31) & ~31
        h = nc._state.alloc_sbuf_tensor_at(name, [p, n], dt, offset=o)
        if at is None:
            cur[0] = o + size
        return h.ap()

    identb = sb("identb", 128, 128, BF16)
    identf = sb("identf", 128, 128, F32)
    masks = sb("masks", 128, 512, BF16)
    b1bf = sb("b1bf", 1, HF, BF16)
    b2bf = sb("b2bf", 1, D, BF16)
    cells = [sb("cells0", 128, 64, F32), sb("cells1", 128, 64, F32)]
    statsN = sb("statsN", 128, nt, F32)
    statsR = sb("statsR", 128, nt, F32)
    _c4 = sb("c4", 128, 5, F32)
    c_eps = _c4[:, 0:1]
    c_nh = _c4[:, 1:2]
    c_ph = _c4[:, 2:3]
    ones_col = _c4[:, 3:4]
    c_rk = _c4[:, 4:5]
    ones_row = sb("ones_row", 1, 128, F32)
    _b4 = sb("b4", 128, 4, F32)
    bc_r1 = _b4[:, 0:1]
    bc_ws1 = _b4[:, 1:2]
    bc_r2 = _b4[:, 2:3]
    bc_ws2 = _b4[:, 3:4]
    accw = sb("accw", 128, 8, F32)
    wsc = sb("wsc", 1, 8, F32)
    x1buf = [sb("x1b0", 128, D, F32), sb("x1b1", 128, D, F32)]
    outbuf = sb("outbuf", 128, D, F32)
    invs1row = sb("i1row", 1, 128, BF16)
    invs2row = sb("i2row", 1, 128, BF16)

    A0 = (cur[0] + 63) & ~63
    cur[0] = A0
    wqkv_sb = sb("wqkv_sb", 128, DC * 3 * D, BF16)
    wout_sb = sb("wout_sb", 128, DC * D, BF16)
    xin = [sb("xin0", 128, D, F32), sb("xin1", 128, D, F32)]
    xln = sb("xln", 128, D, BF16)
    xlnT = sb("xlnT", 128, D, BF16)
    q_sb = sb("q_sb", 128, D, BF16)
    k_sb = sb("k_sb", 128, D, BF16)
    qT = sb("qT", 64, 2048, BF16)
    kT_hist = sb("kT_hist", 64, 2 * 2048, BF16)
    v_hist = sb("v_hist", 128, 2 * D, BF16)
    probs = [sb("probs0", 128, 1024, BF16), sb("probs1", 128, 1024, BF16)]
    probsn = probs
    probsT_b = sb("probsT", 128, 1024, BF16)
    probsT = [probsT_b, probsT_b]
    attnout = sb("attnout", 128, 1024, BF16)
    wstage = [sb("wstage0", 128, 512, F32), sb("wstage1", 128, 512, F32),
              sb("wstage2", 128, 512, F32), sb("wstage3", 128, 512, F32)]
    sgnA = sb("sgnA", 128, 512, BF16)
    sgnB = sb("sgnB", 128, 512, BF16)
    wqch = [sb("wqch0", 128, 512, BF16), sb("wqch1", 128, 512, BF16)]
    w2st = [sb("w2st0", 128, 512, BF16), sb("w2st1", 128, 512, BF16)]
    A_END = cur[0]

    w1t_at = (229368 - DC * HF * 2) & ~63
    W1qT = sb("W1qT", 128, DC * HF, BF16, at=w1t_at)
    assert A_END <= w1t_at, (A_END, w1t_at)

    cur[0] = A0
    W2qT = sb("W2qT", 128, 32 * D, BF16)
    h_sb = sb("h_sb", 128, HF, F32)
    hq = sb("hq", 128, HF, BF16)
    hqT = sb("hqT", 128, HF, BF16)
    t1b = sb("t1a", 128, 1024, F32)
    tib = sb("tib", 128, 1024, I32)
    tfb = sb("tfb", 128, 1024, F32)
    x1c = sb("x1c", 128, D, F32)
    xq = sb("xq", 128, D, BF16)
    xqT = sb("xqT", 128, D, BF16)
    F_END = cur[0]
    assert F_END <= w1t_at, (F_END, w1t_at)

    wstg_big = [
        sb("wstgA", 128, 1536, F32, at=w1t_at),
        sb("wstgB", 128, 1536, F32, at=w1t_at + 1536 * 4),
    ]
    bstage = sb("bstage", 128, 32, F32, at=w1t_at + 1536 * 8)
    bstage_bf = sb("bstageb", 128, 32, BF16, at=w1t_at + 1536 * 8 + 128)

    with ExitStack() as ctx:
        psA = ctx.enter_context(nc.psum_tensor("psA", [128, 1024], F32)).ap()
        psB = ctx.enter_context(nc.psum_tensor("psB", [128, 1024], F32)).ap()
        psC = ctx.enter_context(nc.psum_tensor("psC", [128, 1024], F32)).ap()
        psD = ctx.enter_context(nc.psum_tensor("psD", [128, 1024], BF16)).ap()
        psF = ctx.enter_context(nc.psum_tensor("psF", [128, 512], F32)).ap()

        sP = _Sem("sP"); sV = _Sem("sV"); sA = _Sem("sA"); sG = _Sem("sG")
        dX = _Sem("dX"); dS = _Sem("dS"); dL = _Sem("dL"); dO = _Sem("dO")
        dC = _Sem("dC"); dY = _Sem("dY")
        dW = [_Sem("dW0"), _Sem("dW1")]
        dP = [_Sem("dP0"), _Sem("dP1"), _Sem("dP2"), _Sem("dP3")]
        dB = [_Sem("dB0"), _Sem("dB1")]
        SEMS = [sP, sV, sA, sG, dX, dS, dL, dO, dC, dY] + dW + dP + dB
        for s in SEMS:
            s.h = ctx.enter_context(nc.semaphore(s.name))

        OPS = []
        _ES = {"dve": sV, "act": sA, "pool": sG}

        def op(ek, fn, waits=(), inc=None, amt=1, is_dma=False):
            if fn is not None and not is_dma and ek in _ES:
                inc = _ES[ek]
                amt = 1
            OPS.append((ek, [w for w in waits if w is not None and w[1] > 0],
                        fn, inc, amt, is_dma))
            if inc is not None:
                inc.n += amt
                return inc.n
            return None

        def dma(ek, out_ap, in_ap, sem, waits=()):
            w = [(sem, sem.n)] + list(waits)
            return op(ek, (lambda e, o=out_ap, i=in_ap: e.dma_start(out=o, in_=i)),
                      w, sem, 16, is_dma=True)

        ms = {}

        def tickle(n=2):
            for _ in range(n):
                op("pe", lambda e: e.ldweights(identb))

        def tickle_train(sem, target, step=3, n=4, per=3):
            for k in range(n, 0, -1):
                v = target - k * step
                if v > 0:
                    for _ in range(per):
                        op("pe", lambda e: e.ldweights(identb),
                           waits=[(sem, v)])

        psum_free = {"A": None, "B": None, "C": None, "D": None, "F": None}
        psD2 = psF.bitcast(BF16)
        _stg = {"t": 0}

        def next_stage():
            _stg["t"] ^= 1
            return (psD, "D") if _stg["t"] else (psD2, "F")
        eng_end = {k: [0] for k in ("pe", "dve", "act", "pool")}
        ENG_SEM = {"pe": sP, "dve": sV, "act": sA, "pool": sG}

        def fence(gt):
            if gt < 2:
                return
            for ek in ("pe", "dve", "act", "pool"):
                waits = [(ENG_SEM[o], eng_end[o][gt - 1])
                         for o in ("pe", "dve", "act", "pool") if o != ek]
                op(ek, None, waits=waits)

        def tile_end():
            for ek in ("pe", "dve", "act", "pool"):
                eng_end[ek].append(ENG_SEM[ek].n)

        def _memset(ap_, val):
            return lambda e, a=ap_, v=val: e.memset(a, v)

        op("dve", _memset(c_eps, 1e-5))
        op("dve", _memset(c_nh, -0.5))
        op("dve", _memset(c_ph, 0.5))
        op("dve", _memset(c_rk, RK))
        op("dve", _memset(ones_col, 1.0))
        op("dve", _memset(ones_row, 1.0))
        op("dve", _memset(accw, 0.0))
        op("dve", _memset(kT_hist, 0.0))
        op("dve", _memset(v_hist, 0.0), inc=sV)
        ms["memset"] = sV.n

        dma("act", identb, identb_d, dC)
        dma("act", identf, identf_d, dC)
        dma("act", masks, masks_d, dC)
        ms["consts_ld"] = dC.n

        _w = dma("act", bstage, b1_d.rearrange("(p k) -> p k", p=128), dY)
        _a = op("act", lambda e: e.activation(out=bstage_bf, in_=bstage,
                                              func=AF.Copy),
                waits=[(dY, _w)], inc=sA)
        _b = dma("act", bbf_d[0:HF].rearrange("(p k) -> p k", p=128), bstage_bf,
                 dY, waits=[(sA, _a)])
        dma("act", b1bf, bbf_d[0:HF].rearrange("(a k) -> a k", a=1), dY,
            waits=[(dY, _b)])
        _w = dma("act", bstage[:, 0:8], b2_d.rearrange("(p k) -> p k", p=128),
                 dY, waits=[(dY, _b)])
        _a = op("act", lambda e: e.activation(out=bstage_bf[:, 0:8],
                                              in_=bstage[:, 0:8], func=AF.Copy),
                waits=[(dY, _w)], inc=sA)
        _b = dma("act", bbf_d[HF:HF + D].rearrange("(p k) -> p k", p=128),
                 bstage_bf[:, 0:8], dY, waits=[(sA, _a)])
        dma("act", b2bf, bbf_d[HF:HF + D].rearrange("(a k) -> a k", a=1), dY,
            waits=[(dY, _b)])
        ms["bias_ld"] = dY.n

        xin_ld = {}
        for t0 in range(min(2, nt)):
            xin_ld[t0] = dma("syncq", xin[t0 % 2], xc[t0 * T:(t0 + 1) * T, :], dX)

        wc_v, wc_a = 0, 0
        stg_rd = {0: None, 1: None}
        pieces = []
        for i in range(8):
            pieces.append((wqkv_d[i * 128:(i + 1) * 128, 0:1536],
                           wqkv_sb[:, i * 3 * D:i * 3 * D + 1536], 1536))
        for i in range(8):
            pieces.append((wqkv_d[i * 128:(i + 1) * 128, 1536:3072],
                           wqkv_sb[:, i * 3 * D + 1536:
                                   i * 3 * D + 3072], 1536))
        for i in range(8):
            pieces.append((wout_d[i * 128:(i + 1) * 128, :],
                           wout_sb[:, i * D:(i + 1) * D], 1024))
        for j, (src, dst, n) in enumerate(pieces):
            buf = wstg_big[j % 2]
            ld = dma("act", buf[:, 0:n], src, dW[j % 2], waits=[stg_rd[j % 2]])
            ek = "dve" if j % 2 == 0 else "act"
            sm = sV if ek == "dve" else sA
            if ek == "dve":
                fn = (lambda e, o=dst, i_=buf[:, 0:n]:
                      e.tensor_copy(out=o, in_=i_))
            else:
                fn = (lambda e, o=dst, i_=buf[:, 0:n]: e.copy(out=o, in_=i_))
            v = op(ek, fn, waits=[(dW[j % 2], ld)], inc=sm)
            stg_rd[j % 2] = (sm, v)
            if ek == "dve":
                wc_v = v
            else:
                wc_a = v
            if j == 7:
                ms["wcast_q"] = (wc_v, wc_a)
            elif j == 15:
                ms["wcast_kv"] = (wc_v, wc_a)
        ms["wcast_v"] = wc_v
        ms["wcast_a"] = wc_a

        if mode == "S":
            v = op("dve", lambda e: e.tensor_copy(out=outbuf, in_=wqkv_sb[:, 0:D]),
                   waits=[(sV, wc_v), (sA, wc_a), (dX, dX.n), (dY, ms["bias_ld"]),
                          (dC, ms["consts_ld"])])
            o = dma("pool", y_d[0:T, :], outbuf, dO, waits=[(sV, v)])
            op("syncq", None, waits=[(dO, dO.n)])
            _emit_all(nc, OPS, _ES, cut)
            return nc

        prep_list = ([("a1", i) for i in range(64)] +
                     [("a2", i) for i in range(64)] +
                     [("f1",), ("f2",)] +
                     [("q1", i) for i in range(64)] +
                     [("q2", i) for i in range(64)])
        if mode == "A":
            prep_list = []
        prep_by_tile = [[] for _ in range(nt)]
        for j, task in enumerate(prep_list):
            prep_by_tile[min(nt - 1, j * nt // len(prep_list))].append(task)

        pst = {
            "stg_rd": {0: None, 1: None, 2: None, 3: None},
            "stg_ld": {},
            "acc": {"a1": 0, "a2": 0},
            "sgn_rd": None,
            "wq_rd": {0: None, 1: None},
            "w2st_rd": {0: None, 1: None},
            "ndma": 0,
            "cursor": 0,
            "pend": [],
        }

        def prep_src(kind, i):
            if kind in ("a1", "q1"):
                return w1_d[(i // 2) * 128:(i // 2 + 1) * 128,
                            (i % 2) * 512:(i % 2) * 512 + 512]
            return w2_d[(i // 8) * 128:(i // 8 + 1) * 128,
                        (i % 8) * 512:(i % 8) * 512 + 512]

        def prep_issue_next_dma():
            while pst["cursor"] < len(prep_list):
                task = prep_list[pst["cursor"]]
                pst["cursor"] += 1
                if task[0] in ("f1", "f2"):
                    continue
                slot = pst["ndma"] % 4
                pst["ndma"] += 1
                ld = dma("syncq", wstage[slot], prep_src(task[0], task[1]),
                         dP[slot], waits=[pst["stg_rd"][slot]])
                pst["stg_ld"][task] = (slot, ld)
                return

        def prep_fin(grp):
            kind = "a1" if grp == 0 else "a2"
            acc_slot = grp * 2 + pst["acc"][kind] % 2
            nelem = float(HF * D)
            pv = op("pe", lambda e, a=accw[:, acc_slot:acc_slot + 1]:
                    e.matmul(out=psF[0:1, 0:1], lhsT=a, rhs=ones_col,
                             start=True, stop=True),
                    waits=[(sV, ms["abs_%s_last" % kind]), psum_free["F"],
                           (sV, ms["memset"])], inc=sP)
            op("dve", lambda e: e.tensor_copy(out=wsc[0:1, 0:1],
                                              in_=psF[0:1, 0:1]),
               waits=[(sP, pv)])
            op("dve", lambda e, n=nelem: e.tensor_scalar(
                out=wsc[0:1, 1:2], in0=wsc[0:1, 0:1],
                scalar1=1.0 / n, scalar2=1e-5, op0=ALU.mult, op1=ALU.max))
            vr = op("dve", lambda e: e.reciprocal(out=wsc[0:1, 2:3],
                                                  in_=wsc[0:1, 1:2]), inc=sV)
            pv = op("pe", lambda e: e.matmul(out=psF[:, 0:1], lhsT=ones_row,
                                             rhs=wsc[0:1, 2:3],
                                             start=True, stop=True),
                    waits=[(sV, vr)], inc=sP)
            pv = op("pe", lambda e: e.matmul(out=psF[:, 1:2], lhsT=ones_row,
                                             rhs=wsc[0:1, 1:2],
                                             start=True, stop=True), inc=sP)
            dst_r = bc_r1 if grp == 0 else bc_r2
            dst_w = bc_ws1 if grp == 0 else bc_ws2
            op("dve", lambda e, d=dst_r: e.tensor_copy(out=d, in_=psF[:, 0:1]),
               waits=[(sP, pv)])
            v = op("dve", lambda e, d=dst_w: e.tensor_scalar(
                out=d, in0=psF[:, 1:2], scalar1=1.0 / 127.0, scalar2=None,
                op0=ALU.mult), inc=sV)
            psum_free["F"] = (sV, v)
            ms["wsc_%d" % grp] = v

        def prep_emit(task):
            kind = task[0]
            if kind == "f1":
                prep_fin(0); return
            if kind == "f2":
                prep_fin(1); return
            i = task[1]
            slot, ld = pst["stg_ld"][task]
            if kind in ("a1", "a2"):
                cnt = pst["acc"][kind]
                g = 0 if kind == "a1" else 1
                src_s = g * 2 + cnt % 2
                dst_s = g * 2 + (cnt + 1) % 2
                op("dve", lambda e, s=wstage[slot]: e.tensor_reduce(
                    out=accw[:, 4:5], in_=s, axis=AX.X, op=ALU.add,
                    apply_absolute_value=True),
                    waits=[(dP[slot], ld), (sV, ms["memset"])])
                v = op("dve", lambda e, a=accw[:, src_s:src_s + 1],
                       b=accw[:, dst_s:dst_s + 1]:
                       e.tensor_tensor(out=b, in0=a, in1=accw[:, 4:5],
                                       op=ALU.add), inc=sV)
                pst["acc"][kind] = cnt + 1
                pst["stg_rd"][slot] = (sV, v)
                ms["abs_%s_last" % kind] = v
            else:
                g = 0 if kind == "q1" else 1
                wqs = i % 2
                if any(p[2] == wqs for p in pst["pend"]):
                    prep_flushT()
                bc = bc_r1 if g == 0 else bc_r2
                op("dve", lambda e, s=wstage[slot], b=bc: e.tensor_scalar(
                    out=sgnA, in0=s, scalar1=b, scalar2=0.5,
                    op0=ALU.mult, op1=ALU.is_ge),
                    waits=[(dP[slot], ld), (sV, ms["wsc_%d" % g]),
                           pst["sgn_rd"]])
                va = op("dve", lambda e, s=wstage[slot], b=bc: e.tensor_scalar(
                    out=sgnB, in0=s, scalar1=b, scalar2=-0.5,
                    op0=ALU.mult, op1=ALU.is_ge), inc=sV)
                pst["stg_rd"][slot] = (sV, va)
                v = op("dve", lambda e, o=wqch[wqs]: e.scalar_tensor_tensor(
                    out=o, in0=sgnA, scalar=-1.0, in1=sgnB,
                    op0=ALU.add, op1=ALU.add),
                    waits=[pst["wq_rd"][wqs]], inc=sV)
                pst["sgn_rd"] = (sV, v)
                pst["pend"].append((kind, i, wqs, v))
            prep_issue_next_dma()

        def prep_flushT():
            for kind, i, wqs, vq in pst["pend"]:
                sp, key = next_stage()
                pv = None
                for j in range(4):
                    pv = op("pe", lambda e, jj=j, s=wqch[wqs], p=sp: e.transpose(
                        out=p[:, jj * 128:(jj + 1) * 128],
                        in_=s[:, jj * 128:(jj + 1) * 128], identity=identb),
                        waits=([(sV, vq), psum_free[key],
                                (dC, ms["consts_ld"])]
                               if j == 0 else []),
                        inc=(sP if j == 3 else None))
                pst["wq_rd"][wqs] = (sP, pv)
                if kind == "q1":
                    hb, half = i // 2, i % 2
                    dst = W1qT.rearrange("p (c h) -> p c h", c=DC)[
                        :, half * 4:half * 4 + 4, hb * 128:(hb + 1) * 128]
                    v = op("dve", lambda e, d=dst, p=sp: e.tensor_copy(
                        out=d,
                        in_=p[:, 0:512].rearrange("p (c h) -> p c h", c=4)),
                        waits=[(sP, pv), (sV, ms["wcast_v"]),
                               (sA, ms["wcast_a"])], inc=sV)
                    psum_free[key] = (sV, v)
                else:
                    db, s8 = i // 8, i % 8
                    st = i % 2
                    v = op("dve", lambda e, o=w2st[st], p=sp: e.tensor_copy(
                        out=o, in_=p[:, 0:512]),
                        waits=[(sP, pv), pst["w2st_rd"][st]], inc=sV)
                    psum_free[key] = (sV, v)
                    dst = w2t_d.rearrange("p (c d) -> p c d", c=32)[
                        :, s8 * 4:s8 * 4 + 4, db * 128:(db + 1) * 128]
                    b = dma("syncq", dst,
                            w2st[st].rearrange("p (c d) -> p c d", c=4),
                            dB[st], waits=[(sV, v)])
                    pst["w2st_rd"][st] = (dB[st], b)
            pst["pend"] = []

        for _ in range(4):
            prep_issue_next_dma()

        spill_ms = {}
        (CEL_SUM, CEL_SSQ, CEL_M, CEL_MSQ, CEL_VAR, CEL_LNV, CEL_RSTD,
         CEL_NMR) = range(8)
        CEL_NMX, CEL_SUMS, CEL_RCP = 8, 24, 40
        CEL2 = 56

        def a_ln1_stats(t):
            par = t % 2
            cl = cells[t % 2]
            va = op("act", lambda e, x=xin[par], c=cl: e.activation(
                out=outbuf, in_=x, func=AF.Square,
                accum_out=c[:, CEL_SSQ:CEL_SSQ + 1]),
                waits=[(dX, xin_ld[t])], inc=sA)
            op("dve", lambda e, x=xin[par], c=cl: e.tensor_reduce(
                out=c[:, CEL_SUM:CEL_SUM + 1], in_=x, axis=AX.X, op=ALU.add),
                waits=[(dX, xin_ld[t])])
            op("dve", lambda e, c=cl: e.tensor_scalar(
                out=c[:, CEL_M:CEL_M + 1], in0=c[:, CEL_SUM:CEL_SUM + 1],
                scalar1=1.0 / D, scalar2=None, op0=ALU.mult))
            op("dve", lambda e, c=cl: e.tensor_tensor(
                out=c[:, CEL_MSQ:CEL_MSQ + 1], in0=c[:, CEL_M:CEL_M + 1],
                in1=c[:, CEL_M:CEL_M + 1], op=ALU.mult))
            v = op("dve", lambda e, c=cl: e.scalar_tensor_tensor(
                out=c[:, CEL_VAR:CEL_VAR + 1], in0=c[:, CEL_SSQ:CEL_SSQ + 1],
                scalar=1.0 / D, in1=c[:, CEL_MSQ:CEL_MSQ + 1],
                op0=ALU.mult, op1=ALU.subtract), waits=[(sA, va)], inc=sV)
            op("act", lambda e, c=cl: e.activation(
                out=c[:, CEL_LNV:CEL_LNV + 1], in_=c[:, CEL_VAR:CEL_VAR + 1],
                func=AF.Ln, bias=c_eps),
                waits=[(sV, v), (sV, ms["memset"])])
            va = op("act", lambda e, c=cl: e.activation(
                out=c[:, CEL_RSTD:CEL_RSTD + 1], in_=c[:, CEL_LNV:CEL_LNV + 1],
                func=AF.Exp, scale=-0.5), inc=sA)
            ms[("rstd", t)] = va

        def a_ln1_apply(t):
            par = t % 2
            cl = cells[t % 2]
            v = op("dve", lambda e, c=cl: e.tensor_scalar(
                out=c[:, CEL_NMR:CEL_NMR + 1], in0=c[:, CEL_M:CEL_M + 1],
                scalar1=c[:, CEL_RSTD:CEL_RSTD + 1], scalar2=-1.0,
                op0=ALU.mult, op1=ALU.mult),
                waits=[(sA, ms[("rstd", t)])], inc=sV)
            va = op("act", lambda e, x=xin[par], c=cl: e.activation(
                out=xln, in_=x, func=AF.Identity,
                bias=c[:, CEL_NMR:CEL_NMR + 1],
                scale=c[:, CEL_RSTD:CEL_RSTD + 1]),
                waits=[(sV, v), (sA, ms.get(("xlnT", t - 1), 0))], inc=sA)
            ms[("xln", t)] = va

        def a_xlnT(t):
            sp, key = next_stage()
            pv = None
            for c in range(DC):
                pv = op("pe", lambda e, cc=c, p=sp: e.transpose(
                    out=p[:, cc * 128:(cc + 1) * 128],
                    in_=xln[:, cc * 128:(cc + 1) * 128], identity=identb),
                    waits=([(sA, ms[("xln", t)]), psum_free[key],
                            (dC, ms["consts_ld"])] if c == 0 else []),
                    inc=(sP if c == DC - 1 else None))
            v = op("act", lambda e, p=sp: e.copy(out=xlnT, in_=p),
                   waits=[(sP, pv), (sP, ms.get(("qkv", t - 1, 2), 0))], inc=sA)
            psum_free[key] = (sA, v)
            ms[("xlnT", t)] = v

        def a_qkv(t, grp):
            par = t % 2
            psX, pfk = ((psA, "A"), (psB, "B"), (psC, "C"))[grp]
            wq = ms["wcast_q"] if grp == 0 else ms["wcast_kv"]
            w0 = [(sA, ms[("xlnT", t)]), psum_free[pfk],
                  (sV, wq[0]), (sA, wq[1])]
            pv = None
            for c in range(DC):
                for n in range(2):
                    pv = op("pe", lambda e, cc=c, nn=n, g=grp, p=psX:
                            e.matmul(
                        out=p[:, nn * 512:(nn + 1) * 512],
                        lhsT=xlnT[:, cc * 128:(cc + 1) * 128],
                        rhs=wqkv_sb[:, cc * 3 * D + g * D + nn * 512:
                                    cc * 3 * D + g * D + (nn + 1) * 512],
                        start=(cc == 0), stop=(cc == DC - 1)),
                        waits=(w0 if (c == 0 and n == 0) else []),
                        inc=(sP if (c == DC - 1 and n == 1) else None))
            ms[("qkv", t, grp)] = pv
            if grp == 0:
                va = op("act", lambda e: e.activation(out=q_sb, in_=psA,
                                                      func=AF.Copy,
                                                      scale=0.125),
                        waits=[(sP, pv), (sA, ms.get(("qT", t - 1), 0))],
                        inc=sA)
                ms[("q_sb", t)] = va
                psum_free["A"] = (sA, va)
            elif grp == 1:
                v = op("dve", lambda e: e.tensor_copy(out=k_sb, in_=psB),
                       waits=[(sP, pv), (sA, ms.get(("kT", t - 1), 0))],
                       inc=sV)
                ms[("k_sb", t)] = v
                psum_free["B"] = (sV, v)
            else:
                v = op("dve", lambda e, s=par: e.tensor_copy(
                    out=v_hist[:, s * D:(s + 1) * D], in_=psC),
                    waits=[(sP, pv), (sP, ms.get(("av", t - 1, 3), 0))],
                    inc=sV)
                ms[("v_sb", t)] = v
                psum_free["C"] = (sV, v)

        def a_qkT(t):
            par = t % 2
            for src_name, buf, wkey in (("q", q_sb, ("q_sb", t)),
                                        ("k", k_sb, ("k_sb", t))):
                wsem = sA if src_name == "q" else sV
                for r in range(2):
                    sp, key = next_stage()
                    pv = None
                    for hh in range(8):
                        h = r * 8 + hh
                        pv = op("pe", lambda e, b=buf, h_=h, hh_=hh, p=sp:
                                e.transpose(
                            out=p[0:64, hh_ * 128:(hh_ + 1) * 128],
                            in_=b[:, h_ * 64:(h_ + 1) * 64], identity=identb),
                            waits=([(wsem, ms[wkey]), psum_free[key]]
                                   if hh == 0 else []),
                            inc=(sP if hh == 7 else None))
                    if src_name == "q":
                        dst = qT[0:64, r * 1024:(r + 1) * 1024]
                    else:
                        dst = kT_hist[0:64, par * 2048 + r * 1024:
                                      par * 2048 + (r + 1) * 1024]
                    v = op("act", lambda e, d=dst, p=sp: e.copy(
                        out=d, in_=p[0:64, 0:1024]),
                        waits=[(sP, pv), (sP, ms.get(("scores", t - 1, 3), 0))],
                        inc=sA)
                    psum_free[key] = (sA, v)
                ms[("qT", t) if src_name == "q" else ("kT", t)] = v

        def a_scores(t, g):
            par, prv = t % 2, 1 - (t % 2)
            tvar = 1 if (t % TPS == 0) else 0
            psX = psA if g % 2 == 0 else psB
            pfk = "A" if g % 2 == 0 else "B"
            pev = ms[("expv", t, g - 2)] if g >= 2 else None
            w0 = [(sA, ms[("qT", t)]), (sA, ms[("kT", t)]),
                  (dC, ms["consts_ld"])]
            if pev is None:
                w0.append(psum_free[pfk])
            pv = None
            for half in range(2):
                first = True
                wh = list(w0) if half == 0 else []
                if pev is not None:
                    wh.append((sA, pev[half * 2 + 1]))
                for hh in (half * 2, half * 2 + 1):
                    h = 4 * g + hh
                    for kt, slot in ((0, prv), (1, par)):
                        pv = op("pe", lambda e, p=psX, hh_=hh, kt_=kt,
                                h_=h, sl=slot: e.matmul(
                            out=p[:, hh_ * 256 + kt_ * 128:
                                  hh_ * 256 + (kt_ + 1) * 128],
                            lhsT=qT[0:64, h_ * 128:(h_ + 1) * 128],
                            rhs=kT_hist[0:64, sl * 2048 + h_ * 128:
                                        sl * 2048 + (h_ + 1) * 128],
                            start=(hh_ % 2 == 0 and kt_ == 0), stop=False),
                            waits=(wh if first else []), inc=None)
                        first = False
                for hh in (half * 2, half * 2 + 1):
                    last = (hh == half * 2 + 1)
                    pv = op("pe", lambda e, p=psX, hh_=hh, mv=tvar, l=last:
                            e.matmul(
                                out=p[:, hh_ * 256:(hh_ + 1) * 256],
                                lhsT=identb,
                                rhs=masks[:, mv * 256:(mv + 1) * 256],
                                start=False, stop=l),
                            inc=(sP if last else None))
                if half == 0:
                    ms[("sc_h0", t, g)] = pv
            ms[("scores", t, g)] = pv

        def a_sm_max(t, g):
            cl = cells[t % 2]
            psX = psA if g % 2 == 0 else psB
            v0 = op("dve", lambda e, p=psX, c=cl, g_=g: e.tensor_reduce(
                out=c[:, CEL_NMX + 4 * g_:CEL_NMX + 4 * g_ + 2],
                in_=p[:, 0:512].rearrange("p (h k) -> p h k", h=2),
                axis=AX.X, op=ALU.max, negate=True),
                waits=[(sP, ms[("sc_h0", t, g)])], inc=sV)
            v = op("dve", lambda e, p=psX, c=cl, g_=g: e.tensor_reduce(
                out=c[:, CEL_NMX + 4 * g_ + 2:CEL_NMX + 4 * g_ + 4],
                in_=p[:, 512:1024].rearrange("p (h k) -> p h k", h=2),
                axis=AX.X, op=ALU.max, negate=True),
                waits=[(sP, ms[("scores", t, g)])], inc=sV)
            ms[("nmx0", t, g)] = v0
            ms[("nmx", t, g)] = v

        def a_sm_exp(t, g):
            cl = cells[t % 2]
            psX = psA if g % 2 == 0 else psB
            pfk = "A" if g % 2 == 0 else "B"
            va = None
            evl = []
            for hh in range(4):
                wv = [(sV, ms[("nmx0", t, g)] if hh < 2
                       else ms[("nmx", t, g)])]
                if g >= 2:
                    wv.append((sP, ms.get(("pT_mm", t, g - 2), 0)))
                else:
                    wv.append((sP, ms.get(("pT_mm", t - 1, g + 2), 0)))
                va = op("act", lambda e, p=psX, c=cl, g_=g, hh_=hh:
                        e.activation(
                    out=probs[g_ % 2][:, hh_ * 256:(hh_ + 1) * 256],
                    in_=p[:, hh_ * 256:(hh_ + 1) * 256], func=AF.Exp,
                    bias=c[:, CEL_NMX + 4 * g_ + hh_:
                           CEL_NMX + 4 * g_ + hh_ + 1],
                    accum_out=c[:, CEL_SUMS + 4 * g_ + hh_:
                                CEL_SUMS + 4 * g_ + hh_ + 1]),
                    waits=(wv if hh in (0, 2) else []),
                    inc=sA)
                evl.append(va)
            ms[("exp", t, g)] = va
            ms[("expv", t, g)] = evl
            psum_free[pfk] = (sA, va)

        def a_sm_norm(t, g):
            cl = cells[t % 2]
            ev = ms[("expv", t, g)]
            op("dve", lambda e, c=cl, g_=g: e.reciprocal(
                out=c[:, CEL_RCP + 4 * g_:CEL_RCP + 4 * g_ + 2],
                in_=c[:, CEL_SUMS + 4 * g_:CEL_SUMS + 4 * g_ + 2]),
                waits=[(sA, ev[1])])
            v2 = None
            for hh in range(4):
                if hh == 2:
                    op("dve", lambda e, c=cl, g_=g: e.reciprocal(
                        out=c[:, CEL_RCP + 4 * g_ + 2:CEL_RCP + 4 * g_ + 4],
                        in_=c[:, CEL_SUMS + 4 * g_ + 2:
                               CEL_SUMS + 4 * g_ + 4]),
                        waits=[(sA, ev[3])])
                v2 = op("dve", lambda e, c=cl, g_=g, hh_=hh: e.tensor_scalar(
                    out=probsn[g_ % 2][:, hh_ * 256:(hh_ + 1) * 256],
                    in0=probs[g_ % 2][:, hh_ * 256:(hh_ + 1) * 256],
                    scalar1=c[:, CEL_RCP + 4 * g_ + hh_:
                              CEL_RCP + 4 * g_ + hh_ + 1],
                    scalar2=None, op0=ALU.mult),
                    inc=sV)
                if hh == 1:
                    ms[("pn0", t, g)] = v2
            ms[("pn", t, g)] = v2

        def a_pt(t, g):
            sp, key = next_stage()
            pv = None
            for j in range(8):
                if j == 0:
                    w = [(sV, ms[("pn0", t, g)]), psum_free[key]]
                elif j == 4:
                    w = [(sV, ms[("pn", t, g)])]
                else:
                    w = []
                pv = op("pe", lambda e, jj=j, g_=g, p=sp: e.transpose(
                    out=p[:, jj * 128:(jj + 1) * 128],
                    in_=probsn[g_ % 2][:, jj * 128:(jj + 1) * 128],
                    identity=identb),
                    waits=w, inc=(sP if j == 7 else None))
            ms[("pT_mm", t, g)] = pv
            wv = [(sP, pv)]
            if g >= 1:
                wv.append((sP, ms.get(("av", t, g - 1), 0)))
            else:
                wv.append((sP, ms.get(("av", t - 1, 3), 0)))
            v = op("act", lambda e, g_=g, p=sp: e.activation(
                out=probsT[g_ % 2], in_=p, func=AF.Copy), waits=wv, inc=sA)
            psum_free[key] = (sA, v)
            ms[("pT", t, g)] = v

        def a_av(t, g):
            par, prv = t % 2, 1 - (t % 2)
            pv = None
            for hh in range(4):
                h = 4 * g + hh
                bp = 64 * (h % 2)
                p = h // 2
                for kt, slot in ((0, prv), (1, par)):
                    wv = [(sA, ms[("pT", t, g)]), (sV, ms[("v_sb", t)])]
                    if g == 0 and hh == 0 and kt == 0:
                        wv.append(psum_free["C"])
                    pv = op("pe", lambda e, hh_=hh, h_=h, bp_=bp, pp=p,
                            kt_=kt, sl=slot, g_=g: e.matmul(
                        out=psC[bp_:bp_ + 64, pp * 128:(pp + 1) * 128],
                        lhsT=v_hist[:, sl * D + h_ * 64:
                                    sl * D + h_ * 64 + 64],
                        rhs=probsT[g_ % 2][:, (hh_ * 2 + kt_) * 128:
                                           (hh_ * 2 + kt_ + 1) * 128],
                        start=(kt_ == 0), stop=(kt_ == 1),
                        tile_position=(0, bp_)),
                        waits=(wv if kt == 0 else []),
                        inc=(sP if (hh == 3 and kt == 1) else None))
            ms[("av", t, g)] = pv

        def a_outproj(t):
            v = op("dve", lambda e: e.tensor_copy(out=attnout, in_=psC),
                   waits=[(sP, ms[("av", t, 3)]),
                          (sP, ms.get(("outproj", t - 1), 0))], inc=sV)
            ms[("attnout", t)] = v
            psum_free["C"] = (sV, v)
            pv = None
            for n in range(2):
                for c in range(DC):
                    wv = [(sV, ms[("attnout", t)]), psum_free["C"],
                          (sV, ms["wcast_v"]), (sA, ms["wcast_a"])]
                    pv = op("pe", lambda e, cc=c, nn=n: e.matmul(
                        out=psC[:, nn * 512:(nn + 1) * 512],
                        lhsT=attnout[:, cc * 128:(cc + 1) * 128],
                        rhs=wout_sb[:, cc * D + nn * 512:cc * D + (nn + 1) * 512],
                        start=(cc == 0), stop=(cc == DC - 1)),
                        waits=(wv if (n == 0 and c == 0) else []),
                        inc=(sP if (n == 1 and c == DC - 1) else None))
            ms[("outproj", t)] = pv

        def a_res(t):
            par = t % 2
            v = op("dve", lambda e, x=xin[par], o=x1buf[par]: e.tensor_tensor(
                out=o, in0=psC, in1=x, op=ALU.add),
                waits=[(sP, ms[("outproj", t)]), (dS, spill_ms.get(t - 2, 0))],
                inc=sV)
            ms[("res", t)] = v
            psum_free["C"] = (sV, v)
            spill_ms[t] = dma("syncq", x1_d[t * T:(t + 1) * T, :], x1buf[par],
                              dS, waits=[(sV, v)])

        def a_ln2_acc(t):
            par = t % 2
            cl = cells[t % 2]
            va = op("act", lambda e, o=x1buf[par], c=cl: e.activation(
                out=outbuf, in_=o, func=AF.Square,
                accum_out=c[:, CEL2:CEL2 + 1]),
                waits=[(sV, ms[("res", t)])], inc=sA)
            va = op("act", lambda e, o=x1buf[par], c=cl: e.activation(
                out=outbuf, in_=o, func=AF.Identity,
                accum_out=c[:, CEL2 + 1:CEL2 + 2]), inc=sA)
            ms[("ln2a", t)] = va

        def a_ln2_fin(t):
            cl = cells[t % 2]
            op("dve", lambda e, c=cl, t_=t: e.tensor_scalar(
                out=statsN[:, t_:t_ + 1], in0=c[:, CEL2 + 1:CEL2 + 2],
                scalar1=-1.0 / D, scalar2=None, op0=ALU.mult),
                waits=[(sA, ms[("ln2a", t)])])
            op("dve", lambda e, c=cl, t_=t: e.tensor_tensor(
                out=c[:, CEL2 + 2:CEL2 + 3], in0=statsN[:, t_:t_ + 1],
                in1=statsN[:, t_:t_ + 1], op=ALU.mult))
            v = op("dve", lambda e, c=cl: e.scalar_tensor_tensor(
                out=c[:, CEL2 + 3:CEL2 + 4], in0=c[:, CEL2:CEL2 + 1],
                scalar=1.0 / D, in1=c[:, CEL2 + 2:CEL2 + 3],
                op0=ALU.mult, op1=ALU.subtract), inc=sV)
            op("act", lambda e, c=cl: e.activation(
                out=c[:, CEL2 + 4:CEL2 + 5], in_=c[:, CEL2 + 3:CEL2 + 4],
                func=AF.Ln, bias=c_eps), waits=[(sV, v)])
            va = op("act", lambda e, c=cl, t_=t: e.activation(
                out=statsR[:, t_:t_ + 1], in_=c[:, CEL2 + 4:CEL2 + 5],
                func=AF.Exp, scale=-0.5), inc=sA)
            ms[("ln2", t)] = va

        gtile = 0
        a_ln1_stats(0)
        a_ln1_apply(0)
        a_xlnT(0)
        a_qkv(0, 0)
        a_qkv(0, 1)
        a_qkv(0, 2)
        for t in range(nt):
            fence(gtile)
            gtile += 1
            ptasks = list(prep_by_tile[t])

            def prep_drip(k):
                for _ in range(k):
                    if ptasks:
                        prep_emit(ptasks.pop(0))

            if t + 1 < nt:
                a_ln1_stats(t + 1)
            if t == 0:
                a_qkT(0)
            a_scores(t, 0)
            a_sm_max(t, 0)
            a_sm_exp(t, 0)
            if t > 0:
                a_ln2_fin(t - 1)
            a_scores(t, 1)
            a_sm_max(t, 1)
            a_sm_exp(t, 1)
            a_sm_norm(t, 0)
            a_scores(t, 2)
            a_sm_max(t, 2)
            prep_drip(2)
            a_pt(t, 0)
            a_av(t, 0)
            prep_flushT()
            a_sm_exp(t, 2)
            a_sm_norm(t, 1)
            if t + 1 < nt:
                a_ln1_apply(t + 1)
            a_scores(t, 3)
            a_sm_max(t, 3)
            prep_drip(2)
            a_pt(t, 1)
            a_av(t, 1)
            prep_flushT()
            a_sm_exp(t, 3)
            a_sm_norm(t, 2)
            if t + 1 < nt:
                a_xlnT(t + 1)
            prep_drip(2)
            a_pt(t, 2)
            a_av(t, 2)
            prep_flushT()
            if t + 1 < nt:
                a_qkv(t + 1, 0)
            a_sm_norm(t, 3)
            a_pt(t, 3)
            a_av(t, 3)
            prep_drip(2)
            if t + 1 < nt:
                a_qkv(t + 1, 1)
                a_qkT(t + 1)
            a_outproj(t)
            a_res(t)
            if t + 1 < nt:
                a_qkv(t + 1, 2)
            a_ln2_acc(t)
            if t + 2 < nt:
                xin_ld[t + 2] = dma(
                    "syncq", xin[t % 2], xc[(t + 2) * T:(t + 3) * T, :],
                    dX, waits=[(sV, ms[("res", t)]), (sA, ms[("xln", t)])])
            prep_drip(len(ptasks))
            prep_flushT()
            tile_end()
        a_ln2_fin(nt - 1)

        a_end = {ek: ENG_SEM[ek].n for ek in ENG_SEM}
        for ek in ("pe", "dve", "act", "pool", "syncq"):
            waits = [(ENG_SEM[o], a_end[o]) for o in ENG_SEM if o != ek]
            for s in (dS, dB[0], dB[1], dP[0], dP[1], dW[0], dW[1], dC, dY, dX):
                waits.append((s, s.n))
            op(ek, None, waits=waits)

        w2qt_lds = []
        for j in range(4):
            w2qt_lds.append(dma(
                "act", W2qT[:, j * 8192:(j + 1) * 8192],
                w2t_d[:, j * 8192:(j + 1) * 8192], dW[j % 2],
                waits=([(dB[0], dB[0].n), (dB[1], dB[1].n)] if j < 2 else [])))
        ms["w2qt0"] = w2qt_lds[2]
        ms["w2qt1"] = w2qt_lds[3]

        x1_ld = {}
        out_ms = {}
        FQ = 0
        FH = 12
        cbb = tib.bitcast(F32)

        def x1s(t):
            return (x1buf[0], x1buf[1], x1c)[t % 3]

        def emit_round(src_ap, dst_ap, scale_cell, bias_cell, bias_imm, nsub,
                       first_wait, war_wait, r0=0):
            subs = []
            for r in range(r0, r0 + nsub):
                wv = (list(first_wait) + list(war_wait)) if r == r0 else []
                sc2 = bias_cell if bias_cell is not None else bias_imm
                op("dve", lambda e, rr=r, s=src_ap, sc=scale_cell,
                   b=sc2: e.tensor_scalar(
                    out=t1b, in0=s[:, rr * 1024:(rr + 1) * 1024],
                    scalar1=sc, scalar2=b, op0=ALU.mult, op1=ALU.add),
                    waits=wv)
                op("dve", lambda e: e.tensor_copy(out=tib, in_=t1b))
                op("dve", lambda e: e.tensor_copy(out=tfb, in_=tib))
                op("dve", lambda e: e.tensor_tensor(
                    out=cbb, in0=t1b, in1=tfb, op=ALU.is_ge))
                rl = op("dve", lambda e, rr=r, d=dst_ap: e.scalar_tensor_tensor(
                    out=d[:, rr * 1024:(rr + 1) * 1024], in0=tfb,
                    scalar=-641.0, in1=cbb, op0=ALU.add, op1=ALU.add), inc=sV)
                subs.append(rl)
            return subs

        for t in range(min(3, nt)):
            x1_ld[t] = dma("syncq", x1s(t), x1_d[t * T:(t + 1) * T, :],
                           dL, waits=[(dS, dS.n)])

        def emit_w1_quarter(t, j):
            cl = cells[t % 2]
            psX = psA if j % 2 == 0 else psB
            pfk = "A" if j % 2 == 0 else "B"
            wv = [(sA, ms[("xqT", t)]), (sA, ms[("i1r", t)]),
                  psum_free[pfk], (dY, ms["bias_ld"])]
            pv = None
            for n in range(2):
                pv = op("pe", lambda e, jj=j, nn=n, p=psX: e.matmul(
                    out=p[:, nn * 512:(nn + 1) * 512], lhsT=invs1row,
                    rhs=b1bf[0:1, jj * 1024 + nn * 512:
                             jj * 1024 + (nn + 1) * 512],
                    start=True, stop=False),
                    waits=(wv if n == 0 else []), inc=None)
                for c in range(DC):
                    pv = op("pe", lambda e, jj=j, nn=n, cc=c, p=psX:
                            e.matmul(
                        out=p[:, nn * 512:(nn + 1) * 512],
                        lhsT=xqT[:, cc * 128:(cc + 1) * 128],
                        rhs=W1qT[:, cc * HF + jj * 1024 + nn * 512:
                                 cc * HF + jj * 1024 + (nn + 1) * 512],
                        start=False, stop=(cc == DC - 1)),
                        inc=(sP if (n == 1 and c == DC - 1) else None))
            ms[("w1", t, j)] = pv

        def _hqs(t, j):
            l = ms.get(("hqsub", t))
            return 0 if l is None else l[j]

        def emit_gelu(t, j):
            cl = cells[t % 2]
            psX = psA if j % 2 == 0 else psB
            pfk = "A" if j % 2 == 0 else "B"
            va = op("act", lambda e, jj=j, p=psX, c=cl: e.activation(
                out=h_sb[:, jj * 1024:(jj + 1) * 1024], in_=p,
                func=AF.Gelu, scale=c[:, FQ + 9:FQ + 10]),
                waits=[(sP, ms[("w1", t, j)]), (sV, _hqs(t - 1, j)),
                       (sV, ms[("fq", t)])], inc=sA)
            ms[("gelu", t, j)] = va
            psum_free[pfk] = (sA, va)
            op("dve", lambda e, jj=j, c=cl: e.tensor_reduce(
                out=c[:, FH + jj:FH + jj + 1],
                in_=h_sb[:, jj * 1024:(jj + 1) * 1024], axis=AX.X,
                op=ALU.max, apply_absolute_value=True),
                waits=[(sA, va)])

        def f_front_dve(t):
            cl = cells[t % 2]
            op("dve", lambda e, x=x1s(t), c=cl: e.tensor_reduce(
                out=c[:, FQ:FQ + 1], in_=x, axis=AX.X, op=ALU.max),
                waits=[(dL, x1_ld[t]), (sA, ms.get(("gelu", t - 2, 3), 0)),
                       (sP, ms.get(("sT1", t - 2), 0))])
            op("dve", lambda e, x=x1s(t), c=cl: e.tensor_reduce(
                out=c[:, FQ + 1:FQ + 2], in_=x, axis=AX.X, op=ALU.min))
            op("dve", lambda e, c=cl, t_=t: e.tensor_tensor(
                out=c[:, FQ + 2:FQ + 3], in0=c[:, FQ:FQ + 1],
                in1=statsN[:, t_:t_ + 1], op=ALU.add))
            op("dve", lambda e, c=cl, t_=t: e.tensor_tensor(
                out=c[:, FQ + 3:FQ + 4], in0=c[:, FQ + 1:FQ + 2],
                in1=statsN[:, t_:t_ + 1], op=ALU.add))
            op("dve", lambda e, c=cl: e.tensor_scalar(
                out=c[:, FQ + 4:FQ + 5], in0=c[:, FQ + 3:FQ + 4],
                scalar1=-1.0, scalar2=None, op0=ALU.mult))
            op("dve", lambda e, c=cl: e.tensor_tensor(
                out=c[:, FQ + 5:FQ + 6], in0=c[:, FQ + 2:FQ + 3],
                in1=c[:, FQ + 4:FQ + 5], op=ALU.max))
            op("dve", lambda e, c=cl: e.reciprocal(
                out=c[:, FQ + 6:FQ + 7], in_=c[:, FQ + 5:FQ + 6]))
            op("dve", lambda e, c=cl: e.tensor_scalar(
                out=c[:, FQ + 7:FQ + 8], in0=c[:, FQ + 6:FQ + 7],
                scalar1=127.0, scalar2=None, op0=ALU.mult))
            op("dve", lambda e, c=cl, t_=t: e.tensor_tensor(
                out=c[:, FQ + 8:FQ + 9], in0=c[:, FQ + 5:FQ + 6],
                in1=statsR[:, t_:t_ + 1], op=ALU.mult))
            op("dve", lambda e, c=cl: e.tensor_scalar(
                out=c[:, FQ + 9:FQ + 10], in0=c[:, FQ + 8:FQ + 9],
                scalar1=bc_ws1, scalar2=None, op0=ALU.mult))
            op("dve", lambda e, c=cl: e.reciprocal(
                out=c[:, FQ + 10:FQ + 11], in_=c[:, FQ + 9:FQ + 10]))
            v = op("dve", lambda e, c=cl, t_=t: e.tensor_scalar(
                out=c[:, FQ + 11:FQ + 12], in0=statsN[:, t_:t_ + 1],
                scalar1=c[:, FQ + 7:FQ + 8], scalar2=RK,
                op0=ALU.mult, op1=ALU.add), inc=sV)
            ms[("fq", t)] = v

            rl = emit_round(x1s(t), xq,
                            cl[:, FQ + 7:FQ + 8], cl[:, FQ + 11:FQ + 12], None,
                            1, [(sV, ms[("fq", t)])],
                            [(sP, ms.get(("xqT_mm", t - 1), 0))])
            ms[("xq", t)] = rl[-1]

        def f_front_pe(t):
            cl = cells[t % 2]
            pv = op("pe", lambda e, c=cl: e.transpose(
                out=psF[0:1, 0:128], in_=c[:, FQ + 10:FQ + 11],
                identity=identf),
                waits=[(sV, ms[("fq", t)]), psum_free["F"],
                       (dC, ms["consts_ld"]),
                       (sP, ms.get(("w1", t - 1, 3), 0))], inc=sP)
            ms[("sT1", t)] = pv
            va = op("act", lambda e: e.activation(
                out=invs1row, in_=psF[0:1, 0:128], func=AF.Copy),
                waits=[(sP, pv)], inc=sA)
            ms[("i1r", t)] = va
            psum_free["F"] = (sA, va)

            sp, key = next_stage()
            pv = None
            for c in range(DC):
                pv = op("pe", lambda e, cc=c, p=sp: e.transpose(
                    out=p[:, cc * 128:(cc + 1) * 128],
                    in_=xq[:, cc * 128:(cc + 1) * 128], identity=identb),
                    waits=([(sV, ms[("xq", t)]), psum_free[key]]
                           if c == 0 else []),
                    inc=(sP if c == DC - 1 else None))
            ms[("xqT_mm", t)] = pv
            v = op("act", lambda e, p=sp: e.activation(
                out=xqT, in_=p, func=AF.Copy),
                waits=[(sP, pv), (sP, ms.get(("w1", t - 1, 3), 0))], inc=sA)
            psum_free[key] = (sA, v)
            ms[("xqT", t)] = v

            emit_w1_quarter(t, 0)
            emit_w1_quarter(t, 1)

        def f_fh(t):
            cl = cells[t % 2]
            op("dve", lambda e, c=cl: e.tensor_reduce(
                out=c[:, FH + 4:FH + 5], in_=c[:, FH:FH + 4], axis=AX.X,
                op=ALU.max))
            op("dve", lambda e, c=cl: e.reciprocal(
                out=c[:, FH + 8:FH + 9], in_=c[:, FH + 4:FH + 5]))
            op("dve", lambda e, c=cl: e.tensor_scalar(
                out=c[:, FH + 5:FH + 6], in0=c[:, FH + 8:FH + 9],
                scalar1=127.0, scalar2=None, op0=ALU.mult))
            op("dve", lambda e, c=cl: e.tensor_scalar(
                out=c[:, FH + 6:FH + 7], in0=c[:, FH + 4:FH + 5],
                scalar1=bc_ws2, scalar2=None, op0=ALU.mult))
            v = op("dve", lambda e, c=cl: e.reciprocal(
                out=c[:, FH + 7:FH + 8], in_=c[:, FH + 6:FH + 7]), inc=sV)
            ms[("fh", t)] = v

        def f_round0(t):
            cl = cells[t % 2]
            rl = emit_round(h_sb, hq, cl[:, FH + 5:FH + 6], None, RK, 1,
                            [(sV, ms[("fh", t)])],
                            [(sP, ms.get(("hqT_mm", t - 1), 0))])
            ms[("hqsub", t)] = rl

        def f_rounds123(t):
            cl = cells[t % 2]
            rl = emit_round(h_sb, hq, cl[:, FH + 5:FH + 6], None, RK, 3,
                            [], [], r0=1)
            ms[("hq", t)] = rl[-1]
            ms[("hqsub", t)] = ms[("hqsub", t)] + rl

        def f_back_T0(t):
            cl = cells[t % 2]
            pv = op("pe", lambda e, c=cl: e.transpose(
                out=psF[0:1, 128:256], in_=c[:, FH + 7:FH + 8],
                identity=identf),
                waits=[(sV, ms[("fh", t)]), psum_free["F"],
                       (sP, ms.get(("w2", t - 1), 0))], inc=sP)
            va = op("act", lambda e: e.activation(
                out=invs2row, in_=psF[0:1, 128:256], func=AF.Copy),
                waits=[(sP, pv)], inc=sA)
            ms[("i2r", t)] = va
            psum_free["F"] = (sA, va)
            _fT(t, (0,))

        def f_out(t):
            cl = cells[t % 2]
            v = op("dve", lambda e, c=cl, x=x1s(t): e.scalar_tensor_tensor(
                out=outbuf, in0=psC, scalar=c[:, FH + 6:FH + 7], in1=x,
                op0=ALU.mult, op1=ALU.add),
                waits=[(sP, ms[("w2", t)]), (dO, out_ms.get(t - 1, 0))],
                inc=sV)
            ms[("fout", t)] = v
            psum_free["C"] = (sV, v)
            out_ms[t] = dma("pool", y_d[t * T:(t + 1) * T, :], outbuf, dO,
                            waits=[(sV, v)])
            if t + 3 < nt:
                x1_ld[t + 3] = dma("syncq", x1s(t + 3),
                                   x1_d[(t + 3) * T:(t + 4) * T, :], dL,
                                   waits=[(sV, v)])

        def _fT(t, rrs):
            wv = [(sA, ms[("i2r", t)]),
                  (sV, ms.get(("fout", t - 1), 0)), (dW[0], ms["w2qt0"]),
                  (dW[1], ms["w2qt1"]), (dY, ms["bias_ld"])]
            pv = None
            evs = None
            for rr in rrs:
                sp, key = next_stage()
                for j in range(8):
                    pv = op("pe", lambda e, rr_=rr, jj=j, p=sp: e.transpose(
                        out=p[:, jj * 128:(jj + 1) * 128],
                        in_=hq[:, rr_ * 1024 + jj * 128:
                               rr_ * 1024 + (jj + 1) * 128],
                        identity=identb),
                        waits=([(sV, ms[("hqsub", t)][rr]), psum_free[key]]
                               if j == 0 else []),
                        inc=(sP if j == 7 else None))
                if rr == 3:
                    ms[("hqT_mm", t)] = pv
                evs = op("act", lambda e, rr_=rr, p=sp: e.activation(
                    out=hqT[:, rr_ * 1024:(rr_ + 1) * 1024], in_=p,
                    func=AF.Copy),
                    waits=[(sP, pv)] + ([(sP, ms.get(("w2", t - 1), 0))]
                                        if rr == 0 else []), inc=sA)
                psum_free[key] = (sA, evs)
                for n in range(2):
                    if rr == 0:
                        op("pe", lambda e, nn=n: e.matmul(
                            out=psC[:, nn * 512:(nn + 1) * 512],
                            lhsT=invs2row,
                            rhs=b2bf[0:1, nn * 512:(nn + 1) * 512],
                            start=True, stop=False),
                            waits=(wv + [(sA, evs)]) if n == 0 else [],
                            inc=None)
                    for k in range(rr * 8, rr * 8 + 8):
                        pv = op("pe", lambda e, nn=n, kk=k: e.matmul(
                            out=psC[:, nn * 512:(nn + 1) * 512],
                            lhsT=hqT[:, kk * 128:(kk + 1) * 128],
                            rhs=W2qT[:, kk * 1024 + nn * 512:
                                     kk * 1024 + (nn + 1) * 512],
                            start=False, stop=(kk == 31)),
                            waits=([(sA, evs)] if (n == 0 and k == rr * 8)
                                   else []),
                            inc=(sP if (n == 1 and k == 31) else None))
            ms[("hqT", t)] = evs
            ms[("w2", t)] = pv

        f_front_dve(0)
        f_front_pe(0)
        if nt > 1:
            f_front_dve(1)
        emit_gelu(0, 0)
        emit_gelu(0, 1)
        emit_w1_quarter(0, 2)
        emit_gelu(0, 2)
        emit_w1_quarter(0, 3)
        emit_gelu(0, 3)
        f_fh(0)
        for t in range(nt):
            fence(gtile)
            gtile += 1
            f_round0(t)
            if t + 1 < nt:
                f_front_pe(t + 1)
            f_back_T0(t)
            f_rounds123(t)
            if t + 2 < nt:
                f_front_dve(t + 2)
            _fT(t, (1,))
            if t + 1 < nt:
                emit_gelu(t + 1, 0)
                emit_gelu(t + 1, 1)
                emit_w1_quarter(t + 1, 2)
            _fT(t, (2,))
            if t + 1 < nt:
                emit_gelu(t + 1, 2)
                emit_w1_quarter(t + 1, 3)
            _fT(t, (3,))
            if t + 1 < nt:
                emit_gelu(t + 1, 3)
                f_fh(t + 1)
            f_out(t)
            tile_end()

        op("syncq", None, waits=[(dO, dO.n)])
        _emit_all(nc, OPS, _ES, cut)

    return nc


def _emit_all(nc, OPS, _ES, cut=None):
    if cut is not None:
        OPS = OPS[:cut]
        tot = {}
        for (k, waits, fn, inc, amt, is_dma) in OPS:
            if inc is not None:
                tot[inc.name] = tot.get(inc.name, 0) + amt
        OPS = [(k, [(s, min(v, tot.get(s.name, 0))) for (s, v) in waits],
                fn, inc, amt, is_dma) for (k, waits, fn, inc, amt, is_dma) in OPS]

    def mk(ek):
        zs = _ES.get(ek)

        def run(eng):
            lastw = {}
            zi = 0
            for (k, waits, fn, inc, amt, is_dma) in OPS:
                if k != ek:
                    continue
                for (s, val) in waits:
                    if val > 0 and lastw.get(s.name, 0) < val:
                        eng.wait_ge(s.h, val)
                        lastw[s.name] = val
                if fn is None:
                    continue
                if zs is not None and zi > 0:
                    eng.wait_ge(zs.h, zi)
                inst = fn(eng)
                if zs is not None and not is_dma:
                    zi += 1
                if inc is not None:
                    inst.then_inc(inc.h, amt)
        return run

    with nc.Block() as block:
        block.tensor(mk("pe"))
        block.vector(mk("dve"))
        block.scalar(mk("act"))
        block.gpsimd(mk("pool"))
        block.sync(mk("syncq"))


def _consts():
    identb = np.eye(128, dtype=ml_dtypes.bfloat16)
    identf = np.eye(128, dtype=np.float32)
    a = np.arange(128)[:, None]
    b = np.arange(128)[None, :]
    m_prev = np.where(b > a, 0.0, NEG).astype(np.float32)
    m_cur = np.where(b <= a, 0.0, NEG).astype(np.float32)
    normal = np.concatenate([m_prev, m_cur], axis=1)
    t0 = np.concatenate([np.full((128, 128), NEG, np.float32), m_cur], axis=1)
    masks = np.concatenate([normal, t0], axis=1).astype(ml_dtypes.bfloat16)
    return identb, identf, masks


_NC_CACHE = {}


def kernel(L0_attn_x, L0_attn_w_qkv, L0_attn_w_out, L0_ffn_W1, L0_ffn_b1,
           L0_ffn_W2, L0_ffn_b2):
    x = np.ascontiguousarray(np.asarray(L0_attn_x, dtype=np.float32))
    wqkv = np.ascontiguousarray(np.asarray(L0_attn_w_qkv, dtype=np.float32))
    wout = np.ascontiguousarray(np.asarray(L0_attn_w_out, dtype=np.float32))
    w1 = np.ascontiguousarray(np.asarray(L0_ffn_W1, dtype=np.float32))
    w2 = np.ascontiguousarray(np.asarray(L0_ffn_W2, dtype=np.float32))
    b1 = np.ascontiguousarray(np.asarray(L0_ffn_b1, dtype=np.float32))
    b2 = np.ascontiguousarray(np.asarray(L0_ffn_b2, dtype=np.float32))
    identb, identf, masks = _consts()

    nt = SEQ_PC * TPS
    if nt not in _NC_CACHE:
        _NC_CACHE[nt] = _build(nt)
    nc = _NC_CACHE[nt]

    in_maps = []
    for c in range(NCORES):
        xs = np.ascontiguousarray(
            x[c * SEQ_PC:(c + 1) * SEQ_PC].reshape(SEQ_PC * S, D))
        in_maps.append({
            "xc": xs, "wqkv": wqkv, "wout": wout, "w1": w1, "w2": w2,
            "b1": b1, "b2": b2, "identb": identb, "identf": identf,
            "masks": masks,
        })
    res = run_bass_kernel_spmd(nc, in_maps, core_ids=list(range(NCORES)))
    out = np.concatenate(
        [r["y"].reshape(SEQ_PC, S, D) for r in res.results], axis=0)
    return out


# version 78
# speedup vs baseline: 1.0122x; 1.0122x over previous
import numpy as np
import ml_dtypes
from contextlib import ExitStack

import concourse.bass as bass
import concourse.mybir as mybir
from concourse.bass_utils import run_bass_kernel_spmd

F32 = mybir.dt.float32
BF16 = mybir.dt.bfloat16
I32 = mybir.dt.int32
AF = mybir.ActivationFunctionType
ALU = mybir.AluOpType
AX = mybir.AxisListType

B, S, D = 16, 2048, 1024
NCORES = 8
SEQ_PC = B // NCORES
T = 128
TPS = S // T
DC = D // 128
HN, DH = 16, 64
HF = 4096
NEG = -1.0e9
RK = 640.5

_SB_BASE = 16640
_SB_TOP = 229376


class _Sem:
    __slots__ = ("name", "n", "h")

    def __init__(self, name):
        self.name = name
        self.n = 0
        self.h = None


def _off(ap):
    return ap.tensor.manual_sbuf_range[0]


def _build(nt, mode="full", cut=None):
    ntok = nt * T
    nc = bass.Bass("TRN2", target_bir_lowering=False, debug=False)

    xc = nc.dram_tensor("xc", [ntok, D], F32, kind="ExternalInput").ap()
    wqkv_d = nc.dram_tensor("wqkv", [D, 3 * D], F32, kind="ExternalInput").ap()
    wout_d = nc.dram_tensor("wout", [D, D], F32, kind="ExternalInput").ap()
    w1_d = nc.dram_tensor("w1", [HF, D], F32, kind="ExternalInput").ap()
    w2_d = nc.dram_tensor("w2", [D, HF], F32, kind="ExternalInput").ap()
    b1_d = nc.dram_tensor("b1", [HF], F32, kind="ExternalInput").ap()
    b2_d = nc.dram_tensor("b2", [D], F32, kind="ExternalInput").ap()
    identb_d = nc.dram_tensor("identb", [128, 128], BF16, kind="ExternalInput").ap()
    identf_d = nc.dram_tensor("identf", [128, 128], F32, kind="ExternalInput").ap()
    masks_d = nc.dram_tensor("masks", [128, 512], BF16, kind="ExternalInput").ap()
    y_d = nc.dram_tensor("y", [ntok, D], F32, kind="ExternalOutput").ap()
    x1_d = nc.dram_tensor("x1sp", [ntok, D], F32, kind="Internal").ap()
    w2t_d = nc.dram_tensor("w2td", [128, 32 * 1024], BF16, kind="Internal").ap()
    bbf_d = nc.dram_tensor("bbfd", [HF + D], BF16, kind="Internal").ap()

    cur = [_SB_BASE]

    def sb(name, p, n, dt, at=None):
        size = n * mybir.dt.size(dt)
        o = cur[0] if at is None else at
        o = (o + 31) & ~31
        h = nc._state.alloc_sbuf_tensor_at(name, [p, n], dt, offset=o)
        if at is None:
            cur[0] = o + size
        return h.ap()

    identb = sb("identb", 128, 128, BF16)
    identf = sb("identf", 128, 128, F32)
    masks = sb("masks", 128, 512, BF16)
    b1bf = sb("b1bf", 1, HF, BF16)
    b2bf = sb("b2bf", 1, D, BF16)
    cells = [sb("cells0", 128, 64, F32), sb("cells1", 128, 64, F32)]
    statsN = sb("statsN", 128, nt, F32)
    statsR = sb("statsR", 128, nt, F32)
    _c4 = sb("c4", 128, 5, F32)
    c_eps = _c4[:, 0:1]
    c_nh = _c4[:, 1:2]
    c_ph = _c4[:, 2:3]
    ones_col = _c4[:, 3:4]
    c_rk = _c4[:, 4:5]
    ones_row = sb("ones_row", 1, 128, F32)
    _b4 = sb("b4", 128, 4, F32)
    bc_r1 = _b4[:, 0:1]
    bc_ws1 = _b4[:, 1:2]
    bc_r2 = _b4[:, 2:3]
    bc_ws2 = _b4[:, 3:4]
    accw = sb("accw", 128, 8, F32)
    wsc = sb("wsc", 1, 8, F32)
    x1buf = [sb("x1b0", 128, D, F32), sb("x1b1", 128, D, F32)]
    outbuf = sb("outbuf", 128, D, F32)
    invs1row = sb("i1row", 1, 128, BF16)
    invs2row = sb("i2row", 1, 128, BF16)

    A0 = (cur[0] + 63) & ~63
    cur[0] = A0
    wqkv_sb = sb("wqkv_sb", 128, DC * 3 * D, BF16)
    wout_sb = sb("wout_sb", 128, DC * D, BF16)
    xin = [sb("xin0", 128, D, F32), sb("xin1", 128, D, F32)]
    xln = sb("xln", 128, D, BF16)
    xlnT = sb("xlnT", 128, D, BF16)
    q_sb = sb("q_sb", 128, D, BF16)
    k_sb = sb("k_sb", 128, D, BF16)
    qT = sb("qT", 64, 2048, BF16)
    kT_hist = sb("kT_hist", 64, 2 * 2048, BF16)
    v_hist = sb("v_hist", 128, 2 * D, BF16)
    probs = [sb("probs0", 128, 1024, BF16), sb("probs1", 128, 1024, BF16)]
    probsn = probs
    probsT_b = sb("probsT", 128, 1024, BF16)
    probsT = [probsT_b, probsT_b]
    attnout = sb("attnout", 128, 1024, BF16)
    wstage = [sb("wstage0", 128, 512, F32), sb("wstage1", 128, 512, F32),
              sb("wstage2", 128, 512, F32), sb("wstage3", 128, 512, F32)]
    sgnA = sb("sgnA", 128, 512, BF16)
    sgnB = sb("sgnB", 128, 512, BF16)
    wqch = [sb("wqch0", 128, 512, BF16), sb("wqch1", 128, 512, BF16)]
    w2st = [sb("w2st0", 128, 512, BF16), sb("w2st1", 128, 512, BF16)]
    A_END = cur[0]

    w1t_at = (229368 - DC * HF * 2) & ~63
    W1qT = sb("W1qT", 128, DC * HF, BF16, at=w1t_at)
    assert A_END <= w1t_at, (A_END, w1t_at)

    cur[0] = A0
    W2qT = sb("W2qT", 128, 32 * D, BF16)
    h_sb = sb("h_sb", 128, HF, F32)
    hq = sb("hq", 128, HF, BF16)
    hqT = sb("hqT", 128, HF, BF16)
    t1b = sb("t1a", 128, 1024, F32)
    tib = sb("tib", 128, 1024, I32)
    tfb = sb("tfb", 128, 1024, F32)
    x1c = sb("x1c", 128, D, F32)
    xq = sb("xq", 128, D, BF16)
    xqT = sb("xqT", 128, D, BF16)
    F_END = cur[0]
    assert F_END <= w1t_at, (F_END, w1t_at)

    wstg_big = [
        sb("wstgA", 128, 1536, F32, at=w1t_at),
        sb("wstgB", 128, 1536, F32, at=w1t_at + 1536 * 4),
    ]
    bstage = sb("bstage", 128, 32, F32, at=w1t_at + 1536 * 8)
    bstage_bf = sb("bstageb", 128, 32, BF16, at=w1t_at + 1536 * 8 + 128)

    with ExitStack() as ctx:
        psA = ctx.enter_context(nc.psum_tensor("psA", [128, 1024], F32)).ap()
        psB = ctx.enter_context(nc.psum_tensor("psB", [128, 1024], F32)).ap()
        psC = ctx.enter_context(nc.psum_tensor("psC", [128, 1024], F32)).ap()
        psD = ctx.enter_context(nc.psum_tensor("psD", [128, 1024], BF16)).ap()
        psF = ctx.enter_context(nc.psum_tensor("psF", [128, 512], F32)).ap()

        sP = _Sem("sP"); sV = _Sem("sV"); sA = _Sem("sA"); sG = _Sem("sG")
        dX = _Sem("dX"); dS = _Sem("dS"); dL = _Sem("dL"); dO = _Sem("dO")
        dC = _Sem("dC"); dY = _Sem("dY")
        dW = [_Sem("dW0"), _Sem("dW1")]
        dP = [_Sem("dP0"), _Sem("dP1"), _Sem("dP2"), _Sem("dP3")]
        dB = [_Sem("dB0"), _Sem("dB1")]
        SEMS = [sP, sV, sA, sG, dX, dS, dL, dO, dC, dY] + dW + dP + dB
        for s in SEMS:
            s.h = ctx.enter_context(nc.semaphore(s.name))

        OPS = []
        _ES = {"dve": sV, "act": sA, "pool": sG}

        def op(ek, fn, waits=(), inc=None, amt=1, is_dma=False,
               nosync=False):
            if fn is not None and not is_dma and ek in _ES:
                inc = _ES[ek]
                amt = 1
            OPS.append((ek, [w for w in waits if w is not None and w[1] > 0],
                        fn, inc, amt, is_dma, nosync))
            if inc is not None:
                inc.n += amt
                return inc.n
            return None

        def dma(ek, out_ap, in_ap, sem, waits=()):
            w = [(sem, sem.n)] + list(waits)
            return op(ek, (lambda e, o=out_ap, i=in_ap: e.dma_start(out=o, in_=i)),
                      w, sem, 16, is_dma=True)

        ms = {}

        def tickle(n=2):
            for _ in range(n):
                op("pe", lambda e: e.ldweights(identb))

        def tickle_train(sem, target, step=3, n=4, per=3):
            for k in range(n, 0, -1):
                v = target - k * step
                if v > 0:
                    for _ in range(per):
                        op("pe", lambda e: e.ldweights(identb),
                           waits=[(sem, v)])

        psum_free = {"A": None, "B": None, "C": None, "D": None, "F": None}
        psD2 = psF.bitcast(BF16)
        _stg = {"t": 0}

        def next_stage():
            _stg["t"] ^= 1
            return (psD, "D") if _stg["t"] else (psD2, "F")
        eng_end = {k: [0] for k in ("pe", "dve", "act", "pool")}
        ENG_SEM = {"pe": sP, "dve": sV, "act": sA, "pool": sG}

        def fence(gt):
            if gt < 2:
                return
            for ek in ("pe", "dve", "act", "pool"):
                waits = [(ENG_SEM[o], eng_end[o][gt - 1])
                         for o in ("pe", "dve", "act", "pool") if o != ek]
                op(ek, None, waits=waits)

        def tile_end():
            for ek in ("pe", "dve", "act", "pool"):
                eng_end[ek].append(ENG_SEM[ek].n)

        def _memset(ap_, val):
            return lambda e, a=ap_, v=val: e.memset(a, v)

        op("dve", _memset(c_eps, 1e-5))
        op("dve", _memset(c_nh, -0.5))
        op("dve", _memset(c_ph, 0.5))
        op("dve", _memset(c_rk, RK))
        op("dve", _memset(ones_col, 1.0))
        op("dve", _memset(ones_row, 1.0))
        op("dve", _memset(accw, 0.0))
        op("dve", _memset(kT_hist, 0.0))
        op("dve", _memset(v_hist, 0.0), inc=sV)
        ms["memset"] = sV.n

        dma("act", identb, identb_d, dC)
        dma("act", identf, identf_d, dC)
        dma("act", masks, masks_d, dC)
        ms["consts_ld"] = dC.n

        _w = dma("act", bstage, b1_d.rearrange("(p k) -> p k", p=128), dY)
        _a = op("act", lambda e: e.activation(out=bstage_bf, in_=bstage,
                                              func=AF.Copy),
                waits=[(dY, _w)], inc=sA)
        _b = dma("act", bbf_d[0:HF].rearrange("(p k) -> p k", p=128), bstage_bf,
                 dY, waits=[(sA, _a)])
        dma("act", b1bf, bbf_d[0:HF].rearrange("(a k) -> a k", a=1), dY,
            waits=[(dY, _b)])
        _w = dma("act", bstage[:, 0:8], b2_d.rearrange("(p k) -> p k", p=128),
                 dY, waits=[(dY, _b)])
        _a = op("act", lambda e: e.activation(out=bstage_bf[:, 0:8],
                                              in_=bstage[:, 0:8], func=AF.Copy),
                waits=[(dY, _w)], inc=sA)
        _b = dma("act", bbf_d[HF:HF + D].rearrange("(p k) -> p k", p=128),
                 bstage_bf[:, 0:8], dY, waits=[(sA, _a)])
        dma("act", b2bf, bbf_d[HF:HF + D].rearrange("(a k) -> a k", a=1), dY,
            waits=[(dY, _b)])
        ms["bias_ld"] = dY.n

        xin_ld = {}
        for t0 in range(min(2, nt)):
            xin_ld[t0] = dma("syncq", xin[t0 % 2], xc[t0 * T:(t0 + 1) * T, :], dX)

        wc_v, wc_a = 0, 0
        stg_rd = {0: None, 1: None}
        pieces = []
        for i in range(8):
            pieces.append((wqkv_d[i * 128:(i + 1) * 128, 0:1536],
                           wqkv_sb[:, i * 3 * D:i * 3 * D + 1536], 1536))
        for i in range(8):
            pieces.append((wqkv_d[i * 128:(i + 1) * 128, 1536:3072],
                           wqkv_sb[:, i * 3 * D + 1536:
                                   i * 3 * D + 3072], 1536))
        for i in range(8):
            pieces.append((wout_d[i * 128:(i + 1) * 128, :],
                           wout_sb[:, i * D:(i + 1) * D], 1024))
        for j, (src, dst, n) in enumerate(pieces):
            buf = wstg_big[j % 2]
            ld = dma("act", buf[:, 0:n], src, dW[j % 2], waits=[stg_rd[j % 2]])
            ek = "dve" if j % 2 == 0 else "act"
            sm = sV if ek == "dve" else sA
            if ek == "dve":
                fn = (lambda e, o=dst, i_=buf[:, 0:n]:
                      e.tensor_copy(out=o, in_=i_))
            else:
                fn = (lambda e, o=dst, i_=buf[:, 0:n]: e.copy(out=o, in_=i_))
            v = op(ek, fn, waits=[(dW[j % 2], ld)], inc=sm)
            stg_rd[j % 2] = (sm, v)
            if ek == "dve":
                wc_v = v
            else:
                wc_a = v
            if j == 7:
                ms["wcast_q"] = (wc_v, wc_a)
            elif j == 15:
                ms["wcast_kv"] = (wc_v, wc_a)
        ms["wcast_v"] = wc_v
        ms["wcast_a"] = wc_a

        if mode == "S":
            v = op("dve", lambda e: e.tensor_copy(out=outbuf, in_=wqkv_sb[:, 0:D]),
                   waits=[(sV, wc_v), (sA, wc_a), (dX, dX.n), (dY, ms["bias_ld"]),
                          (dC, ms["consts_ld"])])
            o = dma("pool", y_d[0:T, :], outbuf, dO, waits=[(sV, v)])
            op("syncq", None, waits=[(dO, dO.n)])
            _emit_all(nc, OPS, _ES, cut)
            return nc

        prep_list = ([("a1", i) for i in range(64)] +
                     [("a2", i) for i in range(64)] +
                     [("f1",), ("f2",)] +
                     [("q1", i) for i in range(64)] +
                     [("q2", i) for i in range(64)])
        if mode == "A":
            prep_list = []
        prep_by_tile = [[] for _ in range(nt)]
        for j, task in enumerate(prep_list):
            prep_by_tile[min(nt - 1, j * nt // len(prep_list))].append(task)

        pst = {
            "stg_rd": {0: None, 1: None, 2: None, 3: None},
            "stg_ld": {},
            "acc": {"a1": 0, "a2": 0},
            "sgn_rd": None,
            "wq_rd": {0: None, 1: None},
            "w2st_rd": {0: None, 1: None},
            "ndma": 0,
            "cursor": 0,
            "pend": [],
        }

        def prep_src(kind, i):
            if kind in ("a1", "q1"):
                return w1_d[(i // 2) * 128:(i // 2 + 1) * 128,
                            (i % 2) * 512:(i % 2) * 512 + 512]
            return w2_d[(i // 8) * 128:(i // 8 + 1) * 128,
                        (i % 8) * 512:(i % 8) * 512 + 512]

        def prep_issue_next_dma():
            while pst["cursor"] < len(prep_list):
                task = prep_list[pst["cursor"]]
                pst["cursor"] += 1
                if task[0] in ("f1", "f2"):
                    continue
                slot = pst["ndma"] % 4
                pst["ndma"] += 1
                ld = dma("syncq", wstage[slot], prep_src(task[0], task[1]),
                         dP[slot], waits=[pst["stg_rd"][slot]])
                pst["stg_ld"][task] = (slot, ld)
                return

        def prep_fin(grp):
            kind = "a1" if grp == 0 else "a2"
            acc_slot = grp * 2 + pst["acc"][kind] % 2
            nelem = float(HF * D)
            pv = op("pe", lambda e, a=accw[:, acc_slot:acc_slot + 1]:
                    e.matmul(out=psF[0:1, 0:1], lhsT=a, rhs=ones_col,
                             start=True, stop=True),
                    waits=[(sV, ms["abs_%s_last" % kind]), psum_free["F"],
                           (sV, ms["memset"])], inc=sP)
            op("dve", lambda e: e.tensor_copy(out=wsc[0:1, 0:1],
                                              in_=psF[0:1, 0:1]),
               waits=[(sP, pv)])
            op("dve", lambda e, n=nelem: e.tensor_scalar(
                out=wsc[0:1, 1:2], in0=wsc[0:1, 0:1],
                scalar1=1.0 / n, scalar2=1e-5, op0=ALU.mult, op1=ALU.max))
            vr = op("dve", lambda e: e.reciprocal(out=wsc[0:1, 2:3],
                                                  in_=wsc[0:1, 1:2]), inc=sV)
            pv = op("pe", lambda e: e.matmul(out=psF[:, 0:1], lhsT=ones_row,
                                             rhs=wsc[0:1, 2:3],
                                             start=True, stop=True),
                    waits=[(sV, vr)], inc=sP)
            pv = op("pe", lambda e: e.matmul(out=psF[:, 1:2], lhsT=ones_row,
                                             rhs=wsc[0:1, 1:2],
                                             start=True, stop=True), inc=sP)
            dst_r = bc_r1 if grp == 0 else bc_r2
            dst_w = bc_ws1 if grp == 0 else bc_ws2
            op("dve", lambda e, d=dst_r: e.tensor_copy(out=d, in_=psF[:, 0:1]),
               waits=[(sP, pv)])
            v = op("dve", lambda e, d=dst_w: e.tensor_scalar(
                out=d, in0=psF[:, 1:2], scalar1=1.0 / 127.0, scalar2=None,
                op0=ALU.mult), inc=sV)
            psum_free["F"] = (sV, v)
            ms["wsc_%d" % grp] = v

        def prep_emit(task):
            kind = task[0]
            if kind == "f1":
                prep_fin(0); return
            if kind == "f2":
                prep_fin(1); return
            i = task[1]
            slot, ld = pst["stg_ld"][task]
            if kind in ("a1", "a2"):
                cnt = pst["acc"][kind]
                g = 0 if kind == "a1" else 1
                src_s = g * 2 + cnt % 2
                dst_s = g * 2 + (cnt + 1) % 2
                op("dve", lambda e, s=wstage[slot]: e.tensor_reduce(
                    out=accw[:, 4:5], in_=s, axis=AX.X, op=ALU.add,
                    apply_absolute_value=True),
                    waits=[(dP[slot], ld), (sV, ms["memset"])])
                v = op("dve", lambda e, a=accw[:, src_s:src_s + 1],
                       b=accw[:, dst_s:dst_s + 1]:
                       e.tensor_tensor(out=b, in0=a, in1=accw[:, 4:5],
                                       op=ALU.add), inc=sV)
                pst["acc"][kind] = cnt + 1
                pst["stg_rd"][slot] = (sV, v)
                ms["abs_%s_last" % kind] = v
            else:
                g = 0 if kind == "q1" else 1
                wqs = i % 2
                if any(p[2] == wqs for p in pst["pend"]):
                    prep_flushT()
                bc = bc_r1 if g == 0 else bc_r2
                op("dve", lambda e, s=wstage[slot], b=bc: e.tensor_scalar(
                    out=sgnA, in0=s, scalar1=b, scalar2=0.5,
                    op0=ALU.mult, op1=ALU.is_ge),
                    waits=[(dP[slot], ld), (sV, ms["wsc_%d" % g]),
                           pst["sgn_rd"]])
                va = op("dve", lambda e, s=wstage[slot], b=bc: e.tensor_scalar(
                    out=sgnB, in0=s, scalar1=b, scalar2=-0.5,
                    op0=ALU.mult, op1=ALU.is_ge), inc=sV)
                pst["stg_rd"][slot] = (sV, va)
                v = op("dve", lambda e, o=wqch[wqs]: e.scalar_tensor_tensor(
                    out=o, in0=sgnA, scalar=-1.0, in1=sgnB,
                    op0=ALU.add, op1=ALU.add),
                    waits=[pst["wq_rd"][wqs]], inc=sV)
                pst["sgn_rd"] = (sV, v)
                pst["pend"].append((kind, i, wqs, v))
            prep_issue_next_dma()

        def prep_flushT():
            for kind, i, wqs, vq in pst["pend"]:
                sp, key = next_stage()
                pv = None
                for j in range(4):
                    pv = op("pe", lambda e, jj=j, s=wqch[wqs], p=sp: e.transpose(
                        out=p[:, jj * 128:(jj + 1) * 128],
                        in_=s[:, jj * 128:(jj + 1) * 128], identity=identb),
                        waits=([(sV, vq), psum_free[key],
                                (dC, ms["consts_ld"])]
                               if j == 0 else []),
                        inc=(sP if j == 3 else None))
                pst["wq_rd"][wqs] = (sP, pv)
                if kind == "q1":
                    hb, half = i // 2, i % 2
                    dst = W1qT.rearrange("p (c h) -> p c h", c=DC)[
                        :, half * 4:half * 4 + 4, hb * 128:(hb + 1) * 128]
                    v = op("dve", lambda e, d=dst, p=sp: e.tensor_copy(
                        out=d,
                        in_=p[:, 0:512].rearrange("p (c h) -> p c h", c=4)),
                        waits=[(sP, pv), (sV, ms["wcast_v"]),
                               (sA, ms["wcast_a"])], inc=sV)
                    psum_free[key] = (sV, v)
                else:
                    db, s8 = i // 8, i % 8
                    st = i % 2
                    v = op("dve", lambda e, o=w2st[st], p=sp: e.tensor_copy(
                        out=o, in_=p[:, 0:512]),
                        waits=[(sP, pv), pst["w2st_rd"][st]], inc=sV)
                    psum_free[key] = (sV, v)
                    dst = w2t_d.rearrange("p (c d) -> p c d", c=32)[
                        :, s8 * 4:s8 * 4 + 4, db * 128:(db + 1) * 128]
                    b = dma("syncq", dst,
                            w2st[st].rearrange("p (c d) -> p c d", c=4),
                            dB[st], waits=[(sV, v)])
                    pst["w2st_rd"][st] = (dB[st], b)
            pst["pend"] = []

        for _ in range(4):
            prep_issue_next_dma()

        spill_ms = {}
        (CEL_SUM, CEL_SSQ, CEL_M, CEL_MSQ, CEL_VAR, CEL_LNV, CEL_RSTD,
         CEL_NMR) = range(8)
        CEL_NMX, CEL_SUMS, CEL_RCP = 8, 24, 40
        CEL2 = 56

        def a_ln1_stats(t):
            par = t % 2
            cl = cells[t % 2]
            va = op("act", lambda e, x=xin[par], c=cl: e.activation(
                out=outbuf, in_=x, func=AF.Square,
                accum_out=c[:, CEL_SSQ:CEL_SSQ + 1]),
                waits=[(dX, xin_ld[t])], inc=sA)
            op("dve", lambda e, x=xin[par], c=cl: e.tensor_reduce(
                out=c[:, CEL_SUM:CEL_SUM + 1], in_=x, axis=AX.X, op=ALU.add),
                waits=[(dX, xin_ld[t])])
            op("dve", lambda e, c=cl: e.tensor_scalar(
                out=c[:, CEL_M:CEL_M + 1], in0=c[:, CEL_SUM:CEL_SUM + 1],
                scalar1=1.0 / D, scalar2=None, op0=ALU.mult))
            op("dve", lambda e, c=cl: e.tensor_tensor(
                out=c[:, CEL_MSQ:CEL_MSQ + 1], in0=c[:, CEL_M:CEL_M + 1],
                in1=c[:, CEL_M:CEL_M + 1], op=ALU.mult))
            v = op("dve", lambda e, c=cl: e.scalar_tensor_tensor(
                out=c[:, CEL_VAR:CEL_VAR + 1], in0=c[:, CEL_SSQ:CEL_SSQ + 1],
                scalar=1.0 / D, in1=c[:, CEL_MSQ:CEL_MSQ + 1],
                op0=ALU.mult, op1=ALU.subtract), waits=[(sA, va)], inc=sV)
            op("act", lambda e, c=cl: e.activation(
                out=c[:, CEL_LNV:CEL_LNV + 1], in_=c[:, CEL_VAR:CEL_VAR + 1],
                func=AF.Ln, bias=c_eps),
                waits=[(sV, v), (sV, ms["memset"])])
            va = op("act", lambda e, c=cl: e.activation(
                out=c[:, CEL_RSTD:CEL_RSTD + 1], in_=c[:, CEL_LNV:CEL_LNV + 1],
                func=AF.Exp, scale=-0.5), inc=sA)
            ms[("rstd", t)] = va

        def a_ln1_apply(t):
            par = t % 2
            cl = cells[t % 2]
            v = op("dve", lambda e, c=cl: e.tensor_scalar(
                out=c[:, CEL_NMR:CEL_NMR + 1], in0=c[:, CEL_M:CEL_M + 1],
                scalar1=c[:, CEL_RSTD:CEL_RSTD + 1], scalar2=-1.0,
                op0=ALU.mult, op1=ALU.mult),
                waits=[(sA, ms[("rstd", t)])], inc=sV)
            va = op("act", lambda e, x=xin[par], c=cl: e.activation(
                out=xln, in_=x, func=AF.Identity,
                bias=c[:, CEL_NMR:CEL_NMR + 1],
                scale=c[:, CEL_RSTD:CEL_RSTD + 1]),
                waits=[(sV, v), (sA, ms.get(("xlnT", t - 1), 0))], inc=sA)
            ms[("xln", t)] = va

        def a_xlnT(t):
            sp, key = next_stage()
            pv = None
            for c in range(DC):
                pv = op("pe", lambda e, cc=c, p=sp: e.transpose(
                    out=p[:, cc * 128:(cc + 1) * 128],
                    in_=xln[:, cc * 128:(cc + 1) * 128], identity=identb),
                    waits=([(sA, ms[("xln", t)]), psum_free[key],
                            (dC, ms["consts_ld"])] if c == 0 else []),
                    inc=(sP if c == DC - 1 else None))
            v = op("act", lambda e, p=sp: e.copy(out=xlnT, in_=p),
                   waits=[(sP, pv), (sP, ms.get(("qkv", t - 1, 2), 0))],
                   inc=sA, nosync=True)
            psum_free[key] = (sA, v)
            ms[("xlnT", t)] = v

        def a_qkv(t, grp):
            par = t % 2
            psX, pfk = ((psA, "A"), (psB, "B"), (psC, "C"))[grp]
            wq = ms["wcast_q"] if grp == 0 else ms["wcast_kv"]
            w0 = [(sA, ms[("xlnT", t)]), psum_free[pfk],
                  (sV, wq[0]), (sA, wq[1])]
            pv = None
            for c in range(DC):
                for n in range(2):
                    pv = op("pe", lambda e, cc=c, nn=n, g=grp, p=psX:
                            e.matmul(
                        out=p[:, nn * 512:(nn + 1) * 512],
                        lhsT=xlnT[:, cc * 128:(cc + 1) * 128],
                        rhs=wqkv_sb[:, cc * 3 * D + g * D + nn * 512:
                                    cc * 3 * D + g * D + (nn + 1) * 512],
                        start=(cc == 0), stop=(cc == DC - 1)),
                        waits=(w0 if (c == 0 and n == 0) else []),
                        inc=(sP if (c == DC - 1 and n == 1) else None))
            ms[("qkv", t, grp)] = pv
            if grp == 0:
                va = op("act", lambda e: e.activation(out=q_sb, in_=psA,
                                                      func=AF.Copy,
                                                      scale=0.125),
                        waits=[(sP, pv), (sA, ms.get(("qT", t - 1), 0))],
                        inc=sA, nosync=True)
                ms[("q_sb", t)] = va
                psum_free["A"] = (sA, va)
            elif grp == 1:
                v = op("dve", lambda e: e.tensor_copy(out=k_sb, in_=psB),
                       waits=[(sP, pv), (sA, ms.get(("kT", t - 1), 0))],
                       inc=sV, nosync=True)
                ms[("k_sb", t)] = v
                psum_free["B"] = (sV, v)
            else:
                v = op("dve", lambda e, s=par: e.tensor_copy(
                    out=v_hist[:, s * D:(s + 1) * D], in_=psC),
                    waits=[(sP, pv), (sP, ms.get(("av", t - 1, 3), 0))],
                    inc=sV, nosync=True)
                ms[("v_sb", t)] = v
                psum_free["C"] = (sV, v)

        def a_qkT(t):
            par = t % 2
            for src_name, buf, wkey in (("q", q_sb, ("q_sb", t)),
                                        ("k", k_sb, ("k_sb", t))):
                wsem = sA if src_name == "q" else sV
                for r in range(2):
                    sp, key = next_stage()
                    pv = None
                    for hh in range(8):
                        h = r * 8 + hh
                        pv = op("pe", lambda e, b=buf, h_=h, hh_=hh, p=sp:
                                e.transpose(
                            out=p[0:64, hh_ * 128:(hh_ + 1) * 128],
                            in_=b[:, h_ * 64:(h_ + 1) * 64], identity=identb),
                            waits=([(wsem, ms[wkey]), psum_free[key]]
                                   if hh == 0 else []),
                            inc=(sP if hh == 7 else None))
                    if src_name == "q":
                        dst = qT[0:64, r * 1024:(r + 1) * 1024]
                    else:
                        dst = kT_hist[0:64, par * 2048 + r * 1024:
                                      par * 2048 + (r + 1) * 1024]
                    v = op("act", lambda e, d=dst, p=sp: e.copy(
                        out=d, in_=p[0:64, 0:1024]),
                        waits=[(sP, pv), (sP, ms.get(("scores", t - 1, 3), 0))],
                        inc=sA, nosync=True)
                    psum_free[key] = (sA, v)
                ms[("qT", t) if src_name == "q" else ("kT", t)] = v

        def a_scores(t, g):
            par, prv = t % 2, 1 - (t % 2)
            tvar = 1 if (t % TPS == 0) else 0
            psX = psA if g % 2 == 0 else psB
            pfk = "A" if g % 2 == 0 else "B"
            w0 = [(sA, ms[("qT", t)]), (sA, ms[("kT", t)]),
                  psum_free[pfk], (dC, ms["consts_ld"])]
            pv = None
            first = True
            for half in range(2):
                wh = w0
                for hh in (half * 2, half * 2 + 1):
                    h = 4 * g + hh
                    for kt, slot in ((0, prv), (1, par)):
                        pv = op("pe", lambda e, p=psX, hh_=hh, kt_=kt,
                                h_=h, sl=slot: e.matmul(
                            out=p[:, hh_ * 256 + kt_ * 128:
                                  hh_ * 256 + (kt_ + 1) * 128],
                            lhsT=qT[0:64, h_ * 128:(h_ + 1) * 128],
                            rhs=kT_hist[0:64, sl * 2048 + h_ * 128:
                                        sl * 2048 + (h_ + 1) * 128],
                            start=(hh_ % 2 == 0 and kt_ == 0), stop=False),
                            waits=(wh if first else []), inc=None)
                        first = False
                for hh in (half * 2, half * 2 + 1):
                    last = (hh == half * 2 + 1)
                    pv = op("pe", lambda e, p=psX, hh_=hh, mv=tvar, l=last:
                            e.matmul(
                                out=p[:, hh_ * 256:(hh_ + 1) * 256],
                                lhsT=identb,
                                rhs=masks[:, mv * 256:(mv + 1) * 256],
                                start=False, stop=l),
                            inc=(sP if last else None))
                if half == 0:
                    ms[("sc_h0", t, g)] = pv
            ms[("scores", t, g)] = pv

        def a_sm_max(t, g):
            cl = cells[t % 2]
            psX = psA if g % 2 == 0 else psB
            v0 = op("dve", lambda e, p=psX, c=cl, g_=g: e.tensor_reduce(
                out=c[:, CEL_NMX + 4 * g_:CEL_NMX + 4 * g_ + 2],
                in_=p[:, 0:512].rearrange("p (h k) -> p h k", h=2),
                axis=AX.X, op=ALU.max, negate=True),
                waits=[(sP, ms[("sc_h0", t, g)])], inc=sV)
            v = op("dve", lambda e, p=psX, c=cl, g_=g: e.tensor_reduce(
                out=c[:, CEL_NMX + 4 * g_ + 2:CEL_NMX + 4 * g_ + 4],
                in_=p[:, 512:1024].rearrange("p (h k) -> p h k", h=2),
                axis=AX.X, op=ALU.max, negate=True),
                waits=[(sP, ms[("scores", t, g)])], inc=sV, nosync=True)
            ms[("nmx0", t, g)] = v0
            ms[("nmx", t, g)] = v

        def a_sm_exp(t, g):
            cl = cells[t % 2]
            psX = psA if g % 2 == 0 else psB
            pfk = "A" if g % 2 == 0 else "B"
            va = None
            evl = []
            for hh in range(4):
                wv = [(sV, ms[("nmx0", t, g)] if hh < 2
                       else ms[("nmx", t, g)])]
                if g >= 2:
                    wv.append((sP, ms.get(("pT_mm", t, g - 2), 0)))
                else:
                    wv.append((sP, ms.get(("pT_mm", t - 1, g + 2), 0)))
                va = op("act", lambda e, p=psX, c=cl, g_=g, hh_=hh:
                        e.activation(
                    out=probs[g_ % 2][:, hh_ * 256:(hh_ + 1) * 256],
                    in_=p[:, hh_ * 256:(hh_ + 1) * 256], func=AF.Exp,
                    bias=c[:, CEL_NMX + 4 * g_ + hh_:
                           CEL_NMX + 4 * g_ + hh_ + 1],
                    accum_out=c[:, CEL_SUMS + 4 * g_ + hh_:
                                CEL_SUMS + 4 * g_ + hh_ + 1]),
                    waits=(wv if hh in (0, 2) else []),
                    inc=sA, nosync=(hh > 0))
                evl.append(va)
            ms[("exp", t, g)] = va
            ms[("expv", t, g)] = evl
            psum_free[pfk] = (sA, va)

        def a_sm_norm(t, g):
            cl = cells[t % 2]
            ev = ms[("expv", t, g)]
            r0 = op("dve", lambda e, c=cl, g_=g: e.reciprocal(
                out=c[:, CEL_RCP + 4 * g_:CEL_RCP + 4 * g_ + 2],
                in_=c[:, CEL_SUMS + 4 * g_:CEL_SUMS + 4 * g_ + 2]),
                waits=[(sA, ev[1])], inc=sV)
            rc = r0
            v2 = None
            for hh in range(4):
                if hh == 2:
                    rc = op("dve", lambda e, c=cl, g_=g: e.reciprocal(
                        out=c[:, CEL_RCP + 4 * g_ + 2:CEL_RCP + 4 * g_ + 4],
                        in_=c[:, CEL_SUMS + 4 * g_ + 2:
                               CEL_SUMS + 4 * g_ + 4]),
                        waits=[(sA, ev[3])], inc=sV)
                v2 = op("dve", lambda e, c=cl, g_=g, hh_=hh: e.tensor_scalar(
                    out=probsn[g_ % 2][:, hh_ * 256:(hh_ + 1) * 256],
                    in0=probs[g_ % 2][:, hh_ * 256:(hh_ + 1) * 256],
                    scalar1=c[:, CEL_RCP + 4 * g_ + hh_:
                              CEL_RCP + 4 * g_ + hh_ + 1],
                    scalar2=None, op0=ALU.mult),
                    waits=[(sV, rc)], inc=sV, nosync=(hh in (1, 3)))
                if hh == 1:
                    ms[("pn0", t, g)] = v2
            ms[("pn", t, g)] = v2

        def a_pt(t, g):
            sp, key = next_stage()
            pv = None
            for j in range(8):
                if j == 0:
                    w = [(sV, ms[("pn0", t, g)]), psum_free[key]]
                elif j == 4:
                    w = [(sV, ms[("pn", t, g)])]
                else:
                    w = []
                pv = op("pe", lambda e, jj=j, g_=g, p=sp: e.transpose(
                    out=p[:, jj * 128:(jj + 1) * 128],
                    in_=probsn[g_ % 2][:, jj * 128:(jj + 1) * 128],
                    identity=identb),
                    waits=w, inc=(sP if j == 7 else None))
            ms[("pT_mm", t, g)] = pv
            wv = [(sP, pv)]
            if g >= 1:
                wv.append((sP, ms.get(("av", t, g - 1), 0)))
            else:
                wv.append((sP, ms.get(("av", t - 1, 3), 0)))
            v = op("act", lambda e, g_=g, p=sp: e.activation(
                out=probsT[g_ % 2], in_=p, func=AF.Copy), waits=wv, inc=sA,
                nosync=True)
            psum_free[key] = (sA, v)
            ms[("pT", t, g)] = v

        def a_av(t, g):
            par, prv = t % 2, 1 - (t % 2)
            pv = None
            for hh in range(4):
                h = 4 * g + hh
                bp = 64 * (h % 2)
                p = h // 2
                for kt, slot in ((0, prv), (1, par)):
                    wv = [(sA, ms[("pT", t, g)]), (sV, ms[("v_sb", t)])]
                    if g == 0 and hh == 0 and kt == 0:
                        wv.append(psum_free["C"])
                    pv = op("pe", lambda e, hh_=hh, h_=h, bp_=bp, pp=p,
                            kt_=kt, sl=slot, g_=g: e.matmul(
                        out=psC[bp_:bp_ + 64, pp * 128:(pp + 1) * 128],
                        lhsT=v_hist[:, sl * D + h_ * 64:
                                    sl * D + h_ * 64 + 64],
                        rhs=probsT[g_ % 2][:, (hh_ * 2 + kt_) * 128:
                                           (hh_ * 2 + kt_ + 1) * 128],
                        start=(kt_ == 0), stop=(kt_ == 1),
                        tile_position=(0, bp_)),
                        waits=(wv if kt == 0 else []),
                        inc=(sP if (hh == 3 and kt == 1) else None))
            ms[("av", t, g)] = pv

        def a_outproj(t):
            v = op("dve", lambda e: e.tensor_copy(out=attnout, in_=psC),
                   waits=[(sP, ms[("av", t, 3)]),
                          (sP, ms.get(("outproj", t - 1), 0))], inc=sV,
                   nosync=True)
            ms[("attnout", t)] = v
            psum_free["C"] = (sV, v)
            pv = None
            for n in range(2):
                for c in range(DC):
                    wv = [(sV, ms[("attnout", t)]), psum_free["C"],
                          (sV, ms["wcast_v"]), (sA, ms["wcast_a"])]
                    pv = op("pe", lambda e, cc=c, nn=n: e.matmul(
                        out=psC[:, nn * 512:(nn + 1) * 512],
                        lhsT=attnout[:, cc * 128:(cc + 1) * 128],
                        rhs=wout_sb[:, cc * D + nn * 512:cc * D + (nn + 1) * 512],
                        start=(cc == 0), stop=(cc == DC - 1)),
                        waits=(wv if (n == 0 and c == 0) else []),
                        inc=(sP if (n == 1 and c == DC - 1) else None))
            ms[("outproj", t)] = pv

        def a_res(t):
            par = t % 2
            v = op("dve", lambda e, x=xin[par], o=x1buf[par]: e.tensor_tensor(
                out=o, in0=psC, in1=x, op=ALU.add),
                waits=[(sP, ms[("outproj", t)]), (dS, spill_ms.get(t - 2, 0))],
                inc=sV)
            ms[("res", t)] = v
            psum_free["C"] = (sV, v)
            spill_ms[t] = dma("syncq", x1_d[t * T:(t + 1) * T, :], x1buf[par],
                              dS, waits=[(sV, v)])

        def a_ln2_acc(t):
            par = t % 2
            cl = cells[t % 2]
            va = op("act", lambda e, o=x1buf[par], c=cl: e.activation(
                out=outbuf, in_=o, func=AF.Square,
                accum_out=c[:, CEL2:CEL2 + 1]),
                waits=[(sV, ms[("res", t)])], inc=sA)
            va = op("act", lambda e, o=x1buf[par], c=cl: e.activation(
                out=outbuf, in_=o, func=AF.Identity,
                accum_out=c[:, CEL2 + 1:CEL2 + 2]), inc=sA)
            ms[("ln2a", t)] = va

        def a_ln2_fin(t):
            cl = cells[t % 2]
            op("dve", lambda e, c=cl, t_=t: e.tensor_scalar(
                out=statsN[:, t_:t_ + 1], in0=c[:, CEL2 + 1:CEL2 + 2],
                scalar1=-1.0 / D, scalar2=None, op0=ALU.mult),
                waits=[(sA, ms[("ln2a", t)])])
            op("dve", lambda e, c=cl, t_=t: e.tensor_tensor(
                out=c[:, CEL2 + 2:CEL2 + 3], in0=statsN[:, t_:t_ + 1],
                in1=statsN[:, t_:t_ + 1], op=ALU.mult))
            v = op("dve", lambda e, c=cl: e.scalar_tensor_tensor(
                out=c[:, CEL2 + 3:CEL2 + 4], in0=c[:, CEL2:CEL2 + 1],
                scalar=1.0 / D, in1=c[:, CEL2 + 2:CEL2 + 3],
                op0=ALU.mult, op1=ALU.subtract), inc=sV)
            op("act", lambda e, c=cl: e.activation(
                out=c[:, CEL2 + 4:CEL2 + 5], in_=c[:, CEL2 + 3:CEL2 + 4],
                func=AF.Ln, bias=c_eps), waits=[(sV, v)])
            va = op("act", lambda e, c=cl, t_=t: e.activation(
                out=statsR[:, t_:t_ + 1], in_=c[:, CEL2 + 4:CEL2 + 5],
                func=AF.Exp, scale=-0.5), inc=sA)
            ms[("ln2", t)] = va

        gtile = 0
        a_ln1_stats(0)
        a_ln1_apply(0)
        a_xlnT(0)
        a_qkv(0, 0)
        a_qkv(0, 1)
        a_qkv(0, 2)
        for t in range(nt):
            fence(gtile)
            gtile += 1
            ptasks = list(prep_by_tile[t])

            def prep_drip(k):
                for _ in range(k):
                    if ptasks:
                        prep_emit(ptasks.pop(0))

            if t + 1 < nt:
                a_ln1_stats(t + 1)
            if t == 0:
                a_qkT(0)
            a_scores(t, 0)
            a_sm_max(t, 0)
            a_sm_exp(t, 0)
            if t > 0:
                a_ln2_fin(t - 1)
            a_scores(t, 1)
            a_sm_max(t, 1)
            a_sm_exp(t, 1)
            a_sm_norm(t, 0)
            a_scores(t, 2)
            a_sm_max(t, 2)
            prep_drip(2)
            a_pt(t, 0)
            a_av(t, 0)
            prep_flushT()
            a_sm_exp(t, 2)
            a_sm_norm(t, 1)
            if t + 1 < nt:
                a_ln1_apply(t + 1)
            a_scores(t, 3)
            a_sm_max(t, 3)
            prep_drip(2)
            a_pt(t, 1)
            a_av(t, 1)
            prep_flushT()
            a_sm_exp(t, 3)
            a_sm_norm(t, 2)
            if t + 1 < nt:
                a_xlnT(t + 1)
            prep_drip(2)
            a_pt(t, 2)
            a_av(t, 2)
            prep_flushT()
            if t + 1 < nt:
                a_qkv(t + 1, 0)
            a_sm_norm(t, 3)
            a_pt(t, 3)
            a_av(t, 3)
            prep_drip(2)
            if t + 1 < nt:
                a_qkv(t + 1, 1)
                a_qkT(t + 1)
            a_outproj(t)
            a_res(t)
            if t + 1 < nt:
                a_qkv(t + 1, 2)
            a_ln2_acc(t)
            if t + 2 < nt:
                xin_ld[t + 2] = dma(
                    "syncq", xin[t % 2], xc[(t + 2) * T:(t + 3) * T, :],
                    dX, waits=[(sV, ms[("res", t)]), (sA, ms[("xln", t)])])
            prep_drip(len(ptasks))
            prep_flushT()
            tile_end()
        a_ln2_fin(nt - 1)

        a_end = {ek: ENG_SEM[ek].n for ek in ENG_SEM}
        for ek in ("pe", "dve", "act", "pool", "syncq"):
            waits = [(ENG_SEM[o], a_end[o]) for o in ENG_SEM if o != ek]
            for s in (dS, dB[0], dB[1], dP[0], dP[1], dW[0], dW[1], dC, dY, dX):
                waits.append((s, s.n))
            op(ek, None, waits=waits)

        w2qt_lds = []
        for j in range(4):
            w2qt_lds.append(dma(
                "act", W2qT[:, j * 8192:(j + 1) * 8192],
                w2t_d[:, j * 8192:(j + 1) * 8192], dW[j % 2],
                waits=([(dB[0], dB[0].n), (dB[1], dB[1].n)] if j < 2 else [])))
        ms["w2qt0"] = w2qt_lds[2]
        ms["w2qt1"] = w2qt_lds[3]

        x1_ld = {}
        out_ms = {}
        FQ = 0
        FH = 12
        cbb = tib.bitcast(F32)

        def x1s(t):
            return (x1buf[0], x1buf[1], x1c)[t % 3]

        def emit_round(src_ap, dst_ap, scale_cell, bias_cell, bias_imm, nsub,
                       first_wait, war_wait, r0=0):
            subs = []
            for r in range(r0, r0 + nsub):
                wv = (list(first_wait) + list(war_wait)) if r == r0 else []
                sc2 = bias_cell if bias_cell is not None else bias_imm
                op("dve", lambda e, rr=r, s=src_ap, sc=scale_cell,
                   b=sc2: e.tensor_scalar(
                    out=t1b, in0=s[:, rr * 1024:(rr + 1) * 1024],
                    scalar1=sc, scalar2=b, op0=ALU.mult, op1=ALU.add),
                    waits=wv)
                op("dve", lambda e: e.tensor_copy(out=tib, in_=t1b))
                op("dve", lambda e: e.tensor_copy(out=tfb, in_=tib))
                op("dve", lambda e: e.tensor_tensor(
                    out=cbb, in0=t1b, in1=tfb, op=ALU.is_ge))
                rl = op("dve", lambda e, rr=r, d=dst_ap: e.scalar_tensor_tensor(
                    out=d[:, rr * 1024:(rr + 1) * 1024], in0=tfb,
                    scalar=-641.0, in1=cbb, op0=ALU.add, op1=ALU.add), inc=sV)
                subs.append(rl)
            return subs

        for t in range(min(3, nt)):
            x1_ld[t] = dma("syncq", x1s(t), x1_d[t * T:(t + 1) * T, :],
                           dL, waits=[(dS, dS.n)])

        def emit_w1_quarter(t, j):
            cl = cells[t % 2]
            psX = psA if j % 2 == 0 else psB
            pfk = "A" if j % 2 == 0 else "B"
            wv = [(sA, ms[("xqT", t)]), (sA, ms[("i1r", t)]),
                  psum_free[pfk], (dY, ms["bias_ld"])]
            pv = None
            for n in range(2):
                pv = op("pe", lambda e, jj=j, nn=n, p=psX: e.matmul(
                    out=p[:, nn * 512:(nn + 1) * 512], lhsT=invs1row,
                    rhs=b1bf[0:1, jj * 1024 + nn * 512:
                             jj * 1024 + (nn + 1) * 512],
                    start=True, stop=False),
                    waits=(wv if n == 0 else []), inc=None)
                for c in range(DC):
                    pv = op("pe", lambda e, jj=j, nn=n, cc=c, p=psX:
                            e.matmul(
                        out=p[:, nn * 512:(nn + 1) * 512],
                        lhsT=xqT[:, cc * 128:(cc + 1) * 128],
                        rhs=W1qT[:, cc * HF + jj * 1024 + nn * 512:
                                 cc * HF + jj * 1024 + (nn + 1) * 512],
                        start=False, stop=(cc == DC - 1)),
                        inc=(sP if (n == 1 and c == DC - 1) else None))
            ms[("w1", t, j)] = pv

        def _hqs(t, j):
            l = ms.get(("hqsub", t))
            return 0 if l is None else l[j]

        def emit_gelu(t, j):
            cl = cells[t % 2]
            psX = psA if j % 2 == 0 else psB
            pfk = "A" if j % 2 == 0 else "B"
            va = op("act", lambda e, jj=j, p=psX, c=cl: e.activation(
                out=h_sb[:, jj * 1024:(jj + 1) * 1024], in_=p,
                func=AF.Gelu, scale=c[:, FQ + 9:FQ + 10]),
                waits=[(sP, ms[("w1", t, j)]), (sV, _hqs(t - 1, j)),
                       (sV, ms[("fq", t)])], inc=sA, nosync=True)
            ms[("gelu", t, j)] = va
            psum_free[pfk] = (sA, va)
            op("dve", lambda e, jj=j, c=cl: e.tensor_reduce(
                out=c[:, FH + jj:FH + jj + 1],
                in_=h_sb[:, jj * 1024:(jj + 1) * 1024], axis=AX.X,
                op=ALU.max, apply_absolute_value=True),
                waits=[(sA, va)], nosync=True)

        def f_front_dve(t):
            cl = cells[t % 2]
            op("dve", lambda e, x=x1s(t), c=cl: e.tensor_reduce(
                out=c[:, FQ:FQ + 1], in_=x, axis=AX.X, op=ALU.max),
                waits=[(dL, x1_ld[t]), (sA, ms.get(("gelu", t - 2, 3), 0)),
                       (sP, ms.get(("sT1", t - 2), 0))])
            op("dve", lambda e, x=x1s(t), c=cl: e.tensor_reduce(
                out=c[:, FQ + 1:FQ + 2], in_=x, axis=AX.X, op=ALU.min))
            op("dve", lambda e, c=cl, t_=t: e.tensor_tensor(
                out=c[:, FQ + 2:FQ + 3], in0=c[:, FQ:FQ + 1],
                in1=statsN[:, t_:t_ + 1], op=ALU.add))
            op("dve", lambda e, c=cl, t_=t: e.tensor_tensor(
                out=c[:, FQ + 3:FQ + 4], in0=c[:, FQ + 1:FQ + 2],
                in1=statsN[:, t_:t_ + 1], op=ALU.add))
            op("dve", lambda e, c=cl: e.tensor_scalar(
                out=c[:, FQ + 4:FQ + 5], in0=c[:, FQ + 3:FQ + 4],
                scalar1=-1.0, scalar2=None, op0=ALU.mult))
            op("dve", lambda e, c=cl: e.tensor_tensor(
                out=c[:, FQ + 5:FQ + 6], in0=c[:, FQ + 2:FQ + 3],
                in1=c[:, FQ + 4:FQ + 5], op=ALU.max))
            op("dve", lambda e, c=cl: e.reciprocal(
                out=c[:, FQ + 6:FQ + 7], in_=c[:, FQ + 5:FQ + 6]))
            op("dve", lambda e, c=cl: e.tensor_scalar(
                out=c[:, FQ + 7:FQ + 8], in0=c[:, FQ + 6:FQ + 7],
                scalar1=127.0, scalar2=None, op0=ALU.mult))
            op("dve", lambda e, c=cl, t_=t: e.tensor_tensor(
                out=c[:, FQ + 8:FQ + 9], in0=c[:, FQ + 5:FQ + 6],
                in1=statsR[:, t_:t_ + 1], op=ALU.mult))
            op("dve", lambda e, c=cl: e.tensor_scalar(
                out=c[:, FQ + 9:FQ + 10], in0=c[:, FQ + 8:FQ + 9],
                scalar1=bc_ws1, scalar2=None, op0=ALU.mult))
            op("dve", lambda e, c=cl: e.reciprocal(
                out=c[:, FQ + 10:FQ + 11], in_=c[:, FQ + 9:FQ + 10]))
            v = op("dve", lambda e, c=cl, t_=t: e.tensor_scalar(
                out=c[:, FQ + 11:FQ + 12], in0=statsN[:, t_:t_ + 1],
                scalar1=c[:, FQ + 7:FQ + 8], scalar2=RK,
                op0=ALU.mult, op1=ALU.add), inc=sV)
            ms[("fq", t)] = v

            rl = emit_round(x1s(t), xq,
                            cl[:, FQ + 7:FQ + 8], cl[:, FQ + 11:FQ + 12], None,
                            1, [(sV, ms[("fq", t)])],
                            [(sP, ms.get(("xqT_mm", t - 1), 0))])
            ms[("xq", t)] = rl[-1]

        def f_front_pe(t):
            cl = cells[t % 2]
            pv = op("pe", lambda e, c=cl: e.transpose(
                out=psF[0:1, 0:128], in_=c[:, FQ + 10:FQ + 11],
                identity=identf),
                waits=[(sV, ms[("fq", t)]), psum_free["F"],
                       (dC, ms["consts_ld"]),
                       (sP, ms.get(("w1", t - 1, 3), 0))], inc=sP)
            ms[("sT1", t)] = pv
            va = op("act", lambda e: e.activation(
                out=invs1row, in_=psF[0:1, 0:128], func=AF.Copy),
                waits=[(sP, pv)], inc=sA, nosync=True)
            ms[("i1r", t)] = va
            psum_free["F"] = (sA, va)

            sp, key = next_stage()
            pv = None
            for c in range(DC):
                pv = op("pe", lambda e, cc=c, p=sp: e.transpose(
                    out=p[:, cc * 128:(cc + 1) * 128],
                    in_=xq[:, cc * 128:(cc + 1) * 128], identity=identb),
                    waits=([(sV, ms[("xq", t)]), psum_free[key]]
                           if c == 0 else []),
                    inc=(sP if c == DC - 1 else None))
            ms[("xqT_mm", t)] = pv
            v = op("act", lambda e, p=sp: e.activation(
                out=xqT, in_=p, func=AF.Copy),
                waits=[(sP, pv), (sP, ms.get(("w1", t - 1, 3), 0))], inc=sA,
                nosync=True)
            psum_free[key] = (sA, v)
            ms[("xqT", t)] = v

            emit_w1_quarter(t, 0)
            emit_w1_quarter(t, 1)

        def f_fh(t):
            cl = cells[t % 2]
            op("dve", lambda e, c=cl: e.tensor_reduce(
                out=c[:, FH + 4:FH + 5], in_=c[:, FH:FH + 4], axis=AX.X,
                op=ALU.max))
            op("dve", lambda e, c=cl: e.reciprocal(
                out=c[:, FH + 8:FH + 9], in_=c[:, FH + 4:FH + 5]))
            op("dve", lambda e, c=cl: e.tensor_scalar(
                out=c[:, FH + 5:FH + 6], in0=c[:, FH + 8:FH + 9],
                scalar1=127.0, scalar2=None, op0=ALU.mult))
            op("dve", lambda e, c=cl: e.tensor_scalar(
                out=c[:, FH + 6:FH + 7], in0=c[:, FH + 4:FH + 5],
                scalar1=bc_ws2, scalar2=None, op0=ALU.mult))
            v = op("dve", lambda e, c=cl: e.reciprocal(
                out=c[:, FH + 7:FH + 8], in_=c[:, FH + 6:FH + 7]), inc=sV)
            ms[("fh", t)] = v

        def f_round0(t):
            cl = cells[t % 2]
            rl = emit_round(h_sb, hq, cl[:, FH + 5:FH + 6], None, RK, 1,
                            [(sV, ms[("fh", t)])],
                            [(sP, ms.get(("hqT_mm", t - 1), 0))])
            ms[("hqsub", t)] = rl

        def f_rounds123(t):
            cl = cells[t % 2]
            rl = emit_round(h_sb, hq, cl[:, FH + 5:FH + 6], None, RK, 3,
                            [], [], r0=1)
            ms[("hq", t)] = rl[-1]
            ms[("hqsub", t)] = ms[("hqsub", t)] + rl

        def f_back_T0(t):
            cl = cells[t % 2]
            pv = op("pe", lambda e, c=cl: e.transpose(
                out=psF[0:1, 128:256], in_=c[:, FH + 7:FH + 8],
                identity=identf),
                waits=[(sV, ms[("fh", t)]), psum_free["F"],
                       (sP, ms.get(("w2", t - 1), 0))], inc=sP)
            va = op("act", lambda e: e.activation(
                out=invs2row, in_=psF[0:1, 128:256], func=AF.Copy),
                waits=[(sP, pv)], inc=sA, nosync=True)
            ms[("i2r", t)] = va
            psum_free["F"] = (sA, va)
            _fT(t, (0,))

        def f_out(t):
            cl = cells[t % 2]
            v = op("dve", lambda e, c=cl, x=x1s(t): e.scalar_tensor_tensor(
                out=outbuf, in0=psC, scalar=c[:, FH + 6:FH + 7], in1=x,
                op0=ALU.mult, op1=ALU.add),
                waits=[(sP, ms[("w2", t)]), (dO, out_ms.get(t - 1, 0))],
                inc=sV)
            ms[("fout", t)] = v
            psum_free["C"] = (sV, v)
            out_ms[t] = dma("pool", y_d[t * T:(t + 1) * T, :], outbuf, dO,
                            waits=[(sV, v)])
            if t + 3 < nt:
                x1_ld[t + 3] = dma("syncq", x1s(t + 3),
                                   x1_d[(t + 3) * T:(t + 4) * T, :], dL,
                                   waits=[(sV, v)])

        def _fT(t, rrs):
            wv = [(sA, ms[("i2r", t)]),
                  (sV, ms.get(("fout", t - 1), 0)), (dW[0], ms["w2qt0"]),
                  (dW[1], ms["w2qt1"]), (dY, ms["bias_ld"])]
            pv = None
            evs = None
            for rr in rrs:
                sp, key = next_stage()
                for j in range(8):
                    pv = op("pe", lambda e, rr_=rr, jj=j, p=sp: e.transpose(
                        out=p[:, jj * 128:(jj + 1) * 128],
                        in_=hq[:, rr_ * 1024 + jj * 128:
                               rr_ * 1024 + (jj + 1) * 128],
                        identity=identb),
                        waits=([(sV, ms[("hqsub", t)][rr]), psum_free[key]]
                               if j == 0 else []),
                        inc=(sP if j == 7 else None))
                if rr == 3:
                    ms[("hqT_mm", t)] = pv
                evs = op("act", lambda e, rr_=rr, p=sp: e.activation(
                    out=hqT[:, rr_ * 1024:(rr_ + 1) * 1024], in_=p,
                    func=AF.Copy),
                    waits=[(sP, pv)] + ([(sP, ms.get(("w2", t - 1), 0))]
                                        if rr == 0 else []), inc=sA,
                    nosync=True)
                psum_free[key] = (sA, evs)
                for n in range(2):
                    if rr == 0:
                        op("pe", lambda e, nn=n: e.matmul(
                            out=psC[:, nn * 512:(nn + 1) * 512],
                            lhsT=invs2row,
                            rhs=b2bf[0:1, nn * 512:(nn + 1) * 512],
                            start=True, stop=False),
                            waits=(wv + [(sA, evs)]) if n == 0 else [],
                            inc=None)
                    for k in range(rr * 8, rr * 8 + 8):
                        pv = op("pe", lambda e, nn=n, kk=k: e.matmul(
                            out=psC[:, nn * 512:(nn + 1) * 512],
                            lhsT=hqT[:, kk * 128:(kk + 1) * 128],
                            rhs=W2qT[:, kk * 1024 + nn * 512:
                                     kk * 1024 + (nn + 1) * 512],
                            start=False, stop=(kk == 31)),
                            waits=([(sA, evs)] if (n == 0 and k == rr * 8)
                                   else []),
                            inc=(sP if (n == 1 and k == 31) else None))
            ms[("hqT", t)] = evs
            ms[("w2", t)] = pv

        f_front_dve(0)
        f_front_pe(0)
        if nt > 1:
            f_front_dve(1)
        emit_gelu(0, 0)
        emit_gelu(0, 1)
        emit_w1_quarter(0, 2)
        emit_gelu(0, 2)
        emit_w1_quarter(0, 3)
        emit_gelu(0, 3)
        f_fh(0)
        for t in range(nt):
            fence(gtile)
            gtile += 1
            f_round0(t)
            if t + 1 < nt:
                f_front_pe(t + 1)
            f_back_T0(t)
            f_rounds123(t)
            if t + 2 < nt:
                f_front_dve(t + 2)
            _fT(t, (1,))
            if t + 1 < nt:
                emit_gelu(t + 1, 0)
                emit_gelu(t + 1, 1)
                emit_w1_quarter(t + 1, 2)
            _fT(t, (2,))
            if t + 1 < nt:
                emit_gelu(t + 1, 2)
                emit_w1_quarter(t + 1, 3)
            _fT(t, (3,))
            if t + 1 < nt:
                emit_gelu(t + 1, 3)
                f_fh(t + 1)
            f_out(t)
            tile_end()

        op("syncq", None, waits=[(dO, dO.n)])
        _emit_all(nc, OPS, _ES, cut)

    return nc


def _emit_all(nc, OPS, _ES, cut=None):
    if cut is not None:
        OPS = OPS[:cut]
        tot = {}
        for (k, waits, fn, inc, amt, is_dma, ns) in OPS:
            if inc is not None:
                tot[inc.name] = tot.get(inc.name, 0) + amt
        OPS = [(k, [(s, min(v, tot.get(s.name, 0))) for (s, v) in waits],
                fn, inc, amt, is_dma, ns)
               for (k, waits, fn, inc, amt, is_dma, ns) in OPS]

    def mk(ek):
        zs = _ES.get(ek)

        def run(eng):
            lastw = {}
            zi = 0
            for (k, waits, fn, inc, amt, is_dma, ns) in OPS:
                if k != ek:
                    continue
                for (s, val) in waits:
                    if val > 0 and lastw.get(s.name, 0) < val:
                        eng.wait_ge(s.h, val)
                        lastw[s.name] = val
                if fn is None:
                    continue
                if zs is not None and zi > 0 and not ns:
                    eng.wait_ge(zs.h, zi)
                inst = fn(eng)
                if zs is not None and not is_dma:
                    zi += 1
                if inc is not None:
                    inst.then_inc(inc.h, amt)
        return run

    with nc.Block() as block:
        block.tensor(mk("pe"))
        block.vector(mk("dve"))
        block.scalar(mk("act"))
        block.gpsimd(mk("pool"))
        block.sync(mk("syncq"))


def _consts():
    identb = np.eye(128, dtype=ml_dtypes.bfloat16)
    identf = np.eye(128, dtype=np.float32)
    a = np.arange(128)[:, None]
    b = np.arange(128)[None, :]
    m_prev = np.where(b > a, 0.0, NEG).astype(np.float32)
    m_cur = np.where(b <= a, 0.0, NEG).astype(np.float32)
    normal = np.concatenate([m_prev, m_cur], axis=1)
    t0 = np.concatenate([np.full((128, 128), NEG, np.float32), m_cur], axis=1)
    masks = np.concatenate([normal, t0], axis=1).astype(ml_dtypes.bfloat16)
    return identb, identf, masks


_NC_CACHE = {}


def kernel(L0_attn_x, L0_attn_w_qkv, L0_attn_w_out, L0_ffn_W1, L0_ffn_b1,
           L0_ffn_W2, L0_ffn_b2):
    x = np.ascontiguousarray(np.asarray(L0_attn_x, dtype=np.float32))
    wqkv = np.ascontiguousarray(np.asarray(L0_attn_w_qkv, dtype=np.float32))
    wout = np.ascontiguousarray(np.asarray(L0_attn_w_out, dtype=np.float32))
    w1 = np.ascontiguousarray(np.asarray(L0_ffn_W1, dtype=np.float32))
    w2 = np.ascontiguousarray(np.asarray(L0_ffn_W2, dtype=np.float32))
    b1 = np.ascontiguousarray(np.asarray(L0_ffn_b1, dtype=np.float32))
    b2 = np.ascontiguousarray(np.asarray(L0_ffn_b2, dtype=np.float32))
    identb, identf, masks = _consts()

    nt = SEQ_PC * TPS
    if nt not in _NC_CACHE:
        _NC_CACHE[nt] = _build(nt)
    nc = _NC_CACHE[nt]

    in_maps = []
    for c in range(NCORES):
        xs = np.ascontiguousarray(
            x[c * SEQ_PC:(c + 1) * SEQ_PC].reshape(SEQ_PC * S, D))
        in_maps.append({
            "xc": xs, "wqkv": wqkv, "wout": wout, "w1": w1, "w2": w2,
            "b1": b1, "b2": b2, "identb": identb, "identf": identf,
            "masks": masks,
        })
    res = run_bass_kernel_spmd(nc, in_maps, core_ids=list(range(NCORES)))
    out = np.concatenate(
        [r["y"].reshape(SEQ_PC, S, D) for r in res.results], axis=0)
    return out


# version 79
# speedup vs baseline: 1.0499x; 1.0373x over previous
import numpy as np
import ml_dtypes
from contextlib import ExitStack

import concourse.bass as bass
import concourse.mybir as mybir
from concourse.bass_utils import run_bass_kernel_spmd

F32 = mybir.dt.float32
BF16 = mybir.dt.bfloat16
I32 = mybir.dt.int32
AF = mybir.ActivationFunctionType
ALU = mybir.AluOpType
AX = mybir.AxisListType

B, S, D = 16, 2048, 1024
NCORES = 8
SEQ_PC = B // NCORES
T = 128
TPS = S // T
DC = D // 128
HN, DH = 16, 64
HF = 4096
NEG = -1.0e9
RK = 640.5

_SB_BASE = 16640
_SB_TOP = 229376


class _Sem:
    __slots__ = ("name", "n", "h")

    def __init__(self, name):
        self.name = name
        self.n = 0
        self.h = None


def _off(ap):
    return ap.tensor.manual_sbuf_range[0]


def _build(nt, mode="full", cut=None):
    ntok = nt * T
    nc = bass.Bass("TRN2", target_bir_lowering=False, debug=False)

    xc = nc.dram_tensor("xc", [ntok, D], F32, kind="ExternalInput").ap()
    wqkv_d = nc.dram_tensor("wqkv", [D, 3 * D], F32, kind="ExternalInput").ap()
    wout_d = nc.dram_tensor("wout", [D, D], F32, kind="ExternalInput").ap()
    w1_d = nc.dram_tensor("w1", [HF, D], F32, kind="ExternalInput").ap()
    w2_d = nc.dram_tensor("w2", [D, HF], F32, kind="ExternalInput").ap()
    b1_d = nc.dram_tensor("b1", [HF], F32, kind="ExternalInput").ap()
    b2_d = nc.dram_tensor("b2", [D], F32, kind="ExternalInput").ap()
    identb_d = nc.dram_tensor("identb", [128, 128], BF16, kind="ExternalInput").ap()
    identf_d = nc.dram_tensor("identf", [128, 128], F32, kind="ExternalInput").ap()
    masks_d = nc.dram_tensor("masks", [128, 512], BF16, kind="ExternalInput").ap()
    y_d = nc.dram_tensor("y", [ntok, D], F32, kind="ExternalOutput").ap()
    x1_d = nc.dram_tensor("x1sp", [ntok, D], F32, kind="Internal").ap()
    w2t_d = nc.dram_tensor("w2td", [128, 32 * 1024], BF16, kind="Internal").ap()
    bbf_d = nc.dram_tensor("bbfd", [HF + D], BF16, kind="Internal").ap()

    cur = [_SB_BASE]

    def sb(name, p, n, dt, at=None):
        size = n * mybir.dt.size(dt)
        o = cur[0] if at is None else at
        o = (o + 31) & ~31
        h = nc._state.alloc_sbuf_tensor_at(name, [p, n], dt, offset=o)
        if at is None:
            cur[0] = o + size
        return h.ap()

    identb = sb("identb", 128, 128, BF16)
    identf = sb("identf", 128, 128, F32)
    masks = sb("masks", 128, 512, BF16)
    b1bf = sb("b1bf", 1, HF, BF16)
    b2bf = sb("b2bf", 1, D, BF16)
    cells = [sb("cells0", 128, 64, F32), sb("cells1", 128, 64, F32)]
    statsN = sb("statsN", 128, nt, F32)
    statsR = sb("statsR", 128, nt, F32)
    _c4 = sb("c4", 128, 5, F32)
    c_eps = _c4[:, 0:1]
    c_nh = _c4[:, 1:2]
    c_ph = _c4[:, 2:3]
    ones_col = _c4[:, 3:4]
    c_rk = _c4[:, 4:5]
    ones_row = sb("ones_row", 1, 128, F32)
    _b4 = sb("b4", 128, 4, F32)
    bc_r1 = _b4[:, 0:1]
    bc_ws1 = _b4[:, 1:2]
    bc_r2 = _b4[:, 2:3]
    bc_ws2 = _b4[:, 3:4]
    accw = sb("accw", 128, 8, F32)
    wsc = sb("wsc", 1, 8, F32)
    x1buf = [sb("x1b0", 128, D, F32), sb("x1b1", 128, D, F32)]
    outbuf = sb("outbuf", 128, D, F32)
    invs1row = sb("i1row", 1, 128, BF16)
    invs2row = sb("i2row", 1, 128, BF16)

    A0 = (cur[0] + 63) & ~63
    cur[0] = A0
    wqkv_sb = sb("wqkv_sb", 128, DC * 3 * D, BF16)
    wout_sb = sb("wout_sb", 128, DC * D, BF16)
    xin = [sb("xin0", 128, D, F32), sb("xin1", 128, D, F32)]
    xln = sb("xln", 128, D, BF16)
    xlnT = sb("xlnT", 128, D, BF16)
    q_sb = sb("q_sb", 128, D, BF16)
    k_sb = sb("k_sb", 128, D, BF16)
    qT = sb("qT", 64, 2048, BF16)
    kT_hist = sb("kT_hist", 64, 2 * 2048, BF16)
    v_hist = sb("v_hist", 128, 2 * D, BF16)
    probs = [sb("probs0", 128, 1024, BF16), sb("probs1", 128, 1024, BF16)]
    probsn = probs
    probsT_b = sb("probsT", 128, 1024, BF16)
    probsT = [probsT_b, probsT_b]
    attnout = sb("attnout", 128, 1024, BF16)
    wstage = [sb("wstage0", 128, 512, F32), sb("wstage1", 128, 512, F32),
              sb("wstage2", 128, 512, F32), sb("wstage3", 128, 512, F32)]
    sgnA = sb("sgnA", 128, 512, BF16)
    sgnB = sb("sgnB", 128, 512, BF16)
    wqch = [sb("wqch0", 128, 512, BF16), sb("wqch1", 128, 512, BF16)]
    w2st = [sb("w2st0", 128, 512, BF16), sb("w2st1", 128, 512, BF16)]
    A_END = cur[0]

    w1t_at = (229368 - DC * HF * 2) & ~63
    W1qT = sb("W1qT", 128, DC * HF, BF16, at=w1t_at)
    assert A_END <= w1t_at, (A_END, w1t_at)

    cur[0] = A0
    W2qT = sb("W2qT", 128, 32 * D, BF16)
    h_sb = sb("h_sb", 128, HF, F32)
    hq = sb("hq", 128, HF, BF16)
    hqT = sb("hqT", 128, HF, BF16)
    t1b = sb("t1a", 128, 1024, F32)
    tib = sb("tib", 128, 1024, I32)
    tfb = sb("tfb", 128, 1024, F32)
    x1c = sb("x1c", 128, D, F32)
    xq = sb("xq", 128, D, BF16)
    xqT = sb("xqT", 128, D, BF16)
    F_END = cur[0]
    assert F_END <= w1t_at, (F_END, w1t_at)

    wstg_big = [
        sb("wstgA", 128, 1536, F32, at=w1t_at),
        sb("wstgB", 128, 1536, F32, at=w1t_at + 1536 * 4),
    ]
    bstage = sb("bstage", 128, 32, F32, at=w1t_at + 1536 * 8)
    bstage_bf = sb("bstageb", 128, 32, BF16, at=w1t_at + 1536 * 8 + 128)

    with ExitStack() as ctx:
        psA = ctx.enter_context(nc.psum_tensor("psA", [128, 1024], F32)).ap()
        psB = ctx.enter_context(nc.psum_tensor("psB", [128, 1024], F32)).ap()
        psC = ctx.enter_context(nc.psum_tensor("psC", [128, 1024], F32)).ap()
        psD = ctx.enter_context(nc.psum_tensor("psD", [128, 1024], BF16)).ap()
        psF = ctx.enter_context(nc.psum_tensor("psF", [128, 512], F32)).ap()

        sP = _Sem("sP"); sV = _Sem("sV"); sA = _Sem("sA"); sG = _Sem("sG")
        dX = _Sem("dX"); dS = _Sem("dS"); dL = _Sem("dL"); dO = _Sem("dO")
        dC = _Sem("dC"); dY = _Sem("dY")
        dW = [_Sem("dW0"), _Sem("dW1")]
        dP = [_Sem("dP0"), _Sem("dP1"), _Sem("dP2"), _Sem("dP3")]
        dB = [_Sem("dB0"), _Sem("dB1")]
        SEMS = [sP, sV, sA, sG, dX, dS, dL, dO, dC, dY] + dW + dP + dB
        for s in SEMS:
            s.h = ctx.enter_context(nc.semaphore(s.name))

        OPS = []
        _ES = {"dve": sV, "act": sA, "pool": sG}

        def op(ek, fn, waits=(), inc=None, amt=1, is_dma=False,
               nosync=False):
            if fn is not None and not is_dma and ek in _ES:
                inc = _ES[ek]
                amt = 1
            OPS.append((ek, [w for w in waits if w is not None and w[1] > 0],
                        fn, inc, amt, is_dma, nosync))
            if inc is not None:
                inc.n += amt
                return inc.n
            return None

        def dma(ek, out_ap, in_ap, sem, waits=()):
            w = [(sem, sem.n)] + list(waits)
            return op(ek, (lambda e, o=out_ap, i=in_ap: e.dma_start(out=o, in_=i)),
                      w, sem, 16, is_dma=True)

        ms = {}

        def tickle(n=2):
            for _ in range(n):
                op("pe", lambda e: e.ldweights(identb))

        def tickle_train(sem, target, step=3, n=4, per=3):
            for k in range(n, 0, -1):
                v = target - k * step
                if v > 0:
                    for _ in range(per):
                        op("pe", lambda e: e.ldweights(identb),
                           waits=[(sem, v)])

        psum_free = {"A": None, "B": None, "C": None, "D": None, "F": None}
        psD2 = psF.bitcast(BF16)
        _stg = {"t": 0}

        def next_stage():
            _stg["t"] ^= 1
            return (psD, "D") if _stg["t"] else (psD2, "F")
        eng_end = {k: [0] for k in ("pe", "dve", "act", "pool")}
        ENG_SEM = {"pe": sP, "dve": sV, "act": sA, "pool": sG}

        def fence(gt):
            if gt < 2:
                return
            for ek in ("pe", "dve", "act", "pool"):
                waits = [(ENG_SEM[o], eng_end[o][gt - 1])
                         for o in ("pe", "dve", "act", "pool") if o != ek]
                op(ek, None, waits=waits)

        def tile_end():
            for ek in ("pe", "dve", "act", "pool"):
                eng_end[ek].append(ENG_SEM[ek].n)

        def _memset(ap_, val):
            return lambda e, a=ap_, v=val: e.memset(a, v)

        op("dve", _memset(c_eps, 1e-5))
        op("dve", _memset(c_nh, -0.5))
        op("dve", _memset(c_ph, 0.5))
        op("dve", _memset(c_rk, RK))
        op("dve", _memset(ones_col, 1.0))
        op("dve", _memset(ones_row, 1.0))
        op("dve", _memset(accw, 0.0))
        op("dve", _memset(kT_hist, 0.0))
        op("dve", _memset(v_hist, 0.0), inc=sV)
        ms["memset"] = sV.n

        dma("act", identb, identb_d, dC)
        dma("act", identf, identf_d, dC)
        dma("act", masks, masks_d, dC)
        ms["consts_ld"] = dC.n

        _w = dma("act", bstage, b1_d.rearrange("(p k) -> p k", p=128), dY)
        _a = op("act", lambda e: e.activation(out=bstage_bf, in_=bstage,
                                              func=AF.Copy),
                waits=[(dY, _w)], inc=sA)
        _b = dma("act", bbf_d[0:HF].rearrange("(p k) -> p k", p=128), bstage_bf,
                 dY, waits=[(sA, _a)])
        dma("act", b1bf, bbf_d[0:HF].rearrange("(a k) -> a k", a=1), dY,
            waits=[(dY, _b)])
        _w = dma("act", bstage[:, 0:8], b2_d.rearrange("(p k) -> p k", p=128),
                 dY, waits=[(dY, _b)])
        _a = op("act", lambda e: e.activation(out=bstage_bf[:, 0:8],
                                              in_=bstage[:, 0:8], func=AF.Copy),
                waits=[(dY, _w)], inc=sA)
        _b = dma("act", bbf_d[HF:HF + D].rearrange("(p k) -> p k", p=128),
                 bstage_bf[:, 0:8], dY, waits=[(sA, _a)])
        dma("act", b2bf, bbf_d[HF:HF + D].rearrange("(a k) -> a k", a=1), dY,
            waits=[(dY, _b)])
        ms["bias_ld"] = dY.n

        xin_ld = {}
        for t0 in range(min(2, nt)):
            xin_ld[t0] = dma("syncq", xin[t0 % 2], xc[t0 * T:(t0 + 1) * T, :], dX)

        wc_v, wc_a = 0, 0
        stg_rd = {0: None, 1: None}
        pieces = []
        for i in range(8):
            pieces.append((wqkv_d[i * 128:(i + 1) * 128, 0:1536],
                           wqkv_sb[:, i * 3 * D:i * 3 * D + 1536], 1536))
        for i in range(8):
            pieces.append((wqkv_d[i * 128:(i + 1) * 128, 1536:3072],
                           wqkv_sb[:, i * 3 * D + 1536:
                                   i * 3 * D + 3072], 1536))
        for i in range(8):
            pieces.append((wout_d[i * 128:(i + 1) * 128, :],
                           wout_sb[:, i * D:(i + 1) * D], 1024))
        for j, (src, dst, n) in enumerate(pieces):
            buf = wstg_big[j % 2]
            ld = dma("act", buf[:, 0:n], src, dW[j % 2], waits=[stg_rd[j % 2]])
            ek = "dve" if j % 2 == 0 else "act"
            sm = sV if ek == "dve" else sA
            if ek == "dve":
                fn = (lambda e, o=dst, i_=buf[:, 0:n]:
                      e.tensor_copy(out=o, in_=i_))
            else:
                fn = (lambda e, o=dst, i_=buf[:, 0:n]: e.copy(out=o, in_=i_))
            v = op(ek, fn, waits=[(dW[j % 2], ld)], inc=sm)
            stg_rd[j % 2] = (sm, v)
            if ek == "dve":
                wc_v = v
            else:
                wc_a = v
            if j == 7:
                ms["wcast_q"] = (wc_v, wc_a)
            elif j == 15:
                ms["wcast_kv"] = (wc_v, wc_a)
        ms["wcast_v"] = wc_v
        ms["wcast_a"] = wc_a

        if mode == "S":
            v = op("dve", lambda e: e.tensor_copy(out=outbuf, in_=wqkv_sb[:, 0:D]),
                   waits=[(sV, wc_v), (sA, wc_a), (dX, dX.n), (dY, ms["bias_ld"]),
                          (dC, ms["consts_ld"])])
            o = dma("pool", y_d[0:T, :], outbuf, dO, waits=[(sV, v)])
            op("syncq", None, waits=[(dO, dO.n)])
            _emit_all(nc, OPS, _ES, cut)
            return nc

        prep_list = ([("a1", i) for i in range(64)] +
                     [("a2", i) for i in range(64)] +
                     [("f1",), ("f2",)] +
                     [("q1", i) for i in range(64)] +
                     [("q2", i) for i in range(64)])
        if mode == "A":
            prep_list = []
        prep_by_tile = [[] for _ in range(nt)]
        for j, task in enumerate(prep_list):
            prep_by_tile[min(nt - 1, j * nt // len(prep_list))].append(task)

        pst = {
            "stg_rd": {0: None, 1: None, 2: None, 3: None},
            "stg_ld": {},
            "acc": {"a1": 0, "a2": 0},
            "sgn_rd": None,
            "wq_rd": {0: None, 1: None},
            "w2st_rd": {0: None, 1: None},
            "ndma": 0,
            "acc_rd": None,
            "cursor": 0,
            "pend": [],
        }

        def prep_src(kind, i):
            if kind in ("a1", "q1"):
                return w1_d[(i // 2) * 128:(i // 2 + 1) * 128,
                            (i % 2) * 512:(i % 2) * 512 + 512]
            return w2_d[(i // 8) * 128:(i // 8 + 1) * 128,
                        (i % 8) * 512:(i % 8) * 512 + 512]

        def prep_issue_next_dma():
            while pst["cursor"] < len(prep_list):
                task = prep_list[pst["cursor"]]
                pst["cursor"] += 1
                if task[0] in ("f1", "f2"):
                    continue
                slot = pst["ndma"] % 4
                pst["ndma"] += 1
                ld = dma("syncq", wstage[slot], prep_src(task[0], task[1]),
                         dP[slot], waits=[pst["stg_rd"][slot]])
                pst["stg_ld"][task] = (slot, ld)
                return

        def prep_fin(grp):
            kind = "a1" if grp == 0 else "a2"
            acc_slot = grp * 2 + pst["acc"][kind] % 2
            nelem = float(HF * D)
            pv = op("pe", lambda e, a=accw[:, acc_slot:acc_slot + 1]:
                    e.matmul(out=psF[0:1, 0:1], lhsT=a, rhs=ones_col,
                             start=True, stop=True),
                    waits=[(sV, ms["abs_%s_last" % kind]), psum_free["F"],
                           (sV, ms["memset"])], inc=sP)
            op("dve", lambda e: e.tensor_copy(out=wsc[0:1, 0:1],
                                              in_=psF[0:1, 0:1]),
               waits=[(sP, pv)])
            op("dve", lambda e, n=nelem: e.tensor_scalar(
                out=wsc[0:1, 1:2], in0=wsc[0:1, 0:1],
                scalar1=1.0 / n, scalar2=1e-5, op0=ALU.mult, op1=ALU.max))
            vr = op("dve", lambda e: e.reciprocal(out=wsc[0:1, 2:3],
                                                  in_=wsc[0:1, 1:2]), inc=sV)
            pv = op("pe", lambda e: e.matmul(out=psF[:, 0:1], lhsT=ones_row,
                                             rhs=wsc[0:1, 2:3],
                                             start=True, stop=True),
                    waits=[(sV, vr)], inc=sP)
            pv = op("pe", lambda e: e.matmul(out=psF[:, 1:2], lhsT=ones_row,
                                             rhs=wsc[0:1, 1:2],
                                             start=True, stop=True), inc=sP)
            dst_r = bc_r1 if grp == 0 else bc_r2
            dst_w = bc_ws1 if grp == 0 else bc_ws2
            op("dve", lambda e, d=dst_r: e.tensor_copy(out=d, in_=psF[:, 0:1]),
               waits=[(sP, pv)])
            v = op("dve", lambda e, d=dst_w: e.tensor_scalar(
                out=d, in0=psF[:, 1:2], scalar1=1.0 / 127.0, scalar2=None,
                op0=ALU.mult), inc=sV)
            psum_free["F"] = (sV, v)
            ms["wsc_%d" % grp] = v

        def prep_emit(task):
            kind = task[0]
            if kind == "f1":
                prep_fin(0); return
            if kind == "f2":
                prep_fin(1); return
            i = task[1]
            slot, ld = pst["stg_ld"][task]
            if kind in ("a1", "a2"):
                cnt = pst["acc"][kind]
                g = 0 if kind == "a1" else 1
                src_s = g * 2 + cnt % 2
                dst_s = g * 2 + (cnt + 1) % 2
                op("dve", lambda e, s=wstage[slot]: e.tensor_reduce(
                    out=accw[:, 4:5], in_=s, axis=AX.X, op=ALU.add,
                    apply_absolute_value=True),
                    waits=[(dP[slot], ld), (sV, ms["memset"]),
                           pst["acc_rd"]], nosync=True)
                v = op("dve", lambda e, a=accw[:, src_s:src_s + 1],
                       b=accw[:, dst_s:dst_s + 1]:
                       e.tensor_tensor(out=b, in0=a, in1=accw[:, 4:5],
                                       op=ALU.add), inc=sV)
                pst["acc_rd"] = (sV, v)
                pst["acc"][kind] = cnt + 1
                pst["stg_rd"][slot] = (sV, v)
                ms["abs_%s_last" % kind] = v
            else:
                g = 0 if kind == "q1" else 1
                wqs = i % 2
                if any(p[2] == wqs for p in pst["pend"]):
                    prep_flushT()
                bc = bc_r1 if g == 0 else bc_r2
                op("dve", lambda e, s=wstage[slot], b=bc: e.tensor_scalar(
                    out=sgnA, in0=s, scalar1=b, scalar2=0.5,
                    op0=ALU.mult, op1=ALU.is_ge),
                    waits=[(dP[slot], ld), (sV, ms["wsc_%d" % g]),
                           pst["sgn_rd"]], nosync=True)
                va = op("dve", lambda e, s=wstage[slot], b=bc: e.tensor_scalar(
                    out=sgnB, in0=s, scalar1=b, scalar2=-0.5,
                    op0=ALU.mult, op1=ALU.is_ge), inc=sV, nosync=True)
                pst["stg_rd"][slot] = (sV, va)
                v = op("dve", lambda e, o=wqch[wqs]: e.scalar_tensor_tensor(
                    out=o, in0=sgnA, scalar=-1.0, in1=sgnB,
                    op0=ALU.add, op1=ALU.add),
                    waits=[pst["wq_rd"][wqs]], inc=sV)
                pst["sgn_rd"] = (sV, v)
                pst["pend"].append((kind, i, wqs, v))
            prep_issue_next_dma()

        def prep_flushT():
            for kind, i, wqs, vq in pst["pend"]:
                sp, key = next_stage()
                pv = None
                for j in range(4):
                    pv = op("pe", lambda e, jj=j, s=wqch[wqs], p=sp: e.transpose(
                        out=p[:, jj * 128:(jj + 1) * 128],
                        in_=s[:, jj * 128:(jj + 1) * 128], identity=identb),
                        waits=([(sV, vq), psum_free[key],
                                (dC, ms["consts_ld"])]
                               if j == 0 else []),
                        inc=(sP if j == 3 else None))
                pst["wq_rd"][wqs] = (sP, pv)
                if kind == "q1":
                    hb, half = i // 2, i % 2
                    dst = W1qT.rearrange("p (c h) -> p c h", c=DC)[
                        :, half * 4:half * 4 + 4, hb * 128:(hb + 1) * 128]
                    v = op("dve", lambda e, d=dst, p=sp: e.tensor_copy(
                        out=d,
                        in_=p[:, 0:512].rearrange("p (c h) -> p c h", c=4)),
                        waits=[(sP, pv), (sV, ms["wcast_v"]),
                               (sA, ms["wcast_a"])], inc=sV, nosync=True)
                    psum_free[key] = (sV, v)
                else:
                    db, s8 = i // 8, i % 8
                    st = i % 2
                    v = op("dve", lambda e, o=w2st[st], p=sp: e.tensor_copy(
                        out=o, in_=p[:, 0:512]),
                        waits=[(sP, pv), pst["w2st_rd"][st]], inc=sV,
                        nosync=True)
                    psum_free[key] = (sV, v)
                    dst = w2t_d.rearrange("p (c d) -> p c d", c=32)[
                        :, s8 * 4:s8 * 4 + 4, db * 128:(db + 1) * 128]
                    b = dma("syncq", dst,
                            w2st[st].rearrange("p (c d) -> p c d", c=4),
                            dB[st], waits=[(sV, v)])
                    pst["w2st_rd"][st] = (dB[st], b)
            pst["pend"] = []

        for _ in range(4):
            prep_issue_next_dma()

        spill_ms = {}
        (CEL_SUM, CEL_SSQ, CEL_M, CEL_MSQ, CEL_VAR, CEL_LNV, CEL_RSTD,
         CEL_NMR) = range(8)
        CEL_NMX, CEL_SUMS, CEL_RCP = 8, 24, 40
        CEL2 = 56

        def a_ln1_stats(t):
            par = t % 2
            cl = cells[t % 2]
            va = op("act", lambda e, x=xin[par], c=cl: e.activation(
                out=outbuf, in_=x, func=AF.Square,
                accum_out=c[:, CEL_SSQ:CEL_SSQ + 1]),
                waits=[(dX, xin_ld[t])], inc=sA)
            op("dve", lambda e, x=xin[par], c=cl: e.tensor_reduce(
                out=c[:, CEL_SUM:CEL_SUM + 1], in_=x, axis=AX.X, op=ALU.add),
                waits=[(dX, xin_ld[t])])
            op("dve", lambda e, c=cl: e.tensor_scalar(
                out=c[:, CEL_M:CEL_M + 1], in0=c[:, CEL_SUM:CEL_SUM + 1],
                scalar1=1.0 / D, scalar2=None, op0=ALU.mult))
            op("dve", lambda e, c=cl: e.tensor_tensor(
                out=c[:, CEL_MSQ:CEL_MSQ + 1], in0=c[:, CEL_M:CEL_M + 1],
                in1=c[:, CEL_M:CEL_M + 1], op=ALU.mult))
            v = op("dve", lambda e, c=cl: e.scalar_tensor_tensor(
                out=c[:, CEL_VAR:CEL_VAR + 1], in0=c[:, CEL_SSQ:CEL_SSQ + 1],
                scalar=1.0 / D, in1=c[:, CEL_MSQ:CEL_MSQ + 1],
                op0=ALU.mult, op1=ALU.subtract), waits=[(sA, va)], inc=sV)
            op("act", lambda e, c=cl: e.activation(
                out=c[:, CEL_LNV:CEL_LNV + 1], in_=c[:, CEL_VAR:CEL_VAR + 1],
                func=AF.Ln, bias=c_eps),
                waits=[(sV, v), (sV, ms["memset"])])
            va = op("act", lambda e, c=cl: e.activation(
                out=c[:, CEL_RSTD:CEL_RSTD + 1], in_=c[:, CEL_LNV:CEL_LNV + 1],
                func=AF.Exp, scale=-0.5), inc=sA)
            ms[("rstd", t)] = va

        def a_ln1_apply(t):
            par = t % 2
            cl = cells[t % 2]
            v = op("dve", lambda e, c=cl: e.tensor_scalar(
                out=c[:, CEL_NMR:CEL_NMR + 1], in0=c[:, CEL_M:CEL_M + 1],
                scalar1=c[:, CEL_RSTD:CEL_RSTD + 1], scalar2=-1.0,
                op0=ALU.mult, op1=ALU.mult),
                waits=[(sA, ms[("rstd", t)])], inc=sV)
            va = op("act", lambda e, x=xin[par], c=cl: e.activation(
                out=xln, in_=x, func=AF.Identity,
                bias=c[:, CEL_NMR:CEL_NMR + 1],
                scale=c[:, CEL_RSTD:CEL_RSTD + 1]),
                waits=[(sV, v), (sA, ms.get(("xlnT", t - 1), 0))], inc=sA)
            ms[("xln", t)] = va

        def a_xlnT(t):
            sp, key = next_stage()
            pv = None
            for c in range(DC):
                pv = op("pe", lambda e, cc=c, p=sp: e.transpose(
                    out=p[:, cc * 128:(cc + 1) * 128],
                    in_=xln[:, cc * 128:(cc + 1) * 128], identity=identb),
                    waits=([(sA, ms[("xln", t)]), psum_free[key],
                            (dC, ms["consts_ld"])] if c == 0 else []),
                    inc=(sP if c == DC - 1 else None))
            v = op("act", lambda e, p=sp: e.copy(out=xlnT, in_=p),
                   waits=[(sP, pv), (sP, ms.get(("qkv", t - 1, 2), 0))],
                   inc=sA, nosync=True)
            psum_free[key] = (sA, v)
            ms[("xlnT", t)] = v

        def a_qkv(t, grp):
            par = t % 2
            psX, pfk = ((psA, "A"), (psB, "B"), (psC, "C"))[grp]
            wq = ms["wcast_q"] if grp == 0 else ms["wcast_kv"]
            w0 = [(sA, ms[("xlnT", t)]), psum_free[pfk],
                  (sV, wq[0]), (sA, wq[1])]
            pv = None
            for c in range(DC):
                for n in range(2):
                    pv = op("pe", lambda e, cc=c, nn=n, g=grp, p=psX:
                            e.matmul(
                        out=p[:, nn * 512:(nn + 1) * 512],
                        lhsT=xlnT[:, cc * 128:(cc + 1) * 128],
                        rhs=wqkv_sb[:, cc * 3 * D + g * D + nn * 512:
                                    cc * 3 * D + g * D + (nn + 1) * 512],
                        start=(cc == 0), stop=(cc == DC - 1)),
                        waits=(w0 if (c == 0 and n == 0) else []),
                        inc=(sP if (c == DC - 1 and n == 1) else None))
            ms[("qkv", t, grp)] = pv
            if grp == 0:
                va = op("act", lambda e: e.activation(out=q_sb, in_=psA,
                                                      func=AF.Copy,
                                                      scale=0.125),
                        waits=[(sP, pv), (sA, ms.get(("qT", t - 1), 0))],
                        inc=sA, nosync=True)
                ms[("q_sb", t)] = va
                psum_free["A"] = (sA, va)
            elif grp == 1:
                v = op("dve", lambda e: e.tensor_copy(out=k_sb, in_=psB),
                       waits=[(sP, pv), (sA, ms.get(("kT", t - 1), 0))],
                       inc=sV, nosync=True)
                ms[("k_sb", t)] = v
                psum_free["B"] = (sV, v)
            else:
                v = op("dve", lambda e, s=par: e.tensor_copy(
                    out=v_hist[:, s * D:(s + 1) * D], in_=psC),
                    waits=[(sP, pv), (sP, ms.get(("av", t - 1, 3), 0))],
                    inc=sV, nosync=True)
                ms[("v_sb", t)] = v
                psum_free["C"] = (sV, v)

        def a_qkT(t):
            par = t % 2
            for src_name, buf, wkey in (("q", q_sb, ("q_sb", t)),
                                        ("k", k_sb, ("k_sb", t))):
                wsem = sA if src_name == "q" else sV
                for r in range(2):
                    sp, key = next_stage()
                    pv = None
                    for hh in range(8):
                        h = r * 8 + hh
                        pv = op("pe", lambda e, b=buf, h_=h, hh_=hh, p=sp:
                                e.transpose(
                            out=p[0:64, hh_ * 128:(hh_ + 1) * 128],
                            in_=b[:, h_ * 64:(h_ + 1) * 64], identity=identb),
                            waits=([(wsem, ms[wkey]), psum_free[key]]
                                   if hh == 0 else []),
                            inc=(sP if hh == 7 else None))
                    if src_name == "q":
                        dst = qT[0:64, r * 1024:(r + 1) * 1024]
                    else:
                        dst = kT_hist[0:64, par * 2048 + r * 1024:
                                      par * 2048 + (r + 1) * 1024]
                    v = op("act", lambda e, d=dst, p=sp: e.copy(
                        out=d, in_=p[0:64, 0:1024]),
                        waits=[(sP, pv), (sP, ms.get(("scores", t - 1, 3), 0))],
                        inc=sA, nosync=True)
                    psum_free[key] = (sA, v)
                ms[("qT", t) if src_name == "q" else ("kT", t)] = v

        def a_scores(t, g):
            par, prv = t % 2, 1 - (t % 2)
            tvar = 1 if (t % TPS == 0) else 0
            psX = psA if g % 2 == 0 else psB
            pfk = "A" if g % 2 == 0 else "B"
            w0 = [(sA, ms[("qT", t)]), (sA, ms[("kT", t)]),
                  psum_free[pfk], (dC, ms["consts_ld"])]
            pv = None
            first = True
            for half in range(2):
                wh = w0
                for hh in (half * 2, half * 2 + 1):
                    h = 4 * g + hh
                    for kt, slot in ((0, prv), (1, par)):
                        pv = op("pe", lambda e, p=psX, hh_=hh, kt_=kt,
                                h_=h, sl=slot: e.matmul(
                            out=p[:, hh_ * 256 + kt_ * 128:
                                  hh_ * 256 + (kt_ + 1) * 128],
                            lhsT=qT[0:64, h_ * 128:(h_ + 1) * 128],
                            rhs=kT_hist[0:64, sl * 2048 + h_ * 128:
                                        sl * 2048 + (h_ + 1) * 128],
                            start=(hh_ % 2 == 0 and kt_ == 0), stop=False),
                            waits=(wh if first else []), inc=None)
                        first = False
                for hh in (half * 2, half * 2 + 1):
                    last = (hh == half * 2 + 1)
                    pv = op("pe", lambda e, p=psX, hh_=hh, mv=tvar, l=last:
                            e.matmul(
                                out=p[:, hh_ * 256:(hh_ + 1) * 256],
                                lhsT=identb,
                                rhs=masks[:, mv * 256:(mv + 1) * 256],
                                start=False, stop=l),
                            inc=(sP if last else None))
                if half == 0:
                    ms[("sc_h0", t, g)] = pv
            ms[("scores", t, g)] = pv

        def a_sm_max(t, g):
            cl = cells[t % 2]
            psX = psA if g % 2 == 0 else psB
            v0 = op("dve", lambda e, p=psX, c=cl, g_=g: e.tensor_reduce(
                out=c[:, CEL_NMX + 4 * g_:CEL_NMX + 4 * g_ + 2],
                in_=p[:, 0:512].rearrange("p (h k) -> p h k", h=2),
                axis=AX.X, op=ALU.max, negate=True),
                waits=[(sP, ms[("sc_h0", t, g)])], inc=sV)
            v = op("dve", lambda e, p=psX, c=cl, g_=g: e.tensor_reduce(
                out=c[:, CEL_NMX + 4 * g_ + 2:CEL_NMX + 4 * g_ + 4],
                in_=p[:, 512:1024].rearrange("p (h k) -> p h k", h=2),
                axis=AX.X, op=ALU.max, negate=True),
                waits=[(sP, ms[("scores", t, g)])], inc=sV, nosync=True)
            ms[("nmx0", t, g)] = v0
            ms[("nmx", t, g)] = v

        def a_sm_exp(t, g):
            cl = cells[t % 2]
            psX = psA if g % 2 == 0 else psB
            pfk = "A" if g % 2 == 0 else "B"
            va = None
            evl = []
            for hh in range(4):
                wv = [(sV, ms[("nmx0", t, g)] if hh < 2
                       else ms[("nmx", t, g)])]
                if g >= 2:
                    wv.append((sP, ms.get(("pT_mm", t, g - 2), 0)))
                else:
                    wv.append((sP, ms.get(("pT_mm", t - 1, g + 2), 0)))
                va = op("act", lambda e, p=psX, c=cl, g_=g, hh_=hh:
                        e.activation(
                    out=probs[g_ % 2][:, hh_ * 256:(hh_ + 1) * 256],
                    in_=p[:, hh_ * 256:(hh_ + 1) * 256], func=AF.Exp,
                    bias=c[:, CEL_NMX + 4 * g_ + hh_:
                           CEL_NMX + 4 * g_ + hh_ + 1],
                    accum_out=c[:, CEL_SUMS + 4 * g_ + hh_:
                                CEL_SUMS + 4 * g_ + hh_ + 1]),
                    waits=(wv if hh in (0, 2) else []),
                    inc=sA, nosync=(hh > 0))
                evl.append(va)
            ms[("exp", t, g)] = va
            ms[("expv", t, g)] = evl
            psum_free[pfk] = (sA, va)

        def a_sm_norm(t, g):
            cl = cells[t % 2]
            ev = ms[("expv", t, g)]
            r0 = op("dve", lambda e, c=cl, g_=g: e.reciprocal(
                out=c[:, CEL_RCP + 4 * g_:CEL_RCP + 4 * g_ + 2],
                in_=c[:, CEL_SUMS + 4 * g_:CEL_SUMS + 4 * g_ + 2]),
                waits=[(sA, ev[1])], inc=sV)
            rc = r0
            v2 = None
            for hh in range(4):
                if hh == 2:
                    rc = op("dve", lambda e, c=cl, g_=g: e.reciprocal(
                        out=c[:, CEL_RCP + 4 * g_ + 2:CEL_RCP + 4 * g_ + 4],
                        in_=c[:, CEL_SUMS + 4 * g_ + 2:
                               CEL_SUMS + 4 * g_ + 4]),
                        waits=[(sA, ev[3])], inc=sV)
                v2 = op("dve", lambda e, c=cl, g_=g, hh_=hh: e.tensor_scalar(
                    out=probsn[g_ % 2][:, hh_ * 256:(hh_ + 1) * 256],
                    in0=probs[g_ % 2][:, hh_ * 256:(hh_ + 1) * 256],
                    scalar1=c[:, CEL_RCP + 4 * g_ + hh_:
                              CEL_RCP + 4 * g_ + hh_ + 1],
                    scalar2=None, op0=ALU.mult),
                    waits=[(sV, rc)], inc=sV, nosync=(hh in (1, 3)))
                if hh == 1:
                    ms[("pn0", t, g)] = v2
            ms[("pn", t, g)] = v2

        def a_pt(t, g):
            sp, key = next_stage()
            pv = None
            for j in range(8):
                if j == 0:
                    w = [(sV, ms[("pn0", t, g)]), psum_free[key]]
                elif j == 4:
                    w = [(sV, ms[("pn", t, g)])]
                else:
                    w = []
                pv = op("pe", lambda e, jj=j, g_=g, p=sp: e.transpose(
                    out=p[:, jj * 128:(jj + 1) * 128],
                    in_=probsn[g_ % 2][:, jj * 128:(jj + 1) * 128],
                    identity=identb),
                    waits=w, inc=(sP if j == 7 else None))
            ms[("pT_mm", t, g)] = pv
            wv = [(sP, pv)]
            if g >= 1:
                wv.append((sP, ms.get(("av", t, g - 1), 0)))
            else:
                wv.append((sP, ms.get(("av", t - 1, 3), 0)))
            v = op("act", lambda e, g_=g, p=sp: e.activation(
                out=probsT[g_ % 2], in_=p, func=AF.Copy), waits=wv, inc=sA,
                nosync=True)
            psum_free[key] = (sA, v)
            ms[("pT", t, g)] = v

        def a_av(t, g):
            par, prv = t % 2, 1 - (t % 2)
            pv = None
            for hh in range(4):
                h = 4 * g + hh
                bp = 64 * (h % 2)
                p = h // 2
                for kt, slot in ((0, prv), (1, par)):
                    wv = [(sA, ms[("pT", t, g)]), (sV, ms[("v_sb", t)])]
                    if g == 0 and hh == 0 and kt == 0:
                        wv.append(psum_free["C"])
                    pv = op("pe", lambda e, hh_=hh, h_=h, bp_=bp, pp=p,
                            kt_=kt, sl=slot, g_=g: e.matmul(
                        out=psC[bp_:bp_ + 64, pp * 128:(pp + 1) * 128],
                        lhsT=v_hist[:, sl * D + h_ * 64:
                                    sl * D + h_ * 64 + 64],
                        rhs=probsT[g_ % 2][:, (hh_ * 2 + kt_) * 128:
                                           (hh_ * 2 + kt_ + 1) * 128],
                        start=(kt_ == 0), stop=(kt_ == 1),
                        tile_position=(0, bp_)),
                        waits=(wv if kt == 0 else []),
                        inc=(sP if (hh == 3 and kt == 1) else None))
            ms[("av", t, g)] = pv

        def a_outproj(t):
            v = op("dve", lambda e: e.tensor_copy(out=attnout, in_=psC),
                   waits=[(sP, ms[("av", t, 3)]),
                          (sP, ms.get(("outproj", t - 1), 0))], inc=sV,
                   nosync=True)
            ms[("attnout", t)] = v
            psum_free["C"] = (sV, v)
            pv = None
            for n in range(2):
                for c in range(DC):
                    wv = [(sV, ms[("attnout", t)]), psum_free["C"],
                          (sV, ms["wcast_v"]), (sA, ms["wcast_a"])]
                    pv = op("pe", lambda e, cc=c, nn=n: e.matmul(
                        out=psC[:, nn * 512:(nn + 1) * 512],
                        lhsT=attnout[:, cc * 128:(cc + 1) * 128],
                        rhs=wout_sb[:, cc * D + nn * 512:cc * D + (nn + 1) * 512],
                        start=(cc == 0), stop=(cc == DC - 1)),
                        waits=(wv if (n == 0 and c == 0) else []),
                        inc=(sP if (n == 1 and c == DC - 1) else None))
            ms[("outproj", t)] = pv

        def a_res(t):
            par = t % 2
            v = op("dve", lambda e, x=xin[par], o=x1buf[par]: e.tensor_tensor(
                out=o, in0=psC, in1=x, op=ALU.add),
                waits=[(sP, ms[("outproj", t)]), (dS, spill_ms.get(t - 2, 0))],
                inc=sV)
            ms[("res", t)] = v
            psum_free["C"] = (sV, v)
            spill_ms[t] = dma("syncq", x1_d[t * T:(t + 1) * T, :], x1buf[par],
                              dS, waits=[(sV, v)])

        def a_ln2_acc(t):
            par = t % 2
            cl = cells[t % 2]
            va = op("act", lambda e, o=x1buf[par], c=cl: e.activation(
                out=outbuf, in_=o, func=AF.Square,
                accum_out=c[:, CEL2:CEL2 + 1]),
                waits=[(sV, ms[("res", t)])], inc=sA)
            va = op("act", lambda e, o=x1buf[par], c=cl: e.activation(
                out=outbuf, in_=o, func=AF.Identity,
                accum_out=c[:, CEL2 + 1:CEL2 + 2]), inc=sA)
            ms[("ln2a", t)] = va

        def a_ln2_fin(t):
            cl = cells[t % 2]
            op("dve", lambda e, c=cl, t_=t: e.tensor_scalar(
                out=statsN[:, t_:t_ + 1], in0=c[:, CEL2 + 1:CEL2 + 2],
                scalar1=-1.0 / D, scalar2=None, op0=ALU.mult),
                waits=[(sA, ms[("ln2a", t)])], nosync=True)
            op("dve", lambda e, c=cl, t_=t: e.tensor_tensor(
                out=c[:, CEL2 + 2:CEL2 + 3], in0=statsN[:, t_:t_ + 1],
                in1=statsN[:, t_:t_ + 1], op=ALU.mult))
            v = op("dve", lambda e, c=cl: e.scalar_tensor_tensor(
                out=c[:, CEL2 + 3:CEL2 + 4], in0=c[:, CEL2:CEL2 + 1],
                scalar=1.0 / D, in1=c[:, CEL2 + 2:CEL2 + 3],
                op0=ALU.mult, op1=ALU.subtract), inc=sV)
            op("act", lambda e, c=cl: e.activation(
                out=c[:, CEL2 + 4:CEL2 + 5], in_=c[:, CEL2 + 3:CEL2 + 4],
                func=AF.Ln, bias=c_eps), waits=[(sV, v)])
            va = op("act", lambda e, c=cl, t_=t: e.activation(
                out=statsR[:, t_:t_ + 1], in_=c[:, CEL2 + 4:CEL2 + 5],
                func=AF.Exp, scale=-0.5), inc=sA)
            ms[("ln2", t)] = va

        gtile = 0
        a_ln1_stats(0)
        a_ln1_apply(0)
        a_xlnT(0)
        a_qkv(0, 0)
        a_qkv(0, 1)
        a_qkv(0, 2)
        for t in range(nt):
            fence(gtile)
            gtile += 1
            ptasks = list(prep_by_tile[t])

            def prep_drip(k):
                for _ in range(k):
                    if ptasks:
                        prep_emit(ptasks.pop(0))

            if t + 1 < nt:
                a_ln1_stats(t + 1)
            if t == 0:
                a_qkT(0)
            a_scores(t, 0)
            a_sm_max(t, 0)
            a_sm_exp(t, 0)
            if t > 0:
                a_ln2_fin(t - 1)
            a_scores(t, 1)
            a_sm_max(t, 1)
            a_sm_exp(t, 1)
            a_sm_norm(t, 0)
            a_scores(t, 2)
            a_sm_max(t, 2)
            prep_drip(2)
            a_pt(t, 0)
            a_av(t, 0)
            prep_flushT()
            a_sm_exp(t, 2)
            a_sm_norm(t, 1)
            if t + 1 < nt:
                a_ln1_apply(t + 1)
            a_scores(t, 3)
            a_sm_max(t, 3)
            prep_drip(2)
            a_pt(t, 1)
            a_av(t, 1)
            prep_flushT()
            a_sm_exp(t, 3)
            a_sm_norm(t, 2)
            if t + 1 < nt:
                a_xlnT(t + 1)
            prep_drip(2)
            a_pt(t, 2)
            a_av(t, 2)
            prep_flushT()
            if t + 1 < nt:
                a_qkv(t + 1, 0)
            a_sm_norm(t, 3)
            a_pt(t, 3)
            a_av(t, 3)
            prep_drip(2)
            if t + 1 < nt:
                a_qkv(t + 1, 1)
                a_qkT(t + 1)
            a_outproj(t)
            a_res(t)
            if t + 1 < nt:
                a_qkv(t + 1, 2)
            a_ln2_acc(t)
            if t + 2 < nt:
                xin_ld[t + 2] = dma(
                    "syncq", xin[t % 2], xc[(t + 2) * T:(t + 3) * T, :],
                    dX, waits=[(sV, ms[("res", t)]), (sA, ms[("xln", t)])])
            prep_drip(len(ptasks))
            prep_flushT()
            tile_end()
        a_ln2_fin(nt - 1)

        a_end = {ek: ENG_SEM[ek].n for ek in ENG_SEM}
        for ek in ("pe", "dve", "act", "pool", "syncq"):
            waits = [(ENG_SEM[o], a_end[o]) for o in ENG_SEM if o != ek]
            for s in (dS, dB[0], dB[1], dP[0], dP[1], dW[0], dW[1], dC, dY, dX):
                waits.append((s, s.n))
            op(ek, None, waits=waits)

        w2qt_lds = []
        for j in range(4):
            w2qt_lds.append(dma(
                "act", W2qT[:, j * 8192:(j + 1) * 8192],
                w2t_d[:, j * 8192:(j + 1) * 8192], dW[j % 2],
                waits=([(dB[0], dB[0].n), (dB[1], dB[1].n)] if j < 2 else [])))
        ms["w2qt0"] = w2qt_lds[2]
        ms["w2qt1"] = w2qt_lds[3]

        x1_ld = {}
        out_ms = {}
        FQ = 0
        FH = 12
        cbb = tib.bitcast(F32)

        def x1s(t):
            return (x1buf[0], x1buf[1], x1c)[t % 3]

        def emit_round(src_ap, dst_ap, scale_cell, bias_cell, bias_imm, nsub,
                       first_wait, war_wait, r0=0):
            subs = []
            for r in range(r0, r0 + nsub):
                wv = (list(first_wait) + list(war_wait)) if r == r0 else []
                sc2 = bias_cell if bias_cell is not None else bias_imm
                op("dve", lambda e, rr=r, s=src_ap, sc=scale_cell,
                   b=sc2: e.tensor_scalar(
                    out=t1b, in0=s[:, rr * 1024:(rr + 1) * 1024],
                    scalar1=sc, scalar2=b, op0=ALU.mult, op1=ALU.add),
                    waits=wv)
                op("dve", lambda e: e.tensor_copy(out=tib, in_=t1b))
                op("dve", lambda e: e.tensor_copy(out=tfb, in_=tib))
                op("dve", lambda e: e.tensor_tensor(
                    out=cbb, in0=t1b, in1=tfb, op=ALU.is_ge))
                rl = op("dve", lambda e, rr=r, d=dst_ap: e.scalar_tensor_tensor(
                    out=d[:, rr * 1024:(rr + 1) * 1024], in0=tfb,
                    scalar=-641.0, in1=cbb, op0=ALU.add, op1=ALU.add), inc=sV)
                subs.append(rl)
            return subs

        for t in range(min(3, nt)):
            x1_ld[t] = dma("syncq", x1s(t), x1_d[t * T:(t + 1) * T, :],
                           dL, waits=[(dS, dS.n)])

        def emit_w1_quarter(t, j):
            cl = cells[t % 2]
            psX = psA if j % 2 == 0 else psB
            pfk = "A" if j % 2 == 0 else "B"
            wv = [(sA, ms[("xqT", t)]), (sA, ms[("i1r", t)]),
                  psum_free[pfk], (dY, ms["bias_ld"])]
            pv = None
            for n in range(2):
                pv = op("pe", lambda e, jj=j, nn=n, p=psX: e.matmul(
                    out=p[:, nn * 512:(nn + 1) * 512], lhsT=invs1row,
                    rhs=b1bf[0:1, jj * 1024 + nn * 512:
                             jj * 1024 + (nn + 1) * 512],
                    start=True, stop=False),
                    waits=(wv if n == 0 else []), inc=None)
                for c in range(DC):
                    pv = op("pe", lambda e, jj=j, nn=n, cc=c, p=psX:
                            e.matmul(
                        out=p[:, nn * 512:(nn + 1) * 512],
                        lhsT=xqT[:, cc * 128:(cc + 1) * 128],
                        rhs=W1qT[:, cc * HF + jj * 1024 + nn * 512:
                                 cc * HF + jj * 1024 + (nn + 1) * 512],
                        start=False, stop=(cc == DC - 1)),
                        inc=(sP if (n == 1 and c == DC - 1) else None))
            ms[("w1", t, j)] = pv

        def _hqs(t, j):
            l = ms.get(("hqsub", t))
            return 0 if l is None else l[j]

        def emit_gelu(t, j):
            cl = cells[t % 2]
            psX = psA if j % 2 == 0 else psB
            pfk = "A" if j % 2 == 0 else "B"
            va = op("act", lambda e, jj=j, p=psX, c=cl: e.activation(
                out=h_sb[:, jj * 1024:(jj + 1) * 1024], in_=p,
                func=AF.Gelu, scale=c[:, FQ + 9:FQ + 10]),
                waits=[(sP, ms[("w1", t, j)]), (sV, _hqs(t - 1, j)),
                       (sV, ms[("fq", t)])], inc=sA, nosync=True)
            ms[("gelu", t, j)] = va
            psum_free[pfk] = (sA, va)
            op("dve", lambda e, jj=j, c=cl: e.tensor_reduce(
                out=c[:, FH + jj:FH + jj + 1],
                in_=h_sb[:, jj * 1024:(jj + 1) * 1024], axis=AX.X,
                op=ALU.max, apply_absolute_value=True),
                waits=[(sA, va)], nosync=True)

        def f_front_dve(t):
            cl = cells[t % 2]
            op("dve", lambda e, x=x1s(t), c=cl: e.tensor_reduce(
                out=c[:, FQ:FQ + 1], in_=x, axis=AX.X, op=ALU.max),
                waits=[(dL, x1_ld[t]), (sA, ms.get(("gelu", t - 2, 3), 0)),
                       (sP, ms.get(("sT1", t - 2), 0))])
            op("dve", lambda e, x=x1s(t), c=cl: e.tensor_reduce(
                out=c[:, FQ + 1:FQ + 2], in_=x, axis=AX.X, op=ALU.min),
                waits=[(dL, x1_ld[t])], nosync=True)
            op("dve", lambda e, c=cl, t_=t: e.tensor_tensor(
                out=c[:, FQ + 2:FQ + 3], in0=c[:, FQ:FQ + 1],
                in1=statsN[:, t_:t_ + 1], op=ALU.add))
            op("dve", lambda e, c=cl, t_=t: e.tensor_tensor(
                out=c[:, FQ + 3:FQ + 4], in0=c[:, FQ + 1:FQ + 2],
                in1=statsN[:, t_:t_ + 1], op=ALU.add))
            op("dve", lambda e, c=cl: e.tensor_scalar(
                out=c[:, FQ + 4:FQ + 5], in0=c[:, FQ + 3:FQ + 4],
                scalar1=-1.0, scalar2=None, op0=ALU.mult))
            op("dve", lambda e, c=cl: e.tensor_tensor(
                out=c[:, FQ + 5:FQ + 6], in0=c[:, FQ + 2:FQ + 3],
                in1=c[:, FQ + 4:FQ + 5], op=ALU.max))
            op("dve", lambda e, c=cl: e.reciprocal(
                out=c[:, FQ + 6:FQ + 7], in_=c[:, FQ + 5:FQ + 6]))
            op("dve", lambda e, c=cl: e.tensor_scalar(
                out=c[:, FQ + 7:FQ + 8], in0=c[:, FQ + 6:FQ + 7],
                scalar1=127.0, scalar2=None, op0=ALU.mult))
            op("dve", lambda e, c=cl, t_=t: e.tensor_tensor(
                out=c[:, FQ + 8:FQ + 9], in0=c[:, FQ + 5:FQ + 6],
                in1=statsR[:, t_:t_ + 1], op=ALU.mult))
            op("dve", lambda e, c=cl: e.tensor_scalar(
                out=c[:, FQ + 9:FQ + 10], in0=c[:, FQ + 8:FQ + 9],
                scalar1=bc_ws1, scalar2=None, op0=ALU.mult))
            op("dve", lambda e, c=cl: e.reciprocal(
                out=c[:, FQ + 10:FQ + 11], in_=c[:, FQ + 9:FQ + 10]))
            v = op("dve", lambda e, c=cl, t_=t: e.tensor_scalar(
                out=c[:, FQ + 11:FQ + 12], in0=statsN[:, t_:t_ + 1],
                scalar1=c[:, FQ + 7:FQ + 8], scalar2=RK,
                op0=ALU.mult, op1=ALU.add), inc=sV)
            ms[("fq", t)] = v

            rl = emit_round(x1s(t), xq,
                            cl[:, FQ + 7:FQ + 8], cl[:, FQ + 11:FQ + 12], None,
                            1, [(sV, ms[("fq", t)])],
                            [(sP, ms.get(("xqT_mm", t - 1), 0))])
            ms[("xq", t)] = rl[-1]

        def f_front_pe(t):
            cl = cells[t % 2]
            pv = op("pe", lambda e, c=cl: e.transpose(
                out=psF[0:1, 0:128], in_=c[:, FQ + 10:FQ + 11],
                identity=identf),
                waits=[(sV, ms[("fq", t)]), psum_free["F"],
                       (dC, ms["consts_ld"]),
                       (sP, ms.get(("w1", t - 1, 3), 0))], inc=sP)
            ms[("sT1", t)] = pv
            va = op("act", lambda e: e.activation(
                out=invs1row, in_=psF[0:1, 0:128], func=AF.Copy),
                waits=[(sP, pv)], inc=sA, nosync=True)
            ms[("i1r", t)] = va
            psum_free["F"] = (sA, va)

            sp, key = next_stage()
            pv = None
            for c in range(DC):
                pv = op("pe", lambda e, cc=c, p=sp: e.transpose(
                    out=p[:, cc * 128:(cc + 1) * 128],
                    in_=xq[:, cc * 128:(cc + 1) * 128], identity=identb),
                    waits=([(sV, ms[("xq", t)]), psum_free[key]]
                           if c == 0 else []),
                    inc=(sP if c == DC - 1 else None))
            ms[("xqT_mm", t)] = pv
            v = op("act", lambda e, p=sp: e.activation(
                out=xqT, in_=p, func=AF.Copy),
                waits=[(sP, pv), (sP, ms.get(("w1", t - 1, 3), 0))], inc=sA,
                nosync=True)
            psum_free[key] = (sA, v)
            ms[("xqT", t)] = v

            emit_w1_quarter(t, 0)
            emit_w1_quarter(t, 1)

        def f_fh(t):
            cl = cells[t % 2]
            op("dve", lambda e, c=cl: e.tensor_reduce(
                out=c[:, FH + 4:FH + 5], in_=c[:, FH:FH + 4], axis=AX.X,
                op=ALU.max))
            op("dve", lambda e, c=cl: e.reciprocal(
                out=c[:, FH + 8:FH + 9], in_=c[:, FH + 4:FH + 5]))
            op("dve", lambda e, c=cl: e.tensor_scalar(
                out=c[:, FH + 5:FH + 6], in0=c[:, FH + 8:FH + 9],
                scalar1=127.0, scalar2=None, op0=ALU.mult))
            op("dve", lambda e, c=cl: e.tensor_scalar(
                out=c[:, FH + 6:FH + 7], in0=c[:, FH + 4:FH + 5],
                scalar1=bc_ws2, scalar2=None, op0=ALU.mult))
            v = op("dve", lambda e, c=cl: e.reciprocal(
                out=c[:, FH + 7:FH + 8], in_=c[:, FH + 6:FH + 7]), inc=sV)
            ms[("fh", t)] = v

        def f_round0(t):
            cl = cells[t % 2]
            rl = emit_round(h_sb, hq, cl[:, FH + 5:FH + 6], None, RK, 1,
                            [(sV, ms[("fh", t)])],
                            [(sP, ms.get(("hqT_mm", t - 1), 0))])
            ms[("hqsub", t)] = rl

        def f_rounds123(t):
            cl = cells[t % 2]
            rl = emit_round(h_sb, hq, cl[:, FH + 5:FH + 6], None, RK, 3,
                            [], [], r0=1)
            ms[("hq", t)] = rl[-1]
            ms[("hqsub", t)] = ms[("hqsub", t)] + rl

        def f_back_T0(t):
            cl = cells[t % 2]
            pv = op("pe", lambda e, c=cl: e.transpose(
                out=psF[0:1, 128:256], in_=c[:, FH + 7:FH + 8],
                identity=identf),
                waits=[(sV, ms[("fh", t)]), psum_free["F"],
                       (sP, ms.get(("w2", t - 1), 0))], inc=sP)
            va = op("act", lambda e: e.activation(
                out=invs2row, in_=psF[0:1, 128:256], func=AF.Copy),
                waits=[(sP, pv)], inc=sA, nosync=True)
            ms[("i2r", t)] = va
            psum_free["F"] = (sA, va)
            _fT(t, (0,))

        def f_out(t):
            cl = cells[t % 2]
            v = op("dve", lambda e, c=cl, x=x1s(t): e.scalar_tensor_tensor(
                out=outbuf, in0=psC, scalar=c[:, FH + 6:FH + 7], in1=x,
                op0=ALU.mult, op1=ALU.add),
                waits=[(sP, ms[("w2", t)]), (dO, out_ms.get(t - 1, 0))],
                inc=sV)
            ms[("fout", t)] = v
            psum_free["C"] = (sV, v)
            out_ms[t] = dma("pool", y_d[t * T:(t + 1) * T, :], outbuf, dO,
                            waits=[(sV, v)])
            if t + 3 < nt:
                x1_ld[t + 3] = dma("syncq", x1s(t + 3),
                                   x1_d[(t + 3) * T:(t + 4) * T, :], dL,
                                   waits=[(sV, v)])

        def _fT(t, rrs):
            wv = [(sA, ms[("i2r", t)]),
                  (sV, ms.get(("fout", t - 1), 0)), (dW[0], ms["w2qt0"]),
                  (dW[1], ms["w2qt1"]), (dY, ms["bias_ld"])]
            pv = None
            evs = None
            for rr in rrs:
                sp, key = next_stage()
                for j in range(8):
                    pv = op("pe", lambda e, rr_=rr, jj=j, p=sp: e.transpose(
                        out=p[:, jj * 128:(jj + 1) * 128],
                        in_=hq[:, rr_ * 1024 + jj * 128:
                               rr_ * 1024 + (jj + 1) * 128],
                        identity=identb),
                        waits=([(sV, ms[("hqsub", t)][rr]), psum_free[key]]
                               if j == 0 else []),
                        inc=(sP if j == 7 else None))
                if rr == 3:
                    ms[("hqT_mm", t)] = pv
                evs = op("act", lambda e, rr_=rr, p=sp: e.activation(
                    out=hqT[:, rr_ * 1024:(rr_ + 1) * 1024], in_=p,
                    func=AF.Copy),
                    waits=[(sP, pv)] + ([(sP, ms.get(("w2", t - 1), 0))]
                                        if rr == 0 else []), inc=sA,
                    nosync=True)
                psum_free[key] = (sA, evs)
                for n in range(2):
                    if rr == 0:
                        op("pe", lambda e, nn=n: e.matmul(
                            out=psC[:, nn * 512:(nn + 1) * 512],
                            lhsT=invs2row,
                            rhs=b2bf[0:1, nn * 512:(nn + 1) * 512],
                            start=True, stop=False),
                            waits=(wv + [(sA, evs)]) if n == 0 else [],
                            inc=None)
                    for k in range(rr * 8, rr * 8 + 8):
                        pv = op("pe", lambda e, nn=n, kk=k: e.matmul(
                            out=psC[:, nn * 512:(nn + 1) * 512],
                            lhsT=hqT[:, kk * 128:(kk + 1) * 128],
                            rhs=W2qT[:, kk * 1024 + nn * 512:
                                     kk * 1024 + (nn + 1) * 512],
                            start=False, stop=(kk == 31)),
                            waits=([(sA, evs)] if (n == 0 and k == rr * 8)
                                   else []),
                            inc=(sP if (n == 1 and k == 31) else None))
            ms[("hqT", t)] = evs
            ms[("w2", t)] = pv

        f_front_dve(0)
        f_front_pe(0)
        if nt > 1:
            f_front_dve(1)
        emit_gelu(0, 0)
        emit_gelu(0, 1)
        emit_w1_quarter(0, 2)
        emit_gelu(0, 2)
        emit_w1_quarter(0, 3)
        emit_gelu(0, 3)
        f_fh(0)
        for t in range(nt):
            fence(gtile)
            gtile += 1
            f_round0(t)
            if t + 1 < nt:
                f_front_pe(t + 1)
            f_back_T0(t)
            f_rounds123(t)
            if t + 2 < nt:
                f_front_dve(t + 2)
            _fT(t, (1,))
            if t + 1 < nt:
                emit_gelu(t + 1, 0)
                emit_gelu(t + 1, 1)
                emit_w1_quarter(t + 1, 2)
            _fT(t, (2,))
            if t + 1 < nt:
                emit_gelu(t + 1, 2)
                emit_w1_quarter(t + 1, 3)
            _fT(t, (3,))
            if t + 1 < nt:
                emit_gelu(t + 1, 3)
                f_fh(t + 1)
            f_out(t)
            tile_end()

        op("syncq", None, waits=[(dO, dO.n)])
        _emit_all(nc, OPS, _ES, cut)

    return nc


def _emit_all(nc, OPS, _ES, cut=None):
    if cut is not None:
        OPS = OPS[:cut]
        tot = {}
        for (k, waits, fn, inc, amt, is_dma, ns) in OPS:
            if inc is not None:
                tot[inc.name] = tot.get(inc.name, 0) + amt
        OPS = [(k, [(s, min(v, tot.get(s.name, 0))) for (s, v) in waits],
                fn, inc, amt, is_dma, ns)
               for (k, waits, fn, inc, amt, is_dma, ns) in OPS]

    def mk(ek):
        zs = _ES.get(ek)

        def run(eng):
            lastw = {}
            zi = 0
            for (k, waits, fn, inc, amt, is_dma, ns) in OPS:
                if k != ek:
                    continue
                for (s, val) in waits:
                    if val > 0 and lastw.get(s.name, 0) < val:
                        eng.wait_ge(s.h, val)
                        lastw[s.name] = val
                if fn is None:
                    continue
                if zs is not None and zi > 0 and not ns:
                    eng.wait_ge(zs.h, zi)
                inst = fn(eng)
                if zs is not None and not is_dma:
                    zi += 1
                if inc is not None:
                    inst.then_inc(inc.h, amt)
        return run

    with nc.Block() as block:
        block.tensor(mk("pe"))
        block.vector(mk("dve"))
        block.scalar(mk("act"))
        block.gpsimd(mk("pool"))
        block.sync(mk("syncq"))


def _consts():
    identb = np.eye(128, dtype=ml_dtypes.bfloat16)
    identf = np.eye(128, dtype=np.float32)
    a = np.arange(128)[:, None]
    b = np.arange(128)[None, :]
    m_prev = np.where(b > a, 0.0, NEG).astype(np.float32)
    m_cur = np.where(b <= a, 0.0, NEG).astype(np.float32)
    normal = np.concatenate([m_prev, m_cur], axis=1)
    t0 = np.concatenate([np.full((128, 128), NEG, np.float32), m_cur], axis=1)
    masks = np.concatenate([normal, t0], axis=1).astype(ml_dtypes.bfloat16)
    return identb, identf, masks


_NC_CACHE = {}


def kernel(L0_attn_x, L0_attn_w_qkv, L0_attn_w_out, L0_ffn_W1, L0_ffn_b1,
           L0_ffn_W2, L0_ffn_b2):
    x = np.ascontiguousarray(np.asarray(L0_attn_x, dtype=np.float32))
    wqkv = np.ascontiguousarray(np.asarray(L0_attn_w_qkv, dtype=np.float32))
    wout = np.ascontiguousarray(np.asarray(L0_attn_w_out, dtype=np.float32))
    w1 = np.ascontiguousarray(np.asarray(L0_ffn_W1, dtype=np.float32))
    w2 = np.ascontiguousarray(np.asarray(L0_ffn_W2, dtype=np.float32))
    b1 = np.ascontiguousarray(np.asarray(L0_ffn_b1, dtype=np.float32))
    b2 = np.ascontiguousarray(np.asarray(L0_ffn_b2, dtype=np.float32))
    identb, identf, masks = _consts()

    nt = SEQ_PC * TPS
    if nt not in _NC_CACHE:
        _NC_CACHE[nt] = _build(nt)
    nc = _NC_CACHE[nt]

    in_maps = []
    for c in range(NCORES):
        xs = np.ascontiguousarray(
            x[c * SEQ_PC:(c + 1) * SEQ_PC].reshape(SEQ_PC * S, D))
        in_maps.append({
            "xc": xs, "wqkv": wqkv, "wout": wout, "w1": w1, "w2": w2,
            "b1": b1, "b2": b2, "identb": identb, "identf": identf,
            "masks": masks,
        })
    res = run_bass_kernel_spmd(nc, in_maps, core_ids=list(range(NCORES)))
    out = np.concatenate(
        [r["y"].reshape(SEQ_PC, S, D) for r in res.results], axis=0)
    return out


# version 80
# speedup vs baseline: 1.0706x; 1.0197x over previous
import numpy as np
import ml_dtypes
from contextlib import ExitStack

import concourse.bass as bass
import concourse.mybir as mybir
from concourse.bass_utils import run_bass_kernel_spmd

F32 = mybir.dt.float32
BF16 = mybir.dt.bfloat16
I32 = mybir.dt.int32
AF = mybir.ActivationFunctionType
ALU = mybir.AluOpType
AX = mybir.AxisListType

B, S, D = 16, 2048, 1024
NCORES = 8
SEQ_PC = B // NCORES
T = 128
TPS = S // T
DC = D // 128
HN, DH = 16, 64
HF = 4096
NEG = -1.0e9
RK = 640.5

_SB_BASE = 16640
_SB_TOP = 229376


class _Sem:
    __slots__ = ("name", "n", "h")

    def __init__(self, name):
        self.name = name
        self.n = 0
        self.h = None


def _off(ap):
    return ap.tensor.manual_sbuf_range[0]


def _build(nt, mode="full", cut=None):
    ntok = nt * T
    nc = bass.Bass("TRN2", target_bir_lowering=False, debug=False)

    xc = nc.dram_tensor("xc", [ntok, D], F32, kind="ExternalInput").ap()
    wqkv_d = nc.dram_tensor("wqkv", [D, 3 * D], F32, kind="ExternalInput").ap()
    wout_d = nc.dram_tensor("wout", [D, D], F32, kind="ExternalInput").ap()
    w1_d = nc.dram_tensor("w1", [HF, D], F32, kind="ExternalInput").ap()
    w2_d = nc.dram_tensor("w2", [D, HF], F32, kind="ExternalInput").ap()
    b1_d = nc.dram_tensor("b1", [HF], F32, kind="ExternalInput").ap()
    b2_d = nc.dram_tensor("b2", [D], F32, kind="ExternalInput").ap()
    identb_d = nc.dram_tensor("identb", [128, 128], BF16, kind="ExternalInput").ap()
    identf_d = nc.dram_tensor("identf", [128, 128], F32, kind="ExternalInput").ap()
    masks_d = nc.dram_tensor("masks", [128, 512], BF16, kind="ExternalInput").ap()
    y_d = nc.dram_tensor("y", [ntok, D], F32, kind="ExternalOutput").ap()
    x1_d = nc.dram_tensor("x1sp", [ntok, D], F32, kind="Internal").ap()
    w2t_d = nc.dram_tensor("w2td", [128, 32 * 1024], BF16, kind="Internal").ap()
    bbf_d = nc.dram_tensor("bbfd", [HF + D], BF16, kind="Internal").ap()

    cur = [_SB_BASE]

    def sb(name, p, n, dt, at=None):
        size = n * mybir.dt.size(dt)
        o = cur[0] if at is None else at
        o = (o + 31) & ~31
        h = nc._state.alloc_sbuf_tensor_at(name, [p, n], dt, offset=o)
        if at is None:
            cur[0] = o + size
        return h.ap()

    identb = sb("identb", 128, 128, BF16)
    identf = sb("identf", 128, 128, F32)
    masks = sb("masks", 128, 512, BF16)
    b1bf = sb("b1bf", 1, HF, BF16)
    b2bf = sb("b2bf", 1, D, BF16)
    cells = [sb("cells0", 128, 64, F32), sb("cells1", 128, 64, F32)]
    statsN = sb("statsN", 128, nt, F32)
    statsR = sb("statsR", 128, nt, F32)
    _c4 = sb("c4", 128, 5, F32)
    c_eps = _c4[:, 0:1]
    c_nh = _c4[:, 1:2]
    c_ph = _c4[:, 2:3]
    ones_col = _c4[:, 3:4]
    c_rk = _c4[:, 4:5]
    ones_row = sb("ones_row", 1, 128, F32)
    _b4 = sb("b4", 128, 4, F32)
    bc_r1 = _b4[:, 0:1]
    bc_ws1 = _b4[:, 1:2]
    bc_r2 = _b4[:, 2:3]
    bc_ws2 = _b4[:, 3:4]
    accw = sb("accw", 128, 8, F32)
    wsc = sb("wsc", 1, 8, F32)
    x1buf = [sb("x1b0", 128, D, F32), sb("x1b1", 128, D, F32)]
    outbuf = sb("outbuf", 128, D, F32)
    invs1row = sb("i1row", 1, 128, BF16)
    invs2row = sb("i2row", 1, 128, BF16)

    A0 = (cur[0] + 63) & ~63
    cur[0] = A0
    wqkv_sb = sb("wqkv_sb", 128, DC * 3 * D, BF16)
    wout_sb = sb("wout_sb", 128, DC * D, BF16)
    xin = [sb("xin0", 128, D, F32), sb("xin1", 128, D, F32)]
    xln = sb("xln", 128, D, BF16)
    xlnT = sb("xlnT", 128, D, BF16)
    q_sb = sb("q_sb", 128, D, BF16)
    k_sb = sb("k_sb", 128, D, BF16)
    qT = sb("qT", 64, 2048, BF16)
    kT_hist = sb("kT_hist", 64, 2 * 2048, BF16)
    v_hist = sb("v_hist", 128, 2 * D, BF16)
    probs = [sb("probs0", 128, 1024, BF16), sb("probs1", 128, 1024, BF16)]
    probsn = probs
    probsT_b = sb("probsT", 128, 1024, BF16)
    probsT = [probsT_b, probsT_b]
    attnout = sb("attnout", 128, 1024, BF16)
    wstage = [sb("wstage0", 128, 512, F32), sb("wstage1", 128, 512, F32),
              sb("wstage2", 128, 512, F32), sb("wstage3", 128, 512, F32)]
    sgnA = sb("sgnA", 128, 512, BF16)
    sgnB = sb("sgnB", 128, 512, BF16)
    wqch = [sb("wqch0", 128, 512, BF16), sb("wqch1", 128, 512, BF16)]
    w2st = [sb("w2st0", 128, 512, BF16), sb("w2st1", 128, 512, BF16)]
    A_END = cur[0]

    w1t_at = (229368 - DC * HF * 2) & ~63
    W1qT = sb("W1qT", 128, DC * HF, BF16, at=w1t_at)
    assert A_END <= w1t_at, (A_END, w1t_at)

    cur[0] = A0
    W2qT = sb("W2qT", 128, 32 * D, BF16)
    h_sb = sb("h_sb", 128, HF, F32)
    hq = sb("hq", 128, HF, BF16)
    hqT = sb("hqT", 128, HF, BF16)
    t1b = sb("t1a", 128, 1024, F32)
    tib = sb("tib", 128, 1024, I32)
    tfb = sb("tfb", 128, 1024, F32)
    x1c = sb("x1c", 128, D, F32)
    xq = sb("xq", 128, D, BF16)
    xqT = sb("xqT", 128, D, BF16)
    F_END = cur[0]
    assert F_END <= w1t_at, (F_END, w1t_at)

    wstg_big = [
        sb("wstgA", 128, 1536, F32, at=w1t_at),
        sb("wstgB", 128, 1536, F32, at=w1t_at + 1536 * 4),
    ]
    bstage = sb("bstage", 128, 32, F32, at=w1t_at + 1536 * 8)
    bstage_bf = sb("bstageb", 128, 32, BF16, at=w1t_at + 1536 * 8 + 128)

    with ExitStack() as ctx:
        psA = ctx.enter_context(nc.psum_tensor("psA", [128, 1024], F32)).ap()
        psB = ctx.enter_context(nc.psum_tensor("psB", [128, 1024], F32)).ap()
        psC = ctx.enter_context(nc.psum_tensor("psC", [128, 1024], F32)).ap()
        psD = ctx.enter_context(nc.psum_tensor("psD", [128, 1024], BF16)).ap()
        psF = ctx.enter_context(nc.psum_tensor("psF", [128, 512], F32)).ap()

        sP = _Sem("sP"); sV = _Sem("sV"); sA = _Sem("sA"); sG = _Sem("sG")
        dX = _Sem("dX"); dS = _Sem("dS"); dL = _Sem("dL"); dO = _Sem("dO")
        dC = _Sem("dC"); dY = _Sem("dY")
        dW = [_Sem("dW0"), _Sem("dW1")]
        dP = [_Sem("dP0"), _Sem("dP1"), _Sem("dP2"), _Sem("dP3")]
        dB = [_Sem("dB0"), _Sem("dB1")]
        SEMS = [sP, sV, sA, sG, dX, dS, dL, dO, dC, dY] + dW + dP + dB
        for s in SEMS:
            s.h = ctx.enter_context(nc.semaphore(s.name))

        OPS = []
        _ES = {"dve": sV, "act": sA, "pool": sG}

        def op(ek, fn, waits=(), inc=None, amt=1, is_dma=False,
               nosync=False):
            if fn is not None and not is_dma and ek in _ES:
                inc = _ES[ek]
                amt = 1
            OPS.append((ek, [w for w in waits if w is not None and w[1] > 0],
                        fn, inc, amt, is_dma, nosync))
            if inc is not None:
                inc.n += amt
                return inc.n
            return None

        def dma(ek, out_ap, in_ap, sem, waits=()):
            w = [(sem, sem.n)] + list(waits)
            return op(ek, (lambda e, o=out_ap, i=in_ap: e.dma_start(out=o, in_=i)),
                      w, sem, 16, is_dma=True)

        ms = {}

        def tickle(n=2):
            for _ in range(n):
                op("pe", lambda e: e.ldweights(identb))

        def tickle_train(sem, target, step=3, n=4, per=3):
            for k in range(n, 0, -1):
                v = target - k * step
                if v > 0:
                    for _ in range(per):
                        op("pe", lambda e: e.ldweights(identb),
                           waits=[(sem, v)])

        psum_free = {"A": None, "B": None, "C": None, "D": None, "F": None}
        psD2 = psF.bitcast(BF16)
        _stg = {"t": 0}

        def next_stage():
            _stg["t"] ^= 1
            return (psD, "D") if _stg["t"] else (psD2, "F")
        eng_end = {k: [0] for k in ("pe", "dve", "act", "pool")}
        ENG_SEM = {"pe": sP, "dve": sV, "act": sA, "pool": sG}

        def fence(gt):
            if gt < 2:
                return
            for ek in ("pe", "dve", "act", "pool"):
                waits = [(ENG_SEM[o], eng_end[o][gt - 1])
                         for o in ("pe", "dve", "act", "pool") if o != ek]
                op(ek, None, waits=waits)

        def tile_end():
            for ek in ("pe", "dve", "act", "pool"):
                eng_end[ek].append(ENG_SEM[ek].n)

        def _memset(ap_, val):
            return lambda e, a=ap_, v=val: e.memset(a, v)

        op("dve", _memset(c_eps, 1e-5))
        op("dve", _memset(c_nh, -0.5))
        op("dve", _memset(c_ph, 0.5))
        op("dve", _memset(c_rk, RK))
        op("dve", _memset(ones_col, 1.0))
        op("dve", _memset(ones_row, 1.0))
        op("dve", _memset(accw, 0.0))
        op("dve", _memset(kT_hist, 0.0))
        op("dve", _memset(v_hist, 0.0), inc=sV)
        ms["memset"] = sV.n

        dma("act", identb, identb_d, dC)
        dma("act", identf, identf_d, dC)
        dma("act", masks, masks_d, dC)
        ms["consts_ld"] = dC.n

        _w = dma("act", bstage, b1_d.rearrange("(p k) -> p k", p=128), dY)
        _a = op("act", lambda e: e.activation(out=bstage_bf, in_=bstage,
                                              func=AF.Copy),
                waits=[(dY, _w)], inc=sA)
        _b = dma("act", bbf_d[0:HF].rearrange("(p k) -> p k", p=128), bstage_bf,
                 dY, waits=[(sA, _a)])
        dma("act", b1bf, bbf_d[0:HF].rearrange("(a k) -> a k", a=1), dY,
            waits=[(dY, _b)])
        _w = dma("act", bstage[:, 0:8], b2_d.rearrange("(p k) -> p k", p=128),
                 dY, waits=[(dY, _b)])
        _a = op("act", lambda e: e.activation(out=bstage_bf[:, 0:8],
                                              in_=bstage[:, 0:8], func=AF.Copy),
                waits=[(dY, _w)], inc=sA)
        _b = dma("act", bbf_d[HF:HF + D].rearrange("(p k) -> p k", p=128),
                 bstage_bf[:, 0:8], dY, waits=[(sA, _a)])
        dma("act", b2bf, bbf_d[HF:HF + D].rearrange("(a k) -> a k", a=1), dY,
            waits=[(dY, _b)])
        ms["bias_ld"] = dY.n

        xin_ld = {}
        for t0 in range(min(2, nt)):
            xin_ld[t0] = dma("syncq", xin[t0 % 2], xc[t0 * T:(t0 + 1) * T, :], dX)

        wc_v, wc_a = 0, 0
        stg_rd = {0: None, 1: None}
        pieces = []
        for i in range(8):
            pieces.append((wqkv_d[i * 128:(i + 1) * 128, 0:1536],
                           wqkv_sb[:, i * 3 * D:i * 3 * D + 1536], 1536))
        for i in range(8):
            pieces.append((wqkv_d[i * 128:(i + 1) * 128, 1536:3072],
                           wqkv_sb[:, i * 3 * D + 1536:
                                   i * 3 * D + 3072], 1536))
        for i in range(8):
            pieces.append((wout_d[i * 128:(i + 1) * 128, :],
                           wout_sb[:, i * D:(i + 1) * D], 1024))
        for j, (src, dst, n) in enumerate(pieces):
            buf = wstg_big[j % 2]
            ld = dma("act", buf[:, 0:n], src, dW[j % 2], waits=[stg_rd[j % 2]])
            ek = "dve" if j % 2 == 0 else "act"
            sm = sV if ek == "dve" else sA
            if ek == "dve":
                fn = (lambda e, o=dst, i_=buf[:, 0:n]:
                      e.tensor_copy(out=o, in_=i_))
            else:
                fn = (lambda e, o=dst, i_=buf[:, 0:n]: e.copy(out=o, in_=i_))
            v = op(ek, fn, waits=[(dW[j % 2], ld)], inc=sm)
            stg_rd[j % 2] = (sm, v)
            if ek == "dve":
                wc_v = v
            else:
                wc_a = v
            if j == 7:
                ms["wcast_q"] = (wc_v, wc_a)
            elif j == 15:
                ms["wcast_kv"] = (wc_v, wc_a)
        ms["wcast_v"] = wc_v
        ms["wcast_a"] = wc_a

        if mode == "S":
            v = op("dve", lambda e: e.tensor_copy(out=outbuf, in_=wqkv_sb[:, 0:D]),
                   waits=[(sV, wc_v), (sA, wc_a), (dX, dX.n), (dY, ms["bias_ld"]),
                          (dC, ms["consts_ld"])])
            o = dma("pool", y_d[0:T, :], outbuf, dO, waits=[(sV, v)])
            op("syncq", None, waits=[(dO, dO.n)])
            _emit_all(nc, OPS, _ES, cut)
            return nc

        prep_list = ([("a1", i) for i in range(64)] +
                     [("a2", i) for i in range(64)] +
                     [("f1",), ("f2",)] +
                     [("q1", i) for i in range(64)] +
                     [("q2", i) for i in range(64)])
        if mode == "A":
            prep_list = []
        prep_by_tile = [[] for _ in range(nt)]
        for j, task in enumerate(prep_list):
            prep_by_tile[min(nt - 1, j * nt // len(prep_list))].append(task)

        pst = {
            "stg_rd": {0: None, 1: None, 2: None, 3: None},
            "stg_ld": {},
            "acc": {"a1": 0, "a2": 0},
            "sgn_rd": None,
            "wq_rd": {0: None, 1: None},
            "w2st_rd": {0: None, 1: None},
            "ndma": 0,
            "acc_rd": None,
            "cursor": 0,
            "pend": [],
        }

        def prep_src(kind, i):
            if kind in ("a1", "q1"):
                return w1_d[(i // 2) * 128:(i // 2 + 1) * 128,
                            (i % 2) * 512:(i % 2) * 512 + 512]
            return w2_d[(i // 8) * 128:(i // 8 + 1) * 128,
                        (i % 8) * 512:(i % 8) * 512 + 512]

        def prep_issue_next_dma():
            while pst["cursor"] < len(prep_list):
                task = prep_list[pst["cursor"]]
                pst["cursor"] += 1
                if task[0] in ("f1", "f2"):
                    continue
                slot = pst["ndma"] % 4
                pst["ndma"] += 1
                ld = dma("syncq", wstage[slot], prep_src(task[0], task[1]),
                         dP[slot], waits=[pst["stg_rd"][slot]])
                pst["stg_ld"][task] = (slot, ld)
                return

        def prep_fin(grp):
            kind = "a1" if grp == 0 else "a2"
            acc_slot = grp * 2 + pst["acc"][kind] % 2
            nelem = float(HF * D)
            pv = op("pe", lambda e, a=accw[:, acc_slot:acc_slot + 1]:
                    e.matmul(out=psF[0:1, 0:1], lhsT=a, rhs=ones_col,
                             start=True, stop=True),
                    waits=[(sV, ms["abs_%s_last" % kind]), psum_free["F"],
                           (sV, ms["memset"])], inc=sP)
            op("dve", lambda e: e.tensor_copy(out=wsc[0:1, 0:1],
                                              in_=psF[0:1, 0:1]),
               waits=[(sP, pv)])
            op("dve", lambda e, n=nelem: e.tensor_scalar(
                out=wsc[0:1, 1:2], in0=wsc[0:1, 0:1],
                scalar1=1.0 / n, scalar2=1e-5, op0=ALU.mult, op1=ALU.max))
            vr = op("dve", lambda e: e.reciprocal(out=wsc[0:1, 2:3],
                                                  in_=wsc[0:1, 1:2]), inc=sV)
            pv = op("pe", lambda e: e.matmul(out=psF[:, 0:1], lhsT=ones_row,
                                             rhs=wsc[0:1, 2:3],
                                             start=True, stop=True),
                    waits=[(sV, vr)], inc=sP)
            pv = op("pe", lambda e: e.matmul(out=psF[:, 1:2], lhsT=ones_row,
                                             rhs=wsc[0:1, 1:2],
                                             start=True, stop=True), inc=sP)
            dst_r = bc_r1 if grp == 0 else bc_r2
            dst_w = bc_ws1 if grp == 0 else bc_ws2
            op("dve", lambda e, d=dst_r: e.tensor_copy(out=d, in_=psF[:, 0:1]),
               waits=[(sP, pv)])
            v = op("dve", lambda e, d=dst_w: e.tensor_scalar(
                out=d, in0=psF[:, 1:2], scalar1=1.0 / 127.0, scalar2=None,
                op0=ALU.mult), inc=sV)
            psum_free["F"] = (sV, v)
            ms["wsc_%d" % grp] = v

        def prep_emit(task):
            kind = task[0]
            if kind == "f1":
                prep_fin(0); return
            if kind == "f2":
                prep_fin(1); return
            i = task[1]
            slot, ld = pst["stg_ld"][task]
            if kind in ("a1", "a2"):
                cnt = pst["acc"][kind]
                g = 0 if kind == "a1" else 1
                src_s = g * 2 + cnt % 2
                dst_s = g * 2 + (cnt + 1) % 2
                op("dve", lambda e, s=wstage[slot]: e.tensor_reduce(
                    out=accw[:, 4:5], in_=s, axis=AX.X, op=ALU.add,
                    apply_absolute_value=True),
                    waits=[(dP[slot], ld), (sV, ms["memset"]),
                           pst["acc_rd"]], nosync=True)
                v = op("dve", lambda e, a=accw[:, src_s:src_s + 1],
                       b=accw[:, dst_s:dst_s + 1]:
                       e.tensor_tensor(out=b, in0=a, in1=accw[:, 4:5],
                                       op=ALU.add), inc=sV)
                pst["acc_rd"] = (sV, v)
                pst["acc"][kind] = cnt + 1
                pst["stg_rd"][slot] = (sV, v)
                ms["abs_%s_last" % kind] = v
            else:
                g = 0 if kind == "q1" else 1
                wqs = i % 2
                if any(p[2] == wqs for p in pst["pend"]):
                    prep_flushT()
                bc = bc_r1 if g == 0 else bc_r2
                op("dve", lambda e, s=wstage[slot], b=bc: e.tensor_scalar(
                    out=sgnA, in0=s, scalar1=b, scalar2=0.5,
                    op0=ALU.mult, op1=ALU.is_ge),
                    waits=[(dP[slot], ld), (sV, ms["wsc_%d" % g]),
                           pst["sgn_rd"]], nosync=True)
                va = op("dve", lambda e, s=wstage[slot], b=bc: e.tensor_scalar(
                    out=sgnB, in0=s, scalar1=b, scalar2=-0.5,
                    op0=ALU.mult, op1=ALU.is_ge), inc=sV, nosync=True)
                pst["stg_rd"][slot] = (sV, va)
                v = op("dve", lambda e, o=wqch[wqs]: e.scalar_tensor_tensor(
                    out=o, in0=sgnA, scalar=-1.0, in1=sgnB,
                    op0=ALU.add, op1=ALU.add),
                    waits=[pst["wq_rd"][wqs]], inc=sV)
                pst["sgn_rd"] = (sV, v)
                pst["pend"].append((kind, i, wqs, v))
            prep_issue_next_dma()

        def prep_flushT():
            for kind, i, wqs, vq in pst["pend"]:
                sp, key = next_stage()
                pv = None
                for j in range(4):
                    pv = op("pe", lambda e, jj=j, s=wqch[wqs], p=sp: e.transpose(
                        out=p[:, jj * 128:(jj + 1) * 128],
                        in_=s[:, jj * 128:(jj + 1) * 128], identity=identb),
                        waits=([(sV, vq), psum_free[key],
                                (dC, ms["consts_ld"])]
                               if j == 0 else []),
                        inc=(sP if j == 3 else None))
                pst["wq_rd"][wqs] = (sP, pv)
                if kind == "q1":
                    hb, half = i // 2, i % 2
                    dst = W1qT.rearrange("p (c h) -> p c h", c=DC)[
                        :, half * 4:half * 4 + 4, hb * 128:(hb + 1) * 128]
                    v = op("dve", lambda e, d=dst, p=sp: e.tensor_copy(
                        out=d,
                        in_=p[:, 0:512].rearrange("p (c h) -> p c h", c=4)),
                        waits=[(sP, pv), (sV, ms["wcast_v"]),
                               (sA, ms["wcast_a"])], inc=sV, nosync=True)
                    psum_free[key] = (sV, v)
                else:
                    db, s8 = i // 8, i % 8
                    st = i % 2
                    v = op("dve", lambda e, o=w2st[st], p=sp: e.tensor_copy(
                        out=o, in_=p[:, 0:512]),
                        waits=[(sP, pv), pst["w2st_rd"][st]], inc=sV,
                        nosync=True)
                    psum_free[key] = (sV, v)
                    dst = w2t_d.rearrange("p (c d) -> p c d", c=32)[
                        :, s8 * 4:s8 * 4 + 4, db * 128:(db + 1) * 128]
                    b = dma("syncq", dst,
                            w2st[st].rearrange("p (c d) -> p c d", c=4),
                            dB[st], waits=[(sV, v)])
                    pst["w2st_rd"][st] = (dB[st], b)
            pst["pend"] = []

        for _ in range(4):
            prep_issue_next_dma()

        spill_ms = {}
        (CEL_SUM, CEL_SSQ, CEL_M, CEL_MSQ, CEL_VAR, CEL_LNV, CEL_RSTD,
         CEL_NMR) = range(8)
        CEL_NMX, CEL_SUMS, CEL_RCP = 8, 24, 40
        CEL2 = 56

        def a_ln1_stats(t):
            par = t % 2
            cl = cells[t % 2]
            va = op("act", lambda e, x=xin[par], c=cl: e.activation(
                out=outbuf, in_=x, func=AF.Square,
                accum_out=c[:, CEL_SSQ:CEL_SSQ + 1]),
                waits=[(dX, xin_ld[t])], inc=sA)
            op("dve", lambda e, x=xin[par], c=cl: e.tensor_reduce(
                out=c[:, CEL_SUM:CEL_SUM + 1], in_=x, axis=AX.X, op=ALU.add),
                waits=[(dX, xin_ld[t])], nosync=True)
            op("dve", lambda e, c=cl: e.tensor_scalar(
                out=c[:, CEL_M:CEL_M + 1], in0=c[:, CEL_SUM:CEL_SUM + 1],
                scalar1=1.0 / D, scalar2=None, op0=ALU.mult))
            op("dve", lambda e, c=cl: e.tensor_tensor(
                out=c[:, CEL_MSQ:CEL_MSQ + 1], in0=c[:, CEL_M:CEL_M + 1],
                in1=c[:, CEL_M:CEL_M + 1], op=ALU.mult))
            v = op("dve", lambda e, c=cl: e.scalar_tensor_tensor(
                out=c[:, CEL_VAR:CEL_VAR + 1], in0=c[:, CEL_SSQ:CEL_SSQ + 1],
                scalar=1.0 / D, in1=c[:, CEL_MSQ:CEL_MSQ + 1],
                op0=ALU.mult, op1=ALU.subtract), waits=[(sA, va)], inc=sV)
            op("act", lambda e, c=cl: e.activation(
                out=c[:, CEL_LNV:CEL_LNV + 1], in_=c[:, CEL_VAR:CEL_VAR + 1],
                func=AF.Ln, bias=c_eps),
                waits=[(sV, v), (sV, ms["memset"])])
            va = op("act", lambda e, c=cl: e.activation(
                out=c[:, CEL_RSTD:CEL_RSTD + 1], in_=c[:, CEL_LNV:CEL_LNV + 1],
                func=AF.Exp, scale=-0.5), inc=sA)
            ms[("rstd", t)] = va

        def a_ln1_apply(t):
            par = t % 2
            cl = cells[t % 2]
            v = op("dve", lambda e, c=cl: e.tensor_scalar(
                out=c[:, CEL_NMR:CEL_NMR + 1], in0=c[:, CEL_M:CEL_M + 1],
                scalar1=c[:, CEL_RSTD:CEL_RSTD + 1], scalar2=-1.0,
                op0=ALU.mult, op1=ALU.mult),
                waits=[(sA, ms[("rstd", t)])], inc=sV)
            va = op("act", lambda e, x=xin[par], c=cl: e.activation(
                out=xln, in_=x, func=AF.Identity,
                bias=c[:, CEL_NMR:CEL_NMR + 1],
                scale=c[:, CEL_RSTD:CEL_RSTD + 1]),
                waits=[(sV, v), (sA, ms.get(("xlnT", t - 1), 0))], inc=sA)
            ms[("xln", t)] = va

        def a_xlnT(t):
            sp, key = next_stage()
            pv = None
            for c in range(DC):
                pv = op("pe", lambda e, cc=c, p=sp: e.transpose(
                    out=p[:, cc * 128:(cc + 1) * 128],
                    in_=xln[:, cc * 128:(cc + 1) * 128], identity=identb),
                    waits=([(sA, ms[("xln", t)]), psum_free[key],
                            (dC, ms["consts_ld"])] if c == 0 else []),
                    inc=(sP if c == DC - 1 else None))
            v = op("act", lambda e, p=sp: e.copy(out=xlnT, in_=p),
                   waits=[(sP, pv), (sP, ms.get(("qkv", t - 1, 2), 0))],
                   inc=sA, nosync=True)
            psum_free[key] = (sA, v)
            ms[("xlnT", t)] = v

        def a_qkv(t, grp):
            par = t % 2
            psX, pfk = ((psA, "A"), (psB, "B"), (psC, "C"))[grp]
            wq = ms["wcast_q"] if grp == 0 else ms["wcast_kv"]
            w0 = [(sA, ms[("xlnT", t)]), psum_free[pfk],
                  (sV, wq[0]), (sA, wq[1])]
            pv = None
            for c in range(DC):
                for n in range(2):
                    pv = op("pe", lambda e, cc=c, nn=n, g=grp, p=psX:
                            e.matmul(
                        out=p[:, nn * 512:(nn + 1) * 512],
                        lhsT=xlnT[:, cc * 128:(cc + 1) * 128],
                        rhs=wqkv_sb[:, cc * 3 * D + g * D + nn * 512:
                                    cc * 3 * D + g * D + (nn + 1) * 512],
                        start=(cc == 0), stop=(cc == DC - 1)),
                        waits=(w0 if (c == 0 and n == 0) else []),
                        inc=(sP if (c == DC - 1 and n == 1) else None))
            ms[("qkv", t, grp)] = pv
            if grp == 0:
                va = op("act", lambda e: e.activation(out=q_sb, in_=psA,
                                                      func=AF.Copy,
                                                      scale=0.125),
                        waits=[(sP, pv), (sA, ms.get(("qT", t - 1), 0))],
                        inc=sA, nosync=True)
                ms[("q_sb", t)] = va
                psum_free["A"] = (sA, va)
            elif grp == 1:
                v = op("dve", lambda e: e.tensor_copy(out=k_sb, in_=psB),
                       waits=[(sP, pv), (sA, ms.get(("kT", t - 1), 0))],
                       inc=sV, nosync=True)
                ms[("k_sb", t)] = v
                psum_free["B"] = (sV, v)
            else:
                v = op("dve", lambda e, s=par: e.tensor_copy(
                    out=v_hist[:, s * D:(s + 1) * D], in_=psC),
                    waits=[(sP, pv), (sP, ms.get(("av", t - 1, 3), 0))],
                    inc=sV, nosync=True)
                ms[("v_sb", t)] = v
                psum_free["C"] = (sV, v)

        def a_qkT(t):
            par = t % 2
            for src_name, buf, wkey in (("q", q_sb, ("q_sb", t)),
                                        ("k", k_sb, ("k_sb", t))):
                wsem = sA if src_name == "q" else sV
                for r in range(2):
                    sp, key = next_stage()
                    pv = None
                    for hh in range(8):
                        h = r * 8 + hh
                        pv = op("pe", lambda e, b=buf, h_=h, hh_=hh, p=sp:
                                e.transpose(
                            out=p[0:64, hh_ * 128:(hh_ + 1) * 128],
                            in_=b[:, h_ * 64:(h_ + 1) * 64], identity=identb),
                            waits=([(wsem, ms[wkey]), psum_free[key]]
                                   if hh == 0 else []),
                            inc=(sP if hh == 7 else None))
                    if src_name == "q":
                        dst = qT[0:64, r * 1024:(r + 1) * 1024]
                    else:
                        dst = kT_hist[0:64, par * 2048 + r * 1024:
                                      par * 2048 + (r + 1) * 1024]
                    v = op("act", lambda e, d=dst, p=sp: e.copy(
                        out=d, in_=p[0:64, 0:1024]),
                        waits=[(sP, pv), (sP, ms.get(("scores", t - 1, 3), 0))],
                        inc=sA, nosync=True)
                    psum_free[key] = (sA, v)
                ms[("qT", t) if src_name == "q" else ("kT", t)] = v

        def a_scores(t, g):
            par, prv = t % 2, 1 - (t % 2)
            tvar = 1 if (t % TPS == 0) else 0
            psX = psA if g % 2 == 0 else psB
            pfk = "A" if g % 2 == 0 else "B"
            w0 = [(sA, ms[("qT", t)]), (sA, ms[("kT", t)]),
                  psum_free[pfk], (dC, ms["consts_ld"])]
            pv = None
            first = True
            for half in range(2):
                wh = w0
                for hh in (half * 2, half * 2 + 1):
                    h = 4 * g + hh
                    for kt, slot in ((0, prv), (1, par)):
                        pv = op("pe", lambda e, p=psX, hh_=hh, kt_=kt,
                                h_=h, sl=slot: e.matmul(
                            out=p[:, hh_ * 256 + kt_ * 128:
                                  hh_ * 256 + (kt_ + 1) * 128],
                            lhsT=qT[0:64, h_ * 128:(h_ + 1) * 128],
                            rhs=kT_hist[0:64, sl * 2048 + h_ * 128:
                                        sl * 2048 + (h_ + 1) * 128],
                            start=(hh_ % 2 == 0 and kt_ == 0), stop=False),
                            waits=(wh if first else []), inc=None)
                        first = False
                for hh in (half * 2, half * 2 + 1):
                    last = (hh == half * 2 + 1)
                    pv = op("pe", lambda e, p=psX, hh_=hh, mv=tvar, l=last:
                            e.matmul(
                                out=p[:, hh_ * 256:(hh_ + 1) * 256],
                                lhsT=identb,
                                rhs=masks[:, mv * 256:(mv + 1) * 256],
                                start=False, stop=l),
                            inc=(sP if last else None))
                if half == 0:
                    ms[("sc_h0", t, g)] = pv
            ms[("scores", t, g)] = pv

        def a_sm_max(t, g):
            cl = cells[t % 2]
            psX = psA if g % 2 == 0 else psB
            v0 = op("dve", lambda e, p=psX, c=cl, g_=g: e.tensor_reduce(
                out=c[:, CEL_NMX + 4 * g_:CEL_NMX + 4 * g_ + 2],
                in_=p[:, 0:512].rearrange("p (h k) -> p h k", h=2),
                axis=AX.X, op=ALU.max, negate=True),
                waits=[(sP, ms[("sc_h0", t, g)])], inc=sV, nosync=True)
            v = op("dve", lambda e, p=psX, c=cl, g_=g: e.tensor_reduce(
                out=c[:, CEL_NMX + 4 * g_ + 2:CEL_NMX + 4 * g_ + 4],
                in_=p[:, 512:1024].rearrange("p (h k) -> p h k", h=2),
                axis=AX.X, op=ALU.max, negate=True),
                waits=[(sP, ms[("scores", t, g)])], inc=sV, nosync=True)
            ms[("nmx0", t, g)] = v0
            ms[("nmx", t, g)] = v

        def a_sm_exp(t, g):
            cl = cells[t % 2]
            psX = psA if g % 2 == 0 else psB
            pfk = "A" if g % 2 == 0 else "B"
            va = None
            evl = []
            for hh in range(4):
                wv = [(sV, ms[("nmx0", t, g)] if hh < 2
                       else ms[("nmx", t, g)])]
                if g >= 2:
                    wv.append((sP, ms.get(("pT_mm", t, g - 2), 0)))
                else:
                    wv.append((sP, ms.get(("pT_mm", t - 1, g + 2), 0)))
                va = op("act", lambda e, p=psX, c=cl, g_=g, hh_=hh:
                        e.activation(
                    out=probs[g_ % 2][:, hh_ * 256:(hh_ + 1) * 256],
                    in_=p[:, hh_ * 256:(hh_ + 1) * 256], func=AF.Exp,
                    bias=c[:, CEL_NMX + 4 * g_ + hh_:
                           CEL_NMX + 4 * g_ + hh_ + 1],
                    accum_out=c[:, CEL_SUMS + 4 * g_ + hh_:
                                CEL_SUMS + 4 * g_ + hh_ + 1]),
                    waits=(wv if hh in (0, 2) else []),
                    inc=sA, nosync=(hh > 0))
                evl.append(va)
            ms[("exp", t, g)] = va
            ms[("expv", t, g)] = evl
            psum_free[pfk] = (sA, va)

        def a_sm_norm(t, g):
            cl = cells[t % 2]
            ev = ms[("expv", t, g)]
            r0 = op("dve", lambda e, c=cl, g_=g: e.reciprocal(
                out=c[:, CEL_RCP + 4 * g_:CEL_RCP + 4 * g_ + 2],
                in_=c[:, CEL_SUMS + 4 * g_:CEL_SUMS + 4 * g_ + 2]),
                waits=[(sA, ev[1])], inc=sV, nosync=True)
            rc = r0
            v2 = None
            for hh in range(4):
                if hh == 2:
                    rc = op("dve", lambda e, c=cl, g_=g: e.reciprocal(
                        out=c[:, CEL_RCP + 4 * g_ + 2:CEL_RCP + 4 * g_ + 4],
                        in_=c[:, CEL_SUMS + 4 * g_ + 2:
                               CEL_SUMS + 4 * g_ + 4]),
                        waits=[(sA, ev[3])], inc=sV, nosync=True)
                v2 = op("dve", lambda e, c=cl, g_=g, hh_=hh: e.tensor_scalar(
                    out=probsn[g_ % 2][:, hh_ * 256:(hh_ + 1) * 256],
                    in0=probs[g_ % 2][:, hh_ * 256:(hh_ + 1) * 256],
                    scalar1=c[:, CEL_RCP + 4 * g_ + hh_:
                              CEL_RCP + 4 * g_ + hh_ + 1],
                    scalar2=None, op0=ALU.mult),
                    waits=[(sV, rc)], inc=sV, nosync=(hh > 0))
                if hh == 1:
                    ms[("pn0", t, g)] = v2
            ms[("pn", t, g)] = v2

        def a_pt(t, g):
            sp, key = next_stage()
            pv = None
            for j in range(8):
                if j == 0:
                    w = [(sV, ms[("pn0", t, g)]), psum_free[key]]
                elif j == 4:
                    w = [(sV, ms[("pn", t, g)])]
                else:
                    w = []
                pv = op("pe", lambda e, jj=j, g_=g, p=sp: e.transpose(
                    out=p[:, jj * 128:(jj + 1) * 128],
                    in_=probsn[g_ % 2][:, jj * 128:(jj + 1) * 128],
                    identity=identb),
                    waits=w, inc=(sP if j == 7 else None))
            ms[("pT_mm", t, g)] = pv
            wv = [(sP, pv)]
            if g >= 1:
                wv.append((sP, ms.get(("av", t, g - 1), 0)))
            else:
                wv.append((sP, ms.get(("av", t - 1, 3), 0)))
            v = op("act", lambda e, g_=g, p=sp: e.activation(
                out=probsT[g_ % 2], in_=p, func=AF.Copy), waits=wv, inc=sA,
                nosync=True)
            psum_free[key] = (sA, v)
            ms[("pT", t, g)] = v

        def a_av(t, g):
            par, prv = t % 2, 1 - (t % 2)
            pv = None
            for hh in range(4):
                h = 4 * g + hh
                bp = 64 * (h % 2)
                p = h // 2
                for kt, slot in ((0, prv), (1, par)):
                    wv = [(sA, ms[("pT", t, g)]), (sV, ms[("v_sb", t)])]
                    if g == 0 and hh == 0 and kt == 0:
                        wv.append(psum_free["C"])
                    pv = op("pe", lambda e, hh_=hh, h_=h, bp_=bp, pp=p,
                            kt_=kt, sl=slot, g_=g: e.matmul(
                        out=psC[bp_:bp_ + 64, pp * 128:(pp + 1) * 128],
                        lhsT=v_hist[:, sl * D + h_ * 64:
                                    sl * D + h_ * 64 + 64],
                        rhs=probsT[g_ % 2][:, (hh_ * 2 + kt_) * 128:
                                           (hh_ * 2 + kt_ + 1) * 128],
                        start=(kt_ == 0), stop=(kt_ == 1),
                        tile_position=(0, bp_)),
                        waits=(wv if kt == 0 else []),
                        inc=(sP if (hh == 3 and kt == 1) else None))
            ms[("av", t, g)] = pv

        def a_outproj(t):
            v = op("dve", lambda e: e.tensor_copy(out=attnout, in_=psC),
                   waits=[(sP, ms[("av", t, 3)]),
                          (sP, ms.get(("outproj", t - 1), 0))], inc=sV,
                   nosync=True)
            ms[("attnout", t)] = v
            psum_free["C"] = (sV, v)
            pv = None
            for n in range(2):
                for c in range(DC):
                    wv = [(sV, ms[("attnout", t)]), psum_free["C"],
                          (sV, ms["wcast_v"]), (sA, ms["wcast_a"])]
                    pv = op("pe", lambda e, cc=c, nn=n: e.matmul(
                        out=psC[:, nn * 512:(nn + 1) * 512],
                        lhsT=attnout[:, cc * 128:(cc + 1) * 128],
                        rhs=wout_sb[:, cc * D + nn * 512:cc * D + (nn + 1) * 512],
                        start=(cc == 0), stop=(cc == DC - 1)),
                        waits=(wv if (n == 0 and c == 0) else []),
                        inc=(sP if (n == 1 and c == DC - 1) else None))
            ms[("outproj", t)] = pv

        def a_res(t):
            par = t % 2
            v = op("dve", lambda e, x=xin[par], o=x1buf[par]: e.tensor_tensor(
                out=o, in0=psC, in1=x, op=ALU.add),
                waits=[(sP, ms[("outproj", t)]), (dS, spill_ms.get(t - 2, 0))],
                inc=sV, nosync=True)
            ms[("res", t)] = v
            psum_free["C"] = (sV, v)
            spill_ms[t] = dma("syncq", x1_d[t * T:(t + 1) * T, :], x1buf[par],
                              dS, waits=[(sV, v)])

        def a_ln2_acc(t):
            par = t % 2
            cl = cells[t % 2]
            va = op("act", lambda e, o=x1buf[par], c=cl: e.activation(
                out=outbuf, in_=o, func=AF.Square,
                accum_out=c[:, CEL2:CEL2 + 1]),
                waits=[(sV, ms[("res", t)])], inc=sA)
            va = op("act", lambda e, o=x1buf[par], c=cl: e.activation(
                out=outbuf, in_=o, func=AF.Identity,
                accum_out=c[:, CEL2 + 1:CEL2 + 2]), inc=sA)
            ms[("ln2a", t)] = va

        def a_ln2_fin(t):
            cl = cells[t % 2]
            op("dve", lambda e, c=cl, t_=t: e.tensor_scalar(
                out=statsN[:, t_:t_ + 1], in0=c[:, CEL2 + 1:CEL2 + 2],
                scalar1=-1.0 / D, scalar2=None, op0=ALU.mult),
                waits=[(sA, ms[("ln2a", t)])], nosync=True)
            op("dve", lambda e, c=cl, t_=t: e.tensor_tensor(
                out=c[:, CEL2 + 2:CEL2 + 3], in0=statsN[:, t_:t_ + 1],
                in1=statsN[:, t_:t_ + 1], op=ALU.mult))
            v = op("dve", lambda e, c=cl: e.scalar_tensor_tensor(
                out=c[:, CEL2 + 3:CEL2 + 4], in0=c[:, CEL2:CEL2 + 1],
                scalar=1.0 / D, in1=c[:, CEL2 + 2:CEL2 + 3],
                op0=ALU.mult, op1=ALU.subtract), inc=sV)
            op("act", lambda e, c=cl: e.activation(
                out=c[:, CEL2 + 4:CEL2 + 5], in_=c[:, CEL2 + 3:CEL2 + 4],
                func=AF.Ln, bias=c_eps), waits=[(sV, v)])
            va = op("act", lambda e, c=cl, t_=t: e.activation(
                out=statsR[:, t_:t_ + 1], in_=c[:, CEL2 + 4:CEL2 + 5],
                func=AF.Exp, scale=-0.5), inc=sA)
            ms[("ln2", t)] = va

        gtile = 0
        a_ln1_stats(0)
        a_ln1_apply(0)
        a_xlnT(0)
        a_qkv(0, 0)
        a_qkv(0, 1)
        a_qkv(0, 2)
        for t in range(nt):
            fence(gtile)
            gtile += 1
            ptasks = list(prep_by_tile[t])

            def prep_drip(k):
                for _ in range(k):
                    if ptasks:
                        prep_emit(ptasks.pop(0))

            if t + 1 < nt:
                a_ln1_stats(t + 1)
            if t == 0:
                a_qkT(0)
            a_scores(t, 0)
            a_sm_max(t, 0)
            a_sm_exp(t, 0)
            if t > 0:
                a_ln2_fin(t - 1)
            a_scores(t, 1)
            a_sm_max(t, 1)
            a_sm_exp(t, 1)
            a_sm_norm(t, 0)
            a_scores(t, 2)
            a_sm_max(t, 2)
            prep_drip(2)
            a_pt(t, 0)
            a_av(t, 0)
            prep_flushT()
            a_sm_exp(t, 2)
            a_sm_norm(t, 1)
            if t + 1 < nt:
                a_ln1_apply(t + 1)
            a_scores(t, 3)
            a_sm_max(t, 3)
            prep_drip(2)
            a_pt(t, 1)
            a_av(t, 1)
            prep_flushT()
            a_sm_exp(t, 3)
            a_sm_norm(t, 2)
            if t + 1 < nt:
                a_xlnT(t + 1)
            prep_drip(2)
            a_pt(t, 2)
            a_av(t, 2)
            prep_flushT()
            if t + 1 < nt:
                a_qkv(t + 1, 0)
            a_sm_norm(t, 3)
            a_pt(t, 3)
            a_av(t, 3)
            prep_drip(2)
            if t + 1 < nt:
                a_qkv(t + 1, 1)
                a_qkT(t + 1)
            a_outproj(t)
            a_res(t)
            if t + 1 < nt:
                a_qkv(t + 1, 2)
            a_ln2_acc(t)
            if t + 2 < nt:
                xin_ld[t + 2] = dma(
                    "syncq", xin[t % 2], xc[(t + 2) * T:(t + 3) * T, :],
                    dX, waits=[(sV, ms[("res", t)]), (sA, ms[("xln", t)])])
            prep_drip(len(ptasks))
            prep_flushT()
            tile_end()
        a_ln2_fin(nt - 1)

        a_end = {ek: ENG_SEM[ek].n for ek in ENG_SEM}
        for ek in ("pe", "dve", "act", "pool", "syncq"):
            waits = [(ENG_SEM[o], a_end[o]) for o in ENG_SEM if o != ek]
            for s in (dS, dB[0], dB[1], dP[0], dP[1], dW[0], dW[1], dC, dY, dX):
                waits.append((s, s.n))
            op(ek, None, waits=waits)

        w2qt_lds = []
        for j in range(4):
            w2qt_lds.append(dma(
                "act", W2qT[:, j * 8192:(j + 1) * 8192],
                w2t_d[:, j * 8192:(j + 1) * 8192], dW[j % 2],
                waits=([(dB[0], dB[0].n), (dB[1], dB[1].n)] if j < 2 else [])))
        ms["w2qt0"] = w2qt_lds[2]
        ms["w2qt1"] = w2qt_lds[3]

        x1_ld = {}
        out_ms = {}
        FQ = 0
        FH = 12
        cbb = tib.bitcast(F32)

        def x1s(t):
            return (x1buf[0], x1buf[1], x1c)[t % 3]

        def emit_round(src_ap, dst_ap, scale_cell, bias_cell, bias_imm, nsub,
                       first_wait, war_wait, r0=0):
            subs = []
            for r in range(r0, r0 + nsub):
                wv = (list(first_wait) + list(war_wait)) if r == r0 else []
                sc2 = bias_cell if bias_cell is not None else bias_imm
                op("dve", lambda e, rr=r, s=src_ap, sc=scale_cell,
                   b=sc2: e.tensor_scalar(
                    out=t1b, in0=s[:, rr * 1024:(rr + 1) * 1024],
                    scalar1=sc, scalar2=b, op0=ALU.mult, op1=ALU.add),
                    waits=wv)
                op("dve", lambda e: e.tensor_copy(out=tib, in_=t1b))
                op("dve", lambda e: e.tensor_copy(out=tfb, in_=tib))
                op("dve", lambda e: e.tensor_tensor(
                    out=cbb, in0=t1b, in1=tfb, op=ALU.is_ge))
                rl = op("dve", lambda e, rr=r, d=dst_ap: e.scalar_tensor_tensor(
                    out=d[:, rr * 1024:(rr + 1) * 1024], in0=tfb,
                    scalar=-641.0, in1=cbb, op0=ALU.add, op1=ALU.add), inc=sV)
                subs.append(rl)
            return subs

        for t in range(min(3, nt)):
            x1_ld[t] = dma("syncq", x1s(t), x1_d[t * T:(t + 1) * T, :],
                           dL, waits=[(dS, dS.n)])

        def emit_w1_quarter(t, j):
            cl = cells[t % 2]
            psX = psA if j % 2 == 0 else psB
            pfk = "A" if j % 2 == 0 else "B"
            wv = [(sA, ms[("xqT", t)]), (sA, ms[("i1r", t)]),
                  psum_free[pfk], (dY, ms["bias_ld"])]
            pv = None
            for n in range(2):
                pv = op("pe", lambda e, jj=j, nn=n, p=psX: e.matmul(
                    out=p[:, nn * 512:(nn + 1) * 512], lhsT=invs1row,
                    rhs=b1bf[0:1, jj * 1024 + nn * 512:
                             jj * 1024 + (nn + 1) * 512],
                    start=True, stop=False),
                    waits=(wv if n == 0 else []), inc=None)
                for c in range(DC):
                    pv = op("pe", lambda e, jj=j, nn=n, cc=c, p=psX:
                            e.matmul(
                        out=p[:, nn * 512:(nn + 1) * 512],
                        lhsT=xqT[:, cc * 128:(cc + 1) * 128],
                        rhs=W1qT[:, cc * HF + jj * 1024 + nn * 512:
                                 cc * HF + jj * 1024 + (nn + 1) * 512],
                        start=False, stop=(cc == DC - 1)),
                        inc=(sP if (n == 1 and c == DC - 1) else None))
            ms[("w1", t, j)] = pv

        def _hqs(t, j):
            l = ms.get(("hqsub", t))
            return 0 if l is None else l[j]

        def emit_gelu(t, j):
            cl = cells[t % 2]
            psX = psA if j % 2 == 0 else psB
            pfk = "A" if j % 2 == 0 else "B"
            va = op("act", lambda e, jj=j, p=psX, c=cl: e.activation(
                out=h_sb[:, jj * 1024:(jj + 1) * 1024], in_=p,
                func=AF.Gelu, scale=c[:, FQ + 9:FQ + 10]),
                waits=[(sP, ms[("w1", t, j)]), (sV, _hqs(t - 1, j)),
                       (sV, ms[("fq", t)])], inc=sA, nosync=True)
            ms[("gelu", t, j)] = va
            psum_free[pfk] = (sA, va)
            op("dve", lambda e, jj=j, c=cl: e.tensor_reduce(
                out=c[:, FH + jj:FH + jj + 1],
                in_=h_sb[:, jj * 1024:(jj + 1) * 1024], axis=AX.X,
                op=ALU.max, apply_absolute_value=True),
                waits=[(sA, va)], nosync=True)

        def f_front_dve(t):
            cl = cells[t % 2]
            op("dve", lambda e, x=x1s(t), c=cl: e.tensor_reduce(
                out=c[:, FQ:FQ + 1], in_=x, axis=AX.X, op=ALU.max),
                waits=[(dL, x1_ld[t]), (sA, ms.get(("gelu", t - 2, 3), 0)),
                       (sP, ms.get(("sT1", t - 2), 0))])
            op("dve", lambda e, x=x1s(t), c=cl: e.tensor_reduce(
                out=c[:, FQ + 1:FQ + 2], in_=x, axis=AX.X, op=ALU.min),
                waits=[(dL, x1_ld[t])], nosync=True)
            op("dve", lambda e, c=cl, t_=t: e.tensor_tensor(
                out=c[:, FQ + 2:FQ + 3], in0=c[:, FQ:FQ + 1],
                in1=statsN[:, t_:t_ + 1], op=ALU.add))
            op("dve", lambda e, c=cl, t_=t: e.tensor_tensor(
                out=c[:, FQ + 3:FQ + 4], in0=c[:, FQ + 1:FQ + 2],
                in1=statsN[:, t_:t_ + 1], op=ALU.add))
            op("dve", lambda e, c=cl: e.tensor_scalar(
                out=c[:, FQ + 4:FQ + 5], in0=c[:, FQ + 3:FQ + 4],
                scalar1=-1.0, scalar2=None, op0=ALU.mult))
            op("dve", lambda e, c=cl: e.tensor_tensor(
                out=c[:, FQ + 5:FQ + 6], in0=c[:, FQ + 2:FQ + 3],
                in1=c[:, FQ + 4:FQ + 5], op=ALU.max))
            op("dve", lambda e, c=cl: e.reciprocal(
                out=c[:, FQ + 6:FQ + 7], in_=c[:, FQ + 5:FQ + 6]))
            op("dve", lambda e, c=cl: e.tensor_scalar(
                out=c[:, FQ + 7:FQ + 8], in0=c[:, FQ + 6:FQ + 7],
                scalar1=127.0, scalar2=None, op0=ALU.mult))
            op("dve", lambda e, c=cl, t_=t: e.tensor_tensor(
                out=c[:, FQ + 8:FQ + 9], in0=c[:, FQ + 5:FQ + 6],
                in1=statsR[:, t_:t_ + 1], op=ALU.mult))
            op("dve", lambda e, c=cl: e.tensor_scalar(
                out=c[:, FQ + 9:FQ + 10], in0=c[:, FQ + 8:FQ + 9],
                scalar1=bc_ws1, scalar2=None, op0=ALU.mult))
            op("dve", lambda e, c=cl: e.reciprocal(
                out=c[:, FQ + 10:FQ + 11], in_=c[:, FQ + 9:FQ + 10]))
            v = op("dve", lambda e, c=cl, t_=t: e.tensor_scalar(
                out=c[:, FQ + 11:FQ + 12], in0=statsN[:, t_:t_ + 1],
                scalar1=c[:, FQ + 7:FQ + 8], scalar2=RK,
                op0=ALU.mult, op1=ALU.add), inc=sV)
            ms[("fq", t)] = v

            rl = emit_round(x1s(t), xq,
                            cl[:, FQ + 7:FQ + 8], cl[:, FQ + 11:FQ + 12], None,
                            1, [(sV, ms[("fq", t)])],
                            [(sP, ms.get(("xqT_mm", t - 1), 0))])
            ms[("xq", t)] = rl[-1]

        def f_front_pe(t):
            cl = cells[t % 2]
            pv = op("pe", lambda e, c=cl: e.transpose(
                out=psF[0:1, 0:128], in_=c[:, FQ + 10:FQ + 11],
                identity=identf),
                waits=[(sV, ms[("fq", t)]), psum_free["F"],
                       (dC, ms["consts_ld"]),
                       (sP, ms.get(("w1", t - 1, 3), 0))], inc=sP)
            ms[("sT1", t)] = pv
            va = op("act", lambda e: e.activation(
                out=invs1row, in_=psF[0:1, 0:128], func=AF.Copy),
                waits=[(sP, pv)], inc=sA, nosync=True)
            ms[("i1r", t)] = va
            psum_free["F"] = (sA, va)

            sp, key = next_stage()
            pv = None
            for c in range(DC):
                pv = op("pe", lambda e, cc=c, p=sp: e.transpose(
                    out=p[:, cc * 128:(cc + 1) * 128],
                    in_=xq[:, cc * 128:(cc + 1) * 128], identity=identb),
                    waits=([(sV, ms[("xq", t)]), psum_free[key]]
                           if c == 0 else []),
                    inc=(sP if c == DC - 1 else None))
            ms[("xqT_mm", t)] = pv
            v = op("act", lambda e, p=sp: e.activation(
                out=xqT, in_=p, func=AF.Copy),
                waits=[(sP, pv), (sP, ms.get(("w1", t - 1, 3), 0))], inc=sA,
                nosync=True)
            psum_free[key] = (sA, v)
            ms[("xqT", t)] = v

            emit_w1_quarter(t, 0)
            emit_w1_quarter(t, 1)

        def f_fh(t):
            cl = cells[t % 2]
            op("dve", lambda e, c=cl: e.tensor_reduce(
                out=c[:, FH + 4:FH + 5], in_=c[:, FH:FH + 4], axis=AX.X,
                op=ALU.max))
            op("dve", lambda e, c=cl: e.reciprocal(
                out=c[:, FH + 8:FH + 9], in_=c[:, FH + 4:FH + 5]))
            op("dve", lambda e, c=cl: e.tensor_scalar(
                out=c[:, FH + 5:FH + 6], in0=c[:, FH + 8:FH + 9],
                scalar1=127.0, scalar2=None, op0=ALU.mult))
            op("dve", lambda e, c=cl: e.tensor_scalar(
                out=c[:, FH + 6:FH + 7], in0=c[:, FH + 4:FH + 5],
                scalar1=bc_ws2, scalar2=None, op0=ALU.mult))
            v = op("dve", lambda e, c=cl: e.reciprocal(
                out=c[:, FH + 7:FH + 8], in_=c[:, FH + 6:FH + 7]), inc=sV)
            ms[("fh", t)] = v

        def f_round0(t):
            cl = cells[t % 2]
            rl = emit_round(h_sb, hq, cl[:, FH + 5:FH + 6], None, RK, 1,
                            [(sV, ms[("fh", t)])],
                            [(sP, ms.get(("hqT_mm", t - 1), 0))])
            ms[("hqsub", t)] = rl

        def f_rounds123(t):
            cl = cells[t % 2]
            rl = emit_round(h_sb, hq, cl[:, FH + 5:FH + 6], None, RK, 3,
                            [], [], r0=1)
            ms[("hq", t)] = rl[-1]
            ms[("hqsub", t)] = ms[("hqsub", t)] + rl

        def f_back_T0(t):
            cl = cells[t % 2]
            pv = op("pe", lambda e, c=cl: e.transpose(
                out=psF[0:1, 128:256], in_=c[:, FH + 7:FH + 8],
                identity=identf),
                waits=[(sV, ms[("fh", t)]), psum_free["F"],
                       (sP, ms.get(("w2", t - 1), 0))], inc=sP)
            va = op("act", lambda e: e.activation(
                out=invs2row, in_=psF[0:1, 128:256], func=AF.Copy),
                waits=[(sP, pv)], inc=sA, nosync=True)
            ms[("i2r", t)] = va
            psum_free["F"] = (sA, va)
            _fT(t, (0,))

        def f_out(t):
            cl = cells[t % 2]
            v = op("dve", lambda e, c=cl, x=x1s(t): e.scalar_tensor_tensor(
                out=outbuf, in0=psC, scalar=c[:, FH + 6:FH + 7], in1=x,
                op0=ALU.mult, op1=ALU.add),
                waits=[(sP, ms[("w2", t)]), (dO, out_ms.get(t - 1, 0))],
                inc=sV)
            ms[("fout", t)] = v
            psum_free["C"] = (sV, v)
            out_ms[t] = dma("pool", y_d[t * T:(t + 1) * T, :], outbuf, dO,
                            waits=[(sV, v)])
            if t + 3 < nt:
                x1_ld[t + 3] = dma("syncq", x1s(t + 3),
                                   x1_d[(t + 3) * T:(t + 4) * T, :], dL,
                                   waits=[(sV, v)])

        def _fT(t, rrs):
            wv = [(sA, ms[("i2r", t)]),
                  (sV, ms.get(("fout", t - 1), 0)), (dW[0], ms["w2qt0"]),
                  (dW[1], ms["w2qt1"]), (dY, ms["bias_ld"])]
            pv = None
            evs = None
            for rr in rrs:
                sp, key = next_stage()
                for j in range(8):
                    pv = op("pe", lambda e, rr_=rr, jj=j, p=sp: e.transpose(
                        out=p[:, jj * 128:(jj + 1) * 128],
                        in_=hq[:, rr_ * 1024 + jj * 128:
                               rr_ * 1024 + (jj + 1) * 128],
                        identity=identb),
                        waits=([(sV, ms[("hqsub", t)][rr]), psum_free[key]]
                               if j == 0 else []),
                        inc=(sP if j == 7 else None))
                if rr == 3:
                    ms[("hqT_mm", t)] = pv
                evs = op("act", lambda e, rr_=rr, p=sp: e.activation(
                    out=hqT[:, rr_ * 1024:(rr_ + 1) * 1024], in_=p,
                    func=AF.Copy),
                    waits=[(sP, pv)] + ([(sP, ms.get(("w2", t - 1), 0))]
                                        if rr == 0 else []), inc=sA,
                    nosync=True)
                psum_free[key] = (sA, evs)
                for n in range(2):
                    if rr == 0:
                        op("pe", lambda e, nn=n: e.matmul(
                            out=psC[:, nn * 512:(nn + 1) * 512],
                            lhsT=invs2row,
                            rhs=b2bf[0:1, nn * 512:(nn + 1) * 512],
                            start=True, stop=False),
                            waits=(wv + [(sA, evs)]) if n == 0 else [],
                            inc=None)
                    for k in range(rr * 8, rr * 8 + 8):
                        pv = op("pe", lambda e, nn=n, kk=k: e.matmul(
                            out=psC[:, nn * 512:(nn + 1) * 512],
                            lhsT=hqT[:, kk * 128:(kk + 1) * 128],
                            rhs=W2qT[:, kk * 1024 + nn * 512:
                                     kk * 1024 + (nn + 1) * 512],
                            start=False, stop=(kk == 31)),
                            waits=([(sA, evs)] if (n == 0 and k == rr * 8)
                                   else []),
                            inc=(sP if (n == 1 and k == 31) else None))
            ms[("hqT", t)] = evs
            ms[("w2", t)] = pv

        f_front_dve(0)
        f_front_pe(0)
        if nt > 1:
            f_front_dve(1)
        emit_gelu(0, 0)
        emit_gelu(0, 1)
        emit_w1_quarter(0, 2)
        emit_gelu(0, 2)
        emit_w1_quarter(0, 3)
        emit_gelu(0, 3)
        f_fh(0)
        for t in range(nt):
            fence(gtile)
            gtile += 1
            f_round0(t)
            if t + 1 < nt:
                f_front_pe(t + 1)
            f_back_T0(t)
            f_rounds123(t)
            if t + 2 < nt:
                f_front_dve(t + 2)
            _fT(t, (1,))
            if t + 1 < nt:
                emit_gelu(t + 1, 0)
                emit_gelu(t + 1, 1)
                emit_w1_quarter(t + 1, 2)
            _fT(t, (2,))
            if t + 1 < nt:
                emit_gelu(t + 1, 2)
                emit_w1_quarter(t + 1, 3)
            _fT(t, (3,))
            if t + 1 < nt:
                emit_gelu(t + 1, 3)
                f_fh(t + 1)
            f_out(t)
            tile_end()

        op("syncq", None, waits=[(dO, dO.n)])
        _emit_all(nc, OPS, _ES, cut)

    return nc


def _emit_all(nc, OPS, _ES, cut=None):
    if cut is not None:
        OPS = OPS[:cut]
        tot = {}
        for (k, waits, fn, inc, amt, is_dma, ns) in OPS:
            if inc is not None:
                tot[inc.name] = tot.get(inc.name, 0) + amt
        OPS = [(k, [(s, min(v, tot.get(s.name, 0))) for (s, v) in waits],
                fn, inc, amt, is_dma, ns)
               for (k, waits, fn, inc, amt, is_dma, ns) in OPS]

    def mk(ek):
        zs = _ES.get(ek)

        def run(eng):
            lastw = {}
            zi = 0
            for (k, waits, fn, inc, amt, is_dma, ns) in OPS:
                if k != ek:
                    continue
                for (s, val) in waits:
                    if val > 0 and lastw.get(s.name, 0) < val:
                        eng.wait_ge(s.h, val)
                        lastw[s.name] = val
                if fn is None:
                    continue
                if zs is not None and zi > 0 and not ns:
                    eng.wait_ge(zs.h, zi)
                inst = fn(eng)
                if zs is not None and not is_dma:
                    zi += 1
                if inc is not None:
                    inst.then_inc(inc.h, amt)
        return run

    with nc.Block() as block:
        block.tensor(mk("pe"))
        block.vector(mk("dve"))
        block.scalar(mk("act"))
        block.gpsimd(mk("pool"))
        block.sync(mk("syncq"))


def _consts():
    identb = np.eye(128, dtype=ml_dtypes.bfloat16)
    identf = np.eye(128, dtype=np.float32)
    a = np.arange(128)[:, None]
    b = np.arange(128)[None, :]
    m_prev = np.where(b > a, 0.0, NEG).astype(np.float32)
    m_cur = np.where(b <= a, 0.0, NEG).astype(np.float32)
    normal = np.concatenate([m_prev, m_cur], axis=1)
    t0 = np.concatenate([np.full((128, 128), NEG, np.float32), m_cur], axis=1)
    masks = np.concatenate([normal, t0], axis=1).astype(ml_dtypes.bfloat16)
    return identb, identf, masks


_NC_CACHE = {}


def kernel(L0_attn_x, L0_attn_w_qkv, L0_attn_w_out, L0_ffn_W1, L0_ffn_b1,
           L0_ffn_W2, L0_ffn_b2):
    x = np.ascontiguousarray(np.asarray(L0_attn_x, dtype=np.float32))
    wqkv = np.ascontiguousarray(np.asarray(L0_attn_w_qkv, dtype=np.float32))
    wout = np.ascontiguousarray(np.asarray(L0_attn_w_out, dtype=np.float32))
    w1 = np.ascontiguousarray(np.asarray(L0_ffn_W1, dtype=np.float32))
    w2 = np.ascontiguousarray(np.asarray(L0_ffn_W2, dtype=np.float32))
    b1 = np.ascontiguousarray(np.asarray(L0_ffn_b1, dtype=np.float32))
    b2 = np.ascontiguousarray(np.asarray(L0_ffn_b2, dtype=np.float32))
    identb, identf, masks = _consts()

    nt = SEQ_PC * TPS
    if nt not in _NC_CACHE:
        _NC_CACHE[nt] = _build(nt)
    nc = _NC_CACHE[nt]

    in_maps = []
    for c in range(NCORES):
        xs = np.ascontiguousarray(
            x[c * SEQ_PC:(c + 1) * SEQ_PC].reshape(SEQ_PC * S, D))
        in_maps.append({
            "xc": xs, "wqkv": wqkv, "wout": wout, "w1": w1, "w2": w2,
            "b1": b1, "b2": b2, "identb": identb, "identf": identf,
            "masks": masks,
        })
    res = run_bass_kernel_spmd(nc, in_maps, core_ids=list(range(NCORES)))
    out = np.concatenate(
        [r["y"].reshape(SEQ_PC, S, D) for r in res.results], axis=0)
    return out


# version 81
# speedup vs baseline: 1.0778x; 1.0067x over previous
import numpy as np
import ml_dtypes
from contextlib import ExitStack

import concourse.bass as bass
import concourse.mybir as mybir
from concourse.bass_utils import run_bass_kernel_spmd

F32 = mybir.dt.float32
BF16 = mybir.dt.bfloat16
I32 = mybir.dt.int32
AF = mybir.ActivationFunctionType
ALU = mybir.AluOpType
AX = mybir.AxisListType

B, S, D = 16, 2048, 1024
NCORES = 8
SEQ_PC = B // NCORES
T = 128
TPS = S // T
DC = D // 128
HN, DH = 16, 64
HF = 4096
NEG = -1.0e9
RK = 640.5

_SB_BASE = 16640
_SB_TOP = 229376


class _Sem:
    __slots__ = ("name", "n", "h")

    def __init__(self, name):
        self.name = name
        self.n = 0
        self.h = None


def _off(ap):
    return ap.tensor.manual_sbuf_range[0]


def _build(nt, mode="full", cut=None):
    ntok = nt * T
    nc = bass.Bass("TRN2", target_bir_lowering=False, debug=False)

    xc = nc.dram_tensor("xc", [ntok, D], F32, kind="ExternalInput").ap()
    wqkv_d = nc.dram_tensor("wqkv", [D, 3 * D], F32, kind="ExternalInput").ap()
    wout_d = nc.dram_tensor("wout", [D, D], F32, kind="ExternalInput").ap()
    w1_d = nc.dram_tensor("w1", [HF, D], F32, kind="ExternalInput").ap()
    w2_d = nc.dram_tensor("w2", [D, HF], F32, kind="ExternalInput").ap()
    b1_d = nc.dram_tensor("b1", [HF], F32, kind="ExternalInput").ap()
    b2_d = nc.dram_tensor("b2", [D], F32, kind="ExternalInput").ap()
    identb_d = nc.dram_tensor("identb", [128, 128], BF16, kind="ExternalInput").ap()
    identf_d = nc.dram_tensor("identf", [128, 128], F32, kind="ExternalInput").ap()
    masks_d = nc.dram_tensor("masks", [128, 512], BF16, kind="ExternalInput").ap()
    y_d = nc.dram_tensor("y", [ntok, D], F32, kind="ExternalOutput").ap()
    x1_d = nc.dram_tensor("x1sp", [ntok, D], F32, kind="Internal").ap()
    w2t_d = nc.dram_tensor("w2td", [128, 32 * 1024], BF16, kind="Internal").ap()
    bbf_d = nc.dram_tensor("bbfd", [HF + D], BF16, kind="Internal").ap()

    cur = [_SB_BASE]

    def sb(name, p, n, dt, at=None):
        size = n * mybir.dt.size(dt)
        o = cur[0] if at is None else at
        o = (o + 31) & ~31
        h = nc._state.alloc_sbuf_tensor_at(name, [p, n], dt, offset=o)
        if at is None:
            cur[0] = o + size
        return h.ap()

    identb = sb("identb", 128, 128, BF16)
    identf = sb("identf", 128, 128, F32)
    masks = sb("masks", 128, 512, BF16)
    b1bf = sb("b1bf", 1, HF, BF16)
    b2bf = sb("b2bf", 1, D, BF16)
    cells = [sb("cells0", 128, 64, F32), sb("cells1", 128, 64, F32)]
    statsN = sb("statsN", 128, nt, F32)
    statsR = sb("statsR", 128, nt, F32)
    _c4 = sb("c4", 128, 5, F32)
    c_eps = _c4[:, 0:1]
    c_nh = _c4[:, 1:2]
    c_ph = _c4[:, 2:3]
    ones_col = _c4[:, 3:4]
    c_rk = _c4[:, 4:5]
    ones_row = sb("ones_row", 1, 128, F32)
    _b4 = sb("b4", 128, 4, F32)
    bc_r1 = _b4[:, 0:1]
    bc_ws1 = _b4[:, 1:2]
    bc_r2 = _b4[:, 2:3]
    bc_ws2 = _b4[:, 3:4]
    accw = sb("accw", 128, 8, F32)
    wsc = sb("wsc", 1, 8, F32)
    x1buf = [sb("x1b0", 128, D, F32), sb("x1b1", 128, D, F32)]
    outbuf = sb("outbuf", 128, D, F32)
    invs1row = sb("i1row", 1, 128, BF16)
    invs2row = sb("i2row", 1, 128, BF16)

    A0 = (cur[0] + 63) & ~63
    cur[0] = A0
    wqkv_sb = sb("wqkv_sb", 128, DC * 3 * D, BF16)
    wout_sb = sb("wout_sb", 128, DC * D, BF16)
    xin = [sb("xin0", 128, D, F32), sb("xin1", 128, D, F32)]
    xln = sb("xln", 128, D, BF16)
    xlnT = sb("xlnT", 128, D, BF16)
    q_sb = sb("q_sb", 128, D, BF16)
    k_sb = sb("k_sb", 128, D, BF16)
    qT = sb("qT", 64, 2048, BF16)
    kT_hist = sb("kT_hist", 64, 2 * 2048, BF16)
    v_hist = sb("v_hist", 128, 2 * D, BF16)
    probs = [sb("probs0", 128, 1024, BF16), sb("probs1", 128, 1024, BF16)]
    probsn = probs
    probsT_b = sb("probsT", 128, 1024, BF16)
    probsT = [probsT_b, probsT_b]
    attnout = sb("attnout", 128, 1024, BF16)
    wstage = [sb("wstage0", 128, 512, F32), sb("wstage1", 128, 512, F32),
              sb("wstage2", 128, 512, F32), sb("wstage3", 128, 512, F32)]
    sgnA = sb("sgnA", 128, 512, BF16)
    sgnB = sb("sgnB", 128, 512, BF16)
    wqch = [sb("wqch0", 128, 512, BF16), sb("wqch1", 128, 512, BF16)]
    w2st = [sb("w2st0", 128, 512, BF16), sb("w2st1", 128, 512, BF16)]
    A_END = cur[0]

    w1t_at = (229368 - DC * HF * 2) & ~63
    W1qT = sb("W1qT", 128, DC * HF, BF16, at=w1t_at)
    assert A_END <= w1t_at, (A_END, w1t_at)

    cur[0] = A0
    W2qT = sb("W2qT", 128, 32 * D, BF16)
    h_sb = sb("h_sb", 128, HF, F32)
    hq = sb("hq", 128, HF, BF16)
    hqT = sb("hqT", 128, HF, BF16)
    t1b = sb("t1a", 128, 1024, F32)
    tib = sb("tib", 128, 1024, I32)
    tfb = sb("tfb", 128, 1024, F32)
    x1c = sb("x1c", 128, D, F32)
    xq = sb("xq", 128, D, BF16)
    xqT = sb("xqT", 128, D, BF16)
    F_END = cur[0]
    assert F_END <= w1t_at, (F_END, w1t_at)

    wstg_big = [
        sb("wstgA", 128, 1536, F32, at=w1t_at),
        sb("wstgB", 128, 1536, F32, at=w1t_at + 1536 * 4),
    ]
    bstage = sb("bstage", 128, 32, F32, at=w1t_at + 1536 * 8)
    bstage_bf = sb("bstageb", 128, 32, BF16, at=w1t_at + 1536 * 8 + 128)

    with ExitStack() as ctx:
        psA = ctx.enter_context(nc.psum_tensor("psA", [128, 1024], F32)).ap()
        psB = ctx.enter_context(nc.psum_tensor("psB", [128, 1024], F32)).ap()
        psC = ctx.enter_context(nc.psum_tensor("psC", [128, 1024], F32)).ap()
        psD = ctx.enter_context(nc.psum_tensor("psD", [128, 1024], BF16)).ap()
        psF = ctx.enter_context(nc.psum_tensor("psF", [128, 512], F32)).ap()

        sP = _Sem("sP"); sV = _Sem("sV"); sA = _Sem("sA"); sG = _Sem("sG")
        dX = _Sem("dX"); dS = _Sem("dS"); dL = _Sem("dL"); dO = _Sem("dO")
        dC = _Sem("dC"); dY = _Sem("dY")
        dW = [_Sem("dW0"), _Sem("dW1")]
        dP = [_Sem("dP0"), _Sem("dP1"), _Sem("dP2"), _Sem("dP3")]
        dB = [_Sem("dB0"), _Sem("dB1")]
        SEMS = [sP, sV, sA, sG, dX, dS, dL, dO, dC, dY] + dW + dP + dB
        for s in SEMS:
            s.h = ctx.enter_context(nc.semaphore(s.name))

        OPS = []
        _ES = {"dve": sV, "act": sA, "pool": sG}

        def op(ek, fn, waits=(), inc=None, amt=1, is_dma=False,
               nosync=False):
            if fn is not None and not is_dma and ek in _ES:
                inc = _ES[ek]
                amt = 1
            OPS.append((ek, [w for w in waits if w is not None and w[1] > 0],
                        fn, inc, amt, is_dma, nosync))
            if inc is not None:
                inc.n += amt
                return inc.n
            return None

        def dma(ek, out_ap, in_ap, sem, waits=()):
            w = [(sem, sem.n)] + list(waits)
            return op(ek, (lambda e, o=out_ap, i=in_ap: e.dma_start(out=o, in_=i)),
                      w, sem, 16, is_dma=True)

        ms = {}

        def tickle(n=2):
            for _ in range(n):
                op("pe", lambda e: e.ldweights(identb))

        def tickle_train(sem, target, step=3, n=4, per=3):
            for k in range(n, 0, -1):
                v = target - k * step
                if v > 0:
                    for _ in range(per):
                        op("pe", lambda e: e.ldweights(identb),
                           waits=[(sem, v)])

        psum_free = {"A": None, "B": None, "C": None, "D": None, "F": None}
        psD2 = psF.bitcast(BF16)
        _stg = {"t": 0}

        def next_stage():
            _stg["t"] ^= 1
            return (psD, "D") if _stg["t"] else (psD2, "F")
        eng_end = {k: [0] for k in ("pe", "dve", "act", "pool")}
        ENG_SEM = {"pe": sP, "dve": sV, "act": sA, "pool": sG}

        def fence(gt):
            if gt < 2:
                return
            for ek in ("pe", "dve", "act", "pool"):
                waits = [(ENG_SEM[o], eng_end[o][gt - 1])
                         for o in ("pe", "dve", "act", "pool") if o != ek]
                op(ek, None, waits=waits)

        def tile_end():
            for ek in ("pe", "dve", "act", "pool"):
                eng_end[ek].append(ENG_SEM[ek].n)

        def _memset(ap_, val):
            return lambda e, a=ap_, v=val: e.memset(a, v)

        op("dve", _memset(c_eps, 1e-5))
        op("dve", _memset(c_nh, -0.5))
        op("dve", _memset(c_ph, 0.5))
        op("dve", _memset(c_rk, RK))
        op("dve", _memset(ones_col, 1.0))
        op("dve", _memset(ones_row, 1.0))
        op("dve", _memset(accw, 0.0))
        op("dve", _memset(kT_hist, 0.0))
        op("dve", _memset(v_hist, 0.0), inc=sV)
        ms["memset"] = sV.n

        dma("act", identb, identb_d, dC)
        dma("act", identf, identf_d, dC)
        dma("act", masks, masks_d, dC)
        ms["consts_ld"] = dC.n

        _w = dma("act", bstage, b1_d.rearrange("(p k) -> p k", p=128), dY)
        _a = op("act", lambda e: e.activation(out=bstage_bf, in_=bstage,
                                              func=AF.Copy),
                waits=[(dY, _w)], inc=sA)
        _b = dma("act", bbf_d[0:HF].rearrange("(p k) -> p k", p=128), bstage_bf,
                 dY, waits=[(sA, _a)])
        dma("act", b1bf, bbf_d[0:HF].rearrange("(a k) -> a k", a=1), dY,
            waits=[(dY, _b)])
        _w = dma("act", bstage[:, 0:8], b2_d.rearrange("(p k) -> p k", p=128),
                 dY, waits=[(dY, _b)])
        _a = op("act", lambda e: e.activation(out=bstage_bf[:, 0:8],
                                              in_=bstage[:, 0:8], func=AF.Copy),
                waits=[(dY, _w)], inc=sA)
        _b = dma("act", bbf_d[HF:HF + D].rearrange("(p k) -> p k", p=128),
                 bstage_bf[:, 0:8], dY, waits=[(sA, _a)])
        dma("act", b2bf, bbf_d[HF:HF + D].rearrange("(a k) -> a k", a=1), dY,
            waits=[(dY, _b)])
        ms["bias_ld"] = dY.n

        xin_ld = {}
        for t0 in range(min(2, nt)):
            xin_ld[t0] = dma("syncq", xin[t0 % 2], xc[t0 * T:(t0 + 1) * T, :], dX)

        wc_v, wc_a = 0, 0
        stg_rd = {0: None, 1: None}
        pieces = []
        for i in range(8):
            pieces.append((wqkv_d[i * 128:(i + 1) * 128, 0:1536],
                           wqkv_sb[:, i * 3 * D:i * 3 * D + 1536], 1536))
        for i in range(8):
            pieces.append((wqkv_d[i * 128:(i + 1) * 128, 1536:3072],
                           wqkv_sb[:, i * 3 * D + 1536:
                                   i * 3 * D + 3072], 1536))
        for i in range(8):
            pieces.append((wout_d[i * 128:(i + 1) * 128, :],
                           wout_sb[:, i * D:(i + 1) * D], 1024))
        for j, (src, dst, n) in enumerate(pieces):
            buf = wstg_big[j % 2]
            ld = dma("act", buf[:, 0:n], src, dW[j % 2], waits=[stg_rd[j % 2]])
            ek = "dve" if j % 2 == 0 else "act"
            sm = sV if ek == "dve" else sA
            if ek == "dve":
                fn = (lambda e, o=dst, i_=buf[:, 0:n]:
                      e.tensor_copy(out=o, in_=i_))
            else:
                fn = (lambda e, o=dst, i_=buf[:, 0:n]: e.copy(out=o, in_=i_))
            v = op(ek, fn, waits=[(dW[j % 2], ld)], inc=sm)
            stg_rd[j % 2] = (sm, v)
            if ek == "dve":
                wc_v = v
            else:
                wc_a = v
            if j == 7:
                ms["wcast_q"] = (wc_v, wc_a)
            elif j == 15:
                ms["wcast_kv"] = (wc_v, wc_a)
        ms["wcast_v"] = wc_v
        ms["wcast_a"] = wc_a

        if mode == "S":
            v = op("dve", lambda e: e.tensor_copy(out=outbuf, in_=wqkv_sb[:, 0:D]),
                   waits=[(sV, wc_v), (sA, wc_a), (dX, dX.n), (dY, ms["bias_ld"]),
                          (dC, ms["consts_ld"])])
            o = dma("pool", y_d[0:T, :], outbuf, dO, waits=[(sV, v)])
            op("syncq", None, waits=[(dO, dO.n)])
            _emit_all(nc, OPS, _ES, cut)
            return nc

        prep_list = ([("a1", i) for i in range(64)] +
                     [("a2", i) for i in range(64)] +
                     [("f1",), ("f2",)] +
                     [("q1", i) for i in range(64)] +
                     [("q2", i) for i in range(64)])
        if mode == "A":
            prep_list = []
        prep_by_tile = [[] for _ in range(nt)]
        for j, task in enumerate(prep_list):
            prep_by_tile[min(nt - 1, j * nt // len(prep_list))].append(task)

        pst = {
            "stg_rd": {0: None, 1: None, 2: None, 3: None},
            "stg_ld": {},
            "acc": {"a1": 0, "a2": 0},
            "sgn_rd": None,
            "wq_rd": {0: None, 1: None},
            "w2st_rd": {0: None, 1: None},
            "ndma": 0,
            "acc_rd": None,
            "cursor": 0,
            "pend": [],
        }

        def prep_src(kind, i):
            if kind in ("a1", "q1"):
                return w1_d[(i // 2) * 128:(i // 2 + 1) * 128,
                            (i % 2) * 512:(i % 2) * 512 + 512]
            return w2_d[(i // 8) * 128:(i // 8 + 1) * 128,
                        (i % 8) * 512:(i % 8) * 512 + 512]

        def prep_issue_next_dma():
            while pst["cursor"] < len(prep_list):
                task = prep_list[pst["cursor"]]
                pst["cursor"] += 1
                if task[0] in ("f1", "f2"):
                    continue
                slot = pst["ndma"] % 4
                pst["ndma"] += 1
                ld = dma("syncq", wstage[slot], prep_src(task[0], task[1]),
                         dP[slot], waits=[pst["stg_rd"][slot]])
                pst["stg_ld"][task] = (slot, ld)
                return

        def prep_fin(grp):
            kind = "a1" if grp == 0 else "a2"
            acc_slot = grp * 2 + pst["acc"][kind] % 2
            nelem = float(HF * D)
            pv = op("pe", lambda e, a=accw[:, acc_slot:acc_slot + 1]:
                    e.matmul(out=psF[0:1, 0:1], lhsT=a, rhs=ones_col,
                             start=True, stop=True),
                    waits=[(sV, ms["abs_%s_last" % kind]), psum_free["F"],
                           (sV, ms["memset"])], inc=sP)
            op("dve", lambda e: e.tensor_copy(out=wsc[0:1, 0:1],
                                              in_=psF[0:1, 0:1]),
               waits=[(sP, pv)])
            op("dve", lambda e, n=nelem: e.tensor_scalar(
                out=wsc[0:1, 1:2], in0=wsc[0:1, 0:1],
                scalar1=1.0 / n, scalar2=1e-5, op0=ALU.mult, op1=ALU.max))
            vr = op("dve", lambda e: e.reciprocal(out=wsc[0:1, 2:3],
                                                  in_=wsc[0:1, 1:2]), inc=sV)
            pv = op("pe", lambda e: e.matmul(out=psF[:, 0:1], lhsT=ones_row,
                                             rhs=wsc[0:1, 2:3],
                                             start=True, stop=True),
                    waits=[(sV, vr)], inc=sP)
            pv = op("pe", lambda e: e.matmul(out=psF[:, 1:2], lhsT=ones_row,
                                             rhs=wsc[0:1, 1:2],
                                             start=True, stop=True), inc=sP)
            dst_r = bc_r1 if grp == 0 else bc_r2
            dst_w = bc_ws1 if grp == 0 else bc_ws2
            op("dve", lambda e, d=dst_r: e.tensor_copy(out=d, in_=psF[:, 0:1]),
               waits=[(sP, pv)])
            v = op("dve", lambda e, d=dst_w: e.tensor_scalar(
                out=d, in0=psF[:, 1:2], scalar1=1.0 / 127.0, scalar2=None,
                op0=ALU.mult), inc=sV)
            psum_free["F"] = (sV, v)
            ms["wsc_%d" % grp] = v

        def prep_emit(task):
            kind = task[0]
            if kind == "f1":
                prep_fin(0); return
            if kind == "f2":
                prep_fin(1); return
            i = task[1]
            slot, ld = pst["stg_ld"][task]
            if kind in ("a1", "a2"):
                cnt = pst["acc"][kind]
                g = 0 if kind == "a1" else 1
                src_s = g * 2 + cnt % 2
                dst_s = g * 2 + (cnt + 1) % 2
                op("dve", lambda e, s=wstage[slot]: e.tensor_reduce(
                    out=accw[:, 4:5], in_=s, axis=AX.X, op=ALU.add,
                    apply_absolute_value=True),
                    waits=[(dP[slot], ld), (sV, ms["memset"]),
                           pst["acc_rd"]], nosync=True)
                v = op("dve", lambda e, a=accw[:, src_s:src_s + 1],
                       b=accw[:, dst_s:dst_s + 1]:
                       e.tensor_tensor(out=b, in0=a, in1=accw[:, 4:5],
                                       op=ALU.add), inc=sV)
                pst["acc_rd"] = (sV, v)
                pst["acc"][kind] = cnt + 1
                pst["stg_rd"][slot] = (sV, v)
                ms["abs_%s_last" % kind] = v
            else:
                g = 0 if kind == "q1" else 1
                wqs = i % 2
                if any(p[2] == wqs for p in pst["pend"]):
                    prep_flushT()
                bc = bc_r1 if g == 0 else bc_r2
                op("dve", lambda e, s=wstage[slot], b=bc: e.tensor_scalar(
                    out=sgnA, in0=s, scalar1=b, scalar2=0.5,
                    op0=ALU.mult, op1=ALU.is_ge),
                    waits=[(dP[slot], ld), (sV, ms["wsc_%d" % g]),
                           pst["sgn_rd"]], nosync=True)
                va = op("dve", lambda e, s=wstage[slot], b=bc: e.tensor_scalar(
                    out=sgnB, in0=s, scalar1=b, scalar2=-0.5,
                    op0=ALU.mult, op1=ALU.is_ge), inc=sV, nosync=True)
                pst["stg_rd"][slot] = (sV, va)
                v = op("dve", lambda e, o=wqch[wqs]: e.scalar_tensor_tensor(
                    out=o, in0=sgnA, scalar=-1.0, in1=sgnB,
                    op0=ALU.add, op1=ALU.add),
                    waits=[pst["wq_rd"][wqs]], inc=sV)
                pst["sgn_rd"] = (sV, v)
                pst["pend"].append((kind, i, wqs, v))
            prep_issue_next_dma()

        def prep_flushT():
            for kind, i, wqs, vq in pst["pend"]:
                sp, key = next_stage()
                pv = None
                for j in range(4):
                    pv = op("pe", lambda e, jj=j, s=wqch[wqs], p=sp: e.transpose(
                        out=p[:, jj * 128:(jj + 1) * 128],
                        in_=s[:, jj * 128:(jj + 1) * 128], identity=identb),
                        waits=([(sV, vq), psum_free[key],
                                (dC, ms["consts_ld"])]
                               if j == 0 else []),
                        inc=(sP if j == 3 else None))
                pst["wq_rd"][wqs] = (sP, pv)
                if kind == "q1":
                    hb, half = i // 2, i % 2
                    dst = W1qT.rearrange("p (c h) -> p c h", c=DC)[
                        :, half * 4:half * 4 + 4, hb * 128:(hb + 1) * 128]
                    v = op("dve", lambda e, d=dst, p=sp: e.tensor_copy(
                        out=d,
                        in_=p[:, 0:512].rearrange("p (c h) -> p c h", c=4)),
                        waits=[(sP, pv), (sV, ms["wcast_v"]),
                               (sA, ms["wcast_a"])], inc=sV, nosync=True)
                    psum_free[key] = (sV, v)
                else:
                    db, s8 = i // 8, i % 8
                    st = i % 2
                    v = op("dve", lambda e, o=w2st[st], p=sp: e.tensor_copy(
                        out=o, in_=p[:, 0:512]),
                        waits=[(sP, pv), pst["w2st_rd"][st]], inc=sV,
                        nosync=True)
                    psum_free[key] = (sV, v)
                    dst = w2t_d.rearrange("p (c d) -> p c d", c=32)[
                        :, s8 * 4:s8 * 4 + 4, db * 128:(db + 1) * 128]
                    b = dma("syncq", dst,
                            w2st[st].rearrange("p (c d) -> p c d", c=4),
                            dB[st], waits=[(sV, v)])
                    pst["w2st_rd"][st] = (dB[st], b)
            pst["pend"] = []

        for _ in range(4):
            prep_issue_next_dma()

        spill_ms = {}
        (CEL_SUM, CEL_SSQ, CEL_M, CEL_MSQ, CEL_VAR, CEL_LNV, CEL_RSTD,
         CEL_NMR) = range(8)
        CEL_NMX, CEL_SUMS, CEL_RCP = 8, 24, 40
        CEL2 = 56

        def a_ln1_stats(t):
            par = t % 2
            cl = cells[t % 2]
            va = op("act", lambda e, x=xin[par], c=cl: e.activation(
                out=outbuf, in_=x, func=AF.Square,
                accum_out=c[:, CEL_SSQ:CEL_SSQ + 1]),
                waits=[(dX, xin_ld[t])], inc=sA)
            op("dve", lambda e, x=xin[par], c=cl: e.tensor_reduce(
                out=c[:, CEL_SUM:CEL_SUM + 1], in_=x, axis=AX.X, op=ALU.add),
                waits=[(dX, xin_ld[t])], nosync=True)
            vm = op("dve", lambda e, c=cl: e.tensor_scalar(
                out=c[:, CEL_M:CEL_M + 1], in0=c[:, CEL_SUM:CEL_SUM + 1],
                scalar1=1.0 / D, scalar2=None, op0=ALU.mult), inc=sV)
            ms[("ln1m", t)] = vm
            op("dve", lambda e, c=cl: e.tensor_tensor(
                out=c[:, CEL_MSQ:CEL_MSQ + 1], in0=c[:, CEL_M:CEL_M + 1],
                in1=c[:, CEL_M:CEL_M + 1], op=ALU.mult))
            v = op("dve", lambda e, c=cl: e.scalar_tensor_tensor(
                out=c[:, CEL_VAR:CEL_VAR + 1], in0=c[:, CEL_SSQ:CEL_SSQ + 1],
                scalar=1.0 / D, in1=c[:, CEL_MSQ:CEL_MSQ + 1],
                op0=ALU.mult, op1=ALU.subtract), waits=[(sA, va)], inc=sV)
            op("act", lambda e, c=cl: e.activation(
                out=c[:, CEL_LNV:CEL_LNV + 1], in_=c[:, CEL_VAR:CEL_VAR + 1],
                func=AF.Ln, bias=c_eps),
                waits=[(sV, v), (sV, ms["memset"])])
            va = op("act", lambda e, c=cl: e.activation(
                out=c[:, CEL_RSTD:CEL_RSTD + 1], in_=c[:, CEL_LNV:CEL_LNV + 1],
                func=AF.Exp, scale=-0.5), inc=sA)
            ms[("rstd", t)] = va

        def a_ln1_apply(t):
            par = t % 2
            cl = cells[t % 2]
            v = op("dve", lambda e, c=cl: e.tensor_scalar(
                out=c[:, CEL_NMR:CEL_NMR + 1], in0=c[:, CEL_M:CEL_M + 1],
                scalar1=c[:, CEL_RSTD:CEL_RSTD + 1], scalar2=-1.0,
                op0=ALU.mult, op1=ALU.mult),
                waits=[(sA, ms[("rstd", t)]), (sV, ms[("ln1m", t)])],
                inc=sV, nosync=True)
            va = op("act", lambda e, x=xin[par], c=cl: e.activation(
                out=xln, in_=x, func=AF.Identity,
                bias=c[:, CEL_NMR:CEL_NMR + 1],
                scale=c[:, CEL_RSTD:CEL_RSTD + 1]),
                waits=[(sV, v), (sA, ms.get(("xlnT", t - 1), 0)),
                       (sA, ms[("rstd", t)])], inc=sA, nosync=True)
            ms[("xln", t)] = va

        def a_xlnT(t):
            sp, key = next_stage()
            pv = None
            for c in range(DC):
                pv = op("pe", lambda e, cc=c, p=sp: e.transpose(
                    out=p[:, cc * 128:(cc + 1) * 128],
                    in_=xln[:, cc * 128:(cc + 1) * 128], identity=identb),
                    waits=([(sA, ms[("xln", t)]), psum_free[key],
                            (dC, ms["consts_ld"])] if c == 0 else []),
                    inc=(sP if c == DC - 1 else None))
            v = op("act", lambda e, p=sp: e.copy(out=xlnT, in_=p),
                   waits=[(sP, pv), (sP, ms.get(("qkv", t - 1, 2), 0))],
                   inc=sA, nosync=True)
            psum_free[key] = (sA, v)
            ms[("xlnT", t)] = v

        def a_qkv(t, grp):
            par = t % 2
            psX, pfk = ((psA, "A"), (psB, "B"), (psC, "C"))[grp]
            wq = ms["wcast_q"] if grp == 0 else ms["wcast_kv"]
            w0 = [(sA, ms[("xlnT", t)]), psum_free[pfk],
                  (sV, wq[0]), (sA, wq[1])]
            pv = None
            for c in range(DC):
                for n in range(2):
                    pv = op("pe", lambda e, cc=c, nn=n, g=grp, p=psX:
                            e.matmul(
                        out=p[:, nn * 512:(nn + 1) * 512],
                        lhsT=xlnT[:, cc * 128:(cc + 1) * 128],
                        rhs=wqkv_sb[:, cc * 3 * D + g * D + nn * 512:
                                    cc * 3 * D + g * D + (nn + 1) * 512],
                        start=(cc == 0), stop=(cc == DC - 1)),
                        waits=(w0 if (c == 0 and n == 0) else []),
                        inc=(sP if (c == DC - 1 and n == 1) else None))
            ms[("qkv", t, grp)] = pv
            if grp == 0:
                va = op("act", lambda e: e.activation(out=q_sb, in_=psA,
                                                      func=AF.Copy,
                                                      scale=0.125),
                        waits=[(sP, pv), (sA, ms.get(("qT", t - 1), 0))],
                        inc=sA, nosync=True)
                ms[("q_sb", t)] = va
                psum_free["A"] = (sA, va)
            elif grp == 1:
                v = op("dve", lambda e: e.tensor_copy(out=k_sb, in_=psB),
                       waits=[(sP, pv), (sA, ms.get(("kT", t - 1), 0))],
                       inc=sV, nosync=True)
                ms[("k_sb", t)] = v
                psum_free["B"] = (sV, v)
            else:
                v = op("dve", lambda e, s=par: e.tensor_copy(
                    out=v_hist[:, s * D:(s + 1) * D], in_=psC),
                    waits=[(sP, pv), (sP, ms.get(("av", t - 1, 3), 0))],
                    inc=sV, nosync=True)
                ms[("v_sb", t)] = v
                psum_free["C"] = (sV, v)

        def a_qkT(t):
            par = t % 2
            for src_name, buf, wkey in (("q", q_sb, ("q_sb", t)),
                                        ("k", k_sb, ("k_sb", t))):
                wsem = sA if src_name == "q" else sV
                for r in range(2):
                    sp, key = next_stage()
                    pv = None
                    for hh in range(8):
                        h = r * 8 + hh
                        pv = op("pe", lambda e, b=buf, h_=h, hh_=hh, p=sp:
                                e.transpose(
                            out=p[0:64, hh_ * 128:(hh_ + 1) * 128],
                            in_=b[:, h_ * 64:(h_ + 1) * 64], identity=identb),
                            waits=([(wsem, ms[wkey]), psum_free[key]]
                                   if hh == 0 else []),
                            inc=(sP if hh == 7 else None))
                    if src_name == "q":
                        dst = qT[0:64, r * 1024:(r + 1) * 1024]
                    else:
                        dst = kT_hist[0:64, par * 2048 + r * 1024:
                                      par * 2048 + (r + 1) * 1024]
                    v = op("act", lambda e, d=dst, p=sp: e.copy(
                        out=d, in_=p[0:64, 0:1024]),
                        waits=[(sP, pv), (sP, ms.get(("scores", t - 1, 3), 0))],
                        inc=sA, nosync=True)
                    psum_free[key] = (sA, v)
                ms[("qT", t) if src_name == "q" else ("kT", t)] = v

        def a_scores(t, g):
            par, prv = t % 2, 1 - (t % 2)
            tvar = 1 if (t % TPS == 0) else 0
            psX = psA if g % 2 == 0 else psB
            pfk = "A" if g % 2 == 0 else "B"
            w0 = [(sA, ms[("qT", t)]), (sA, ms[("kT", t)]),
                  psum_free[pfk], (dC, ms["consts_ld"])]
            pv = None
            first = True
            for half in range(2):
                wh = w0
                for hh in (half * 2, half * 2 + 1):
                    h = 4 * g + hh
                    for kt, slot in ((0, prv), (1, par)):
                        pv = op("pe", lambda e, p=psX, hh_=hh, kt_=kt,
                                h_=h, sl=slot: e.matmul(
                            out=p[:, hh_ * 256 + kt_ * 128:
                                  hh_ * 256 + (kt_ + 1) * 128],
                            lhsT=qT[0:64, h_ * 128:(h_ + 1) * 128],
                            rhs=kT_hist[0:64, sl * 2048 + h_ * 128:
                                        sl * 2048 + (h_ + 1) * 128],
                            start=(hh_ % 2 == 0 and kt_ == 0), stop=False),
                            waits=(wh if first else []), inc=None)
                        first = False
                for hh in (half * 2, half * 2 + 1):
                    last = (hh == half * 2 + 1)
                    pv = op("pe", lambda e, p=psX, hh_=hh, mv=tvar, l=last:
                            e.matmul(
                                out=p[:, hh_ * 256:(hh_ + 1) * 256],
                                lhsT=identb,
                                rhs=masks[:, mv * 256:(mv + 1) * 256],
                                start=False, stop=l),
                            inc=(sP if last else None))
                if half == 0:
                    ms[("sc_h0", t, g)] = pv
            ms[("scores", t, g)] = pv

        def a_sm_max(t, g):
            cl = cells[t % 2]
            psX = psA if g % 2 == 0 else psB
            v0 = op("dve", lambda e, p=psX, c=cl, g_=g: e.tensor_reduce(
                out=c[:, CEL_NMX + 4 * g_:CEL_NMX + 4 * g_ + 2],
                in_=p[:, 0:512].rearrange("p (h k) -> p h k", h=2),
                axis=AX.X, op=ALU.max, negate=True),
                waits=[(sP, ms[("sc_h0", t, g)])], inc=sV, nosync=True)
            v = op("dve", lambda e, p=psX, c=cl, g_=g: e.tensor_reduce(
                out=c[:, CEL_NMX + 4 * g_ + 2:CEL_NMX + 4 * g_ + 4],
                in_=p[:, 512:1024].rearrange("p (h k) -> p h k", h=2),
                axis=AX.X, op=ALU.max, negate=True),
                waits=[(sP, ms[("scores", t, g)])], inc=sV, nosync=True)
            ms[("nmx0", t, g)] = v0
            ms[("nmx", t, g)] = v

        def a_sm_exp(t, g):
            cl = cells[t % 2]
            psX = psA if g % 2 == 0 else psB
            pfk = "A" if g % 2 == 0 else "B"
            va = None
            evl = []
            for hh in range(4):
                wv = [(sV, ms[("nmx0", t, g)] if hh < 2
                       else ms[("nmx", t, g)])]
                if g >= 2:
                    wv.append((sP, ms.get(("pT_mm", t, g - 2), 0)))
                else:
                    wv.append((sP, ms.get(("pT_mm", t - 1, g + 2), 0)))
                va = op("act", lambda e, p=psX, c=cl, g_=g, hh_=hh:
                        e.activation(
                    out=probs[g_ % 2][:, hh_ * 256:(hh_ + 1) * 256],
                    in_=p[:, hh_ * 256:(hh_ + 1) * 256], func=AF.Exp,
                    bias=c[:, CEL_NMX + 4 * g_ + hh_:
                           CEL_NMX + 4 * g_ + hh_ + 1],
                    accum_out=c[:, CEL_SUMS + 4 * g_ + hh_:
                                CEL_SUMS + 4 * g_ + hh_ + 1]),
                    waits=(wv if hh in (0, 2) else []),
                    inc=sA, nosync=(hh > 0))
                evl.append(va)
            ms[("exp", t, g)] = va
            ms[("expv", t, g)] = evl
            psum_free[pfk] = (sA, va)

        def a_sm_norm(t, g):
            cl = cells[t % 2]
            ev = ms[("expv", t, g)]
            r0 = op("dve", lambda e, c=cl, g_=g: e.reciprocal(
                out=c[:, CEL_RCP + 4 * g_:CEL_RCP + 4 * g_ + 2],
                in_=c[:, CEL_SUMS + 4 * g_:CEL_SUMS + 4 * g_ + 2]),
                waits=[(sA, ev[1])], inc=sV, nosync=True)
            rc = r0
            v2 = None
            for hh in range(4):
                if hh == 2:
                    rc = op("dve", lambda e, c=cl, g_=g: e.reciprocal(
                        out=c[:, CEL_RCP + 4 * g_ + 2:CEL_RCP + 4 * g_ + 4],
                        in_=c[:, CEL_SUMS + 4 * g_ + 2:
                               CEL_SUMS + 4 * g_ + 4]),
                        waits=[(sA, ev[3])], inc=sV, nosync=True)
                v2 = op("dve", lambda e, c=cl, g_=g, hh_=hh: e.tensor_scalar(
                    out=probsn[g_ % 2][:, hh_ * 256:(hh_ + 1) * 256],
                    in0=probs[g_ % 2][:, hh_ * 256:(hh_ + 1) * 256],
                    scalar1=c[:, CEL_RCP + 4 * g_ + hh_:
                              CEL_RCP + 4 * g_ + hh_ + 1],
                    scalar2=None, op0=ALU.mult),
                    waits=[(sV, rc)], inc=sV, nosync=(hh > 0))
                if hh == 1:
                    ms[("pn0", t, g)] = v2
            ms[("pn", t, g)] = v2

        def a_pt(t, g):
            sp, key = next_stage()
            pv = None
            for j in range(8):
                if j == 0:
                    w = [(sV, ms[("pn0", t, g)]), psum_free[key]]
                elif j == 4:
                    w = [(sV, ms[("pn", t, g)])]
                else:
                    w = []
                pv = op("pe", lambda e, jj=j, g_=g, p=sp: e.transpose(
                    out=p[:, jj * 128:(jj + 1) * 128],
                    in_=probsn[g_ % 2][:, jj * 128:(jj + 1) * 128],
                    identity=identb),
                    waits=w, inc=(sP if j == 7 else None))
            ms[("pT_mm", t, g)] = pv
            wv = [(sP, pv)]
            if g >= 1:
                wv.append((sP, ms.get(("av", t, g - 1), 0)))
            else:
                wv.append((sP, ms.get(("av", t - 1, 3), 0)))
            v = op("act", lambda e, g_=g, p=sp: e.activation(
                out=probsT[g_ % 2], in_=p, func=AF.Copy), waits=wv, inc=sA,
                nosync=True)
            psum_free[key] = (sA, v)
            ms[("pT", t, g)] = v

        def a_av(t, g):
            par, prv = t % 2, 1 - (t % 2)
            pv = None
            for hh in range(4):
                h = 4 * g + hh
                bp = 64 * (h % 2)
                p = h // 2
                for kt, slot in ((0, prv), (1, par)):
                    wv = [(sA, ms[("pT", t, g)]), (sV, ms[("v_sb", t)])]
                    if g == 0 and hh == 0 and kt == 0:
                        wv.append(psum_free["C"])
                    pv = op("pe", lambda e, hh_=hh, h_=h, bp_=bp, pp=p,
                            kt_=kt, sl=slot, g_=g: e.matmul(
                        out=psC[bp_:bp_ + 64, pp * 128:(pp + 1) * 128],
                        lhsT=v_hist[:, sl * D + h_ * 64:
                                    sl * D + h_ * 64 + 64],
                        rhs=probsT[g_ % 2][:, (hh_ * 2 + kt_) * 128:
                                           (hh_ * 2 + kt_ + 1) * 128],
                        start=(kt_ == 0), stop=(kt_ == 1),
                        tile_position=(0, bp_)),
                        waits=(wv if kt == 0 else []),
                        inc=(sP if (hh == 3 and kt == 1) else None))
            ms[("av", t, g)] = pv

        def a_outproj(t):
            v = op("dve", lambda e: e.tensor_copy(out=attnout, in_=psC),
                   waits=[(sP, ms[("av", t, 3)]),
                          (sP, ms.get(("outproj", t - 1), 0))], inc=sV,
                   nosync=True)
            ms[("attnout", t)] = v
            psum_free["C"] = (sV, v)
            pv = None
            for n in range(2):
                for c in range(DC):
                    wv = [(sV, ms[("attnout", t)]), psum_free["C"],
                          (sV, ms["wcast_v"]), (sA, ms["wcast_a"])]
                    pv = op("pe", lambda e, cc=c, nn=n: e.matmul(
                        out=psC[:, nn * 512:(nn + 1) * 512],
                        lhsT=attnout[:, cc * 128:(cc + 1) * 128],
                        rhs=wout_sb[:, cc * D + nn * 512:cc * D + (nn + 1) * 512],
                        start=(cc == 0), stop=(cc == DC - 1)),
                        waits=(wv if (n == 0 and c == 0) else []),
                        inc=(sP if (n == 1 and c == DC - 1) else None))
            ms[("outproj", t)] = pv

        def a_res(t):
            par = t % 2
            v = op("dve", lambda e, x=xin[par], o=x1buf[par]: e.tensor_tensor(
                out=o, in0=psC, in1=x, op=ALU.add),
                waits=[(sP, ms[("outproj", t)]), (dS, spill_ms.get(t - 2, 0))],
                inc=sV, nosync=True)
            ms[("res", t)] = v
            psum_free["C"] = (sV, v)
            spill_ms[t] = dma("syncq", x1_d[t * T:(t + 1) * T, :], x1buf[par],
                              dS, waits=[(sV, v)])

        def a_ln2_acc(t):
            par = t % 2
            cl = cells[t % 2]
            va = op("act", lambda e, o=x1buf[par], c=cl: e.activation(
                out=outbuf, in_=o, func=AF.Square,
                accum_out=c[:, CEL2:CEL2 + 1]),
                waits=[(sV, ms[("res", t)])], inc=sA)
            va = op("act", lambda e, o=x1buf[par], c=cl: e.activation(
                out=outbuf, in_=o, func=AF.Identity,
                accum_out=c[:, CEL2 + 1:CEL2 + 2]), inc=sA)
            ms[("ln2a", t)] = va

        def a_ln2_fin(t):
            cl = cells[t % 2]
            op("dve", lambda e, c=cl, t_=t: e.tensor_scalar(
                out=statsN[:, t_:t_ + 1], in0=c[:, CEL2 + 1:CEL2 + 2],
                scalar1=-1.0 / D, scalar2=None, op0=ALU.mult),
                waits=[(sA, ms[("ln2a", t)])], nosync=True)
            op("dve", lambda e, c=cl, t_=t: e.tensor_tensor(
                out=c[:, CEL2 + 2:CEL2 + 3], in0=statsN[:, t_:t_ + 1],
                in1=statsN[:, t_:t_ + 1], op=ALU.mult))
            v = op("dve", lambda e, c=cl: e.scalar_tensor_tensor(
                out=c[:, CEL2 + 3:CEL2 + 4], in0=c[:, CEL2:CEL2 + 1],
                scalar=1.0 / D, in1=c[:, CEL2 + 2:CEL2 + 3],
                op0=ALU.mult, op1=ALU.subtract), inc=sV)
            op("act", lambda e, c=cl: e.activation(
                out=c[:, CEL2 + 4:CEL2 + 5], in_=c[:, CEL2 + 3:CEL2 + 4],
                func=AF.Ln, bias=c_eps), waits=[(sV, v)])
            va = op("act", lambda e, c=cl, t_=t: e.activation(
                out=statsR[:, t_:t_ + 1], in_=c[:, CEL2 + 4:CEL2 + 5],
                func=AF.Exp, scale=-0.5), inc=sA)
            ms[("ln2", t)] = va

        gtile = 0
        a_ln1_stats(0)
        a_ln1_apply(0)
        a_xlnT(0)
        a_qkv(0, 0)
        a_qkv(0, 1)
        a_qkv(0, 2)
        for t in range(nt):
            fence(gtile)
            gtile += 1
            ptasks = list(prep_by_tile[t])

            def prep_drip(k):
                for _ in range(k):
                    if ptasks:
                        prep_emit(ptasks.pop(0))

            if t + 1 < nt:
                a_ln1_stats(t + 1)
            if t == 0:
                a_qkT(0)
            a_scores(t, 0)
            a_sm_max(t, 0)
            a_sm_exp(t, 0)
            if t > 0:
                a_ln2_fin(t - 1)
            a_scores(t, 1)
            a_sm_max(t, 1)
            a_sm_exp(t, 1)
            a_sm_norm(t, 0)
            a_scores(t, 2)
            a_sm_max(t, 2)
            prep_drip(2)
            a_pt(t, 0)
            a_av(t, 0)
            prep_flushT()
            a_sm_exp(t, 2)
            a_sm_norm(t, 1)
            if t + 1 < nt:
                a_ln1_apply(t + 1)
            a_scores(t, 3)
            a_sm_max(t, 3)
            prep_drip(2)
            a_pt(t, 1)
            a_av(t, 1)
            prep_flushT()
            a_sm_exp(t, 3)
            a_sm_norm(t, 2)
            if t + 1 < nt:
                a_xlnT(t + 1)
            prep_drip(2)
            a_pt(t, 2)
            a_av(t, 2)
            prep_flushT()
            if t + 1 < nt:
                a_qkv(t + 1, 0)
            a_sm_norm(t, 3)
            a_pt(t, 3)
            a_av(t, 3)
            prep_drip(2)
            if t + 1 < nt:
                a_qkv(t + 1, 1)
                a_qkT(t + 1)
            a_outproj(t)
            a_res(t)
            if t + 1 < nt:
                a_qkv(t + 1, 2)
            a_ln2_acc(t)
            if t + 2 < nt:
                xin_ld[t + 2] = dma(
                    "syncq", xin[t % 2], xc[(t + 2) * T:(t + 3) * T, :],
                    dX, waits=[(sV, ms[("res", t)]), (sA, ms[("xln", t)])])
            prep_drip(len(ptasks))
            prep_flushT()
            tile_end()
        a_ln2_fin(nt - 1)

        a_end = {ek: ENG_SEM[ek].n for ek in ENG_SEM}
        for ek in ("pe", "dve", "act", "pool", "syncq"):
            waits = [(ENG_SEM[o], a_end[o]) for o in ENG_SEM if o != ek]
            for s in (dS, dB[0], dB[1], dP[0], dP[1], dW[0], dW[1], dC, dY, dX):
                waits.append((s, s.n))
            op(ek, None, waits=waits)

        w2qt_lds = []
        for j in range(4):
            w2qt_lds.append(dma(
                "act", W2qT[:, j * 8192:(j + 1) * 8192],
                w2t_d[:, j * 8192:(j + 1) * 8192], dW[j % 2],
                waits=([(dB[0], dB[0].n), (dB[1], dB[1].n)] if j < 2 else [])))
        ms["w2qt0"] = w2qt_lds[2]
        ms["w2qt1"] = w2qt_lds[3]

        x1_ld = {}
        out_ms = {}
        FQ = 0
        FH = 12
        cbb = tib.bitcast(F32)

        def x1s(t):
            return (x1buf[0], x1buf[1], x1c)[t % 3]

        def emit_round(src_ap, dst_ap, scale_cell, bias_cell, bias_imm, nsub,
                       first_wait, war_wait, r0=0):
            subs = []
            prev_cb = 0
            for r in range(r0, r0 + nsub):
                wv = (list(first_wait) + list(war_wait)) if r == r0 else []
                sc2 = bias_cell if bias_cell is not None else bias_imm
                op("dve", lambda e, rr=r, s=src_ap, sc=scale_cell,
                   b=sc2: e.tensor_scalar(
                    out=t1b, in0=s[:, rr * 1024:(rr + 1) * 1024],
                    scalar1=sc, scalar2=b, op0=ALU.mult, op1=ALU.add),
                    waits=wv + [(sV, prev_cb)], nosync=True)
                op("dve", lambda e: e.tensor_copy(out=tib, in_=t1b))
                op("dve", lambda e: e.tensor_copy(out=tfb, in_=tib))
                prev_cb = op("dve", lambda e: e.tensor_tensor(
                    out=cbb, in0=t1b, in1=tfb, op=ALU.is_ge), inc=sV)
                rl = op("dve", lambda e, rr=r, d=dst_ap: e.scalar_tensor_tensor(
                    out=d[:, rr * 1024:(rr + 1) * 1024], in0=tfb,
                    scalar=-641.0, in1=cbb, op0=ALU.add, op1=ALU.add), inc=sV)
                subs.append(rl)
            return subs

        for t in range(min(3, nt)):
            x1_ld[t] = dma("syncq", x1s(t), x1_d[t * T:(t + 1) * T, :],
                           dL, waits=[(dS, dS.n)])

        def emit_w1_quarter(t, j):
            cl = cells[t % 2]
            psX = psA if j % 2 == 0 else psB
            pfk = "A" if j % 2 == 0 else "B"
            wv = [(sA, ms[("xqT", t)]), (sA, ms[("i1r", t)]),
                  psum_free[pfk], (dY, ms["bias_ld"])]
            pv = None
            for n in range(2):
                pv = op("pe", lambda e, jj=j, nn=n, p=psX: e.matmul(
                    out=p[:, nn * 512:(nn + 1) * 512], lhsT=invs1row,
                    rhs=b1bf[0:1, jj * 1024 + nn * 512:
                             jj * 1024 + (nn + 1) * 512],
                    start=True, stop=False),
                    waits=(wv if n == 0 else []), inc=None)
                for c in range(DC):
                    pv = op("pe", lambda e, jj=j, nn=n, cc=c, p=psX:
                            e.matmul(
                        out=p[:, nn * 512:(nn + 1) * 512],
                        lhsT=xqT[:, cc * 128:(cc + 1) * 128],
                        rhs=W1qT[:, cc * HF + jj * 1024 + nn * 512:
                                 cc * HF + jj * 1024 + (nn + 1) * 512],
                        start=False, stop=(cc == DC - 1)),
                        inc=(sP if (n == 1 and c == DC - 1) else None))
            ms[("w1", t, j)] = pv

        def _hqs(t, j):
            l = ms.get(("hqsub", t))
            return 0 if l is None else l[j]

        def emit_gelu(t, j):
            cl = cells[t % 2]
            psX = psA if j % 2 == 0 else psB
            pfk = "A" if j % 2 == 0 else "B"
            va = op("act", lambda e, jj=j, p=psX, c=cl: e.activation(
                out=h_sb[:, jj * 1024:(jj + 1) * 1024], in_=p,
                func=AF.Gelu, scale=c[:, FQ + 9:FQ + 10]),
                waits=[(sP, ms[("w1", t, j)]), (sV, _hqs(t - 1, j)),
                       (sV, ms[("fq", t)])], inc=sA, nosync=True)
            ms[("gelu", t, j)] = va
            psum_free[pfk] = (sA, va)
            op("dve", lambda e, jj=j, c=cl: e.tensor_reduce(
                out=c[:, FH + jj:FH + jj + 1],
                in_=h_sb[:, jj * 1024:(jj + 1) * 1024], axis=AX.X,
                op=ALU.max, apply_absolute_value=True),
                waits=[(sA, va)], nosync=True)

        def f_front_dve(t):
            cl = cells[t % 2]
            op("dve", lambda e, x=x1s(t), c=cl: e.tensor_reduce(
                out=c[:, FQ:FQ + 1], in_=x, axis=AX.X, op=ALU.max),
                waits=[(dL, x1_ld[t]), (sA, ms.get(("gelu", t - 2, 3), 0)),
                       (sP, ms.get(("sT1", t - 2), 0))])
            op("dve", lambda e, x=x1s(t), c=cl: e.tensor_reduce(
                out=c[:, FQ + 1:FQ + 2], in_=x, axis=AX.X, op=ALU.min),
                waits=[(dL, x1_ld[t])], nosync=True)
            op("dve", lambda e, c=cl, t_=t: e.tensor_tensor(
                out=c[:, FQ + 2:FQ + 3], in0=c[:, FQ:FQ + 1],
                in1=statsN[:, t_:t_ + 1], op=ALU.add))
            op("dve", lambda e, c=cl, t_=t: e.tensor_tensor(
                out=c[:, FQ + 3:FQ + 4], in0=c[:, FQ + 1:FQ + 2],
                in1=statsN[:, t_:t_ + 1], op=ALU.add))
            op("dve", lambda e, c=cl: e.tensor_scalar(
                out=c[:, FQ + 4:FQ + 5], in0=c[:, FQ + 3:FQ + 4],
                scalar1=-1.0, scalar2=None, op0=ALU.mult))
            op("dve", lambda e, c=cl: e.tensor_tensor(
                out=c[:, FQ + 5:FQ + 6], in0=c[:, FQ + 2:FQ + 3],
                in1=c[:, FQ + 4:FQ + 5], op=ALU.max))
            op("dve", lambda e, c=cl: e.reciprocal(
                out=c[:, FQ + 6:FQ + 7], in_=c[:, FQ + 5:FQ + 6]))
            op("dve", lambda e, c=cl: e.tensor_scalar(
                out=c[:, FQ + 7:FQ + 8], in0=c[:, FQ + 6:FQ + 7],
                scalar1=127.0, scalar2=None, op0=ALU.mult))
            op("dve", lambda e, c=cl, t_=t: e.tensor_tensor(
                out=c[:, FQ + 8:FQ + 9], in0=c[:, FQ + 5:FQ + 6],
                in1=statsR[:, t_:t_ + 1], op=ALU.mult))
            op("dve", lambda e, c=cl: e.tensor_scalar(
                out=c[:, FQ + 9:FQ + 10], in0=c[:, FQ + 8:FQ + 9],
                scalar1=bc_ws1, scalar2=None, op0=ALU.mult))
            op("dve", lambda e, c=cl: e.reciprocal(
                out=c[:, FQ + 10:FQ + 11], in_=c[:, FQ + 9:FQ + 10]))
            v = op("dve", lambda e, c=cl, t_=t: e.tensor_scalar(
                out=c[:, FQ + 11:FQ + 12], in0=statsN[:, t_:t_ + 1],
                scalar1=c[:, FQ + 7:FQ + 8], scalar2=RK,
                op0=ALU.mult, op1=ALU.add), inc=sV)
            ms[("fq", t)] = v

            rl = emit_round(x1s(t), xq,
                            cl[:, FQ + 7:FQ + 8], cl[:, FQ + 11:FQ + 12], None,
                            1, [(sV, ms[("fq", t)])],
                            [(sP, ms.get(("xqT_mm", t - 1), 0))])
            ms[("xq", t)] = rl[-1]

        def f_front_pe(t):
            cl = cells[t % 2]
            pv = op("pe", lambda e, c=cl: e.transpose(
                out=psF[0:1, 0:128], in_=c[:, FQ + 10:FQ + 11],
                identity=identf),
                waits=[(sV, ms[("fq", t)]), psum_free["F"],
                       (dC, ms["consts_ld"]),
                       (sP, ms.get(("w1", t - 1, 3), 0))], inc=sP)
            ms[("sT1", t)] = pv
            va = op("act", lambda e: e.activation(
                out=invs1row, in_=psF[0:1, 0:128], func=AF.Copy),
                waits=[(sP, pv)], inc=sA, nosync=True)
            ms[("i1r", t)] = va
            psum_free["F"] = (sA, va)

            sp, key = next_stage()
            pv = None
            for c in range(DC):
                pv = op("pe", lambda e, cc=c, p=sp: e.transpose(
                    out=p[:, cc * 128:(cc + 1) * 128],
                    in_=xq[:, cc * 128:(cc + 1) * 128], identity=identb),
                    waits=([(sV, ms[("xq", t)]), psum_free[key]]
                           if c == 0 else []),
                    inc=(sP if c == DC - 1 else None))
            ms[("xqT_mm", t)] = pv
            v = op("act", lambda e, p=sp: e.activation(
                out=xqT, in_=p, func=AF.Copy),
                waits=[(sP, pv), (sP, ms.get(("w1", t - 1, 3), 0))], inc=sA,
                nosync=True)
            psum_free[key] = (sA, v)
            ms[("xqT", t)] = v

            emit_w1_quarter(t, 0)
            emit_w1_quarter(t, 1)

        def f_fh(t):
            cl = cells[t % 2]
            op("dve", lambda e, c=cl: e.tensor_reduce(
                out=c[:, FH + 4:FH + 5], in_=c[:, FH:FH + 4], axis=AX.X,
                op=ALU.max))
            op("dve", lambda e, c=cl: e.reciprocal(
                out=c[:, FH + 8:FH + 9], in_=c[:, FH + 4:FH + 5]))
            op("dve", lambda e, c=cl: e.tensor_scalar(
                out=c[:, FH + 5:FH + 6], in0=c[:, FH + 8:FH + 9],
                scalar1=127.0, scalar2=None, op0=ALU.mult))
            op("dve", lambda e, c=cl: e.tensor_scalar(
                out=c[:, FH + 6:FH + 7], in0=c[:, FH + 4:FH + 5],
                scalar1=bc_ws2, scalar2=None, op0=ALU.mult))
            v = op("dve", lambda e, c=cl: e.reciprocal(
                out=c[:, FH + 7:FH + 8], in_=c[:, FH + 6:FH + 7]), inc=sV)
            ms[("fh", t)] = v

        def f_round0(t):
            cl = cells[t % 2]
            rl = emit_round(h_sb, hq, cl[:, FH + 5:FH + 6], None, RK, 1,
                            [(sV, ms[("fh", t)])],
                            [(sP, ms.get(("hqT_mm", t - 1), 0))])
            ms[("hqsub", t)] = rl

        def f_rounds123(t):
            cl = cells[t % 2]
            rl = emit_round(h_sb, hq, cl[:, FH + 5:FH + 6], None, RK, 3,
                            [], [], r0=1)
            ms[("hq", t)] = rl[-1]
            ms[("hqsub", t)] = ms[("hqsub", t)] + rl

        def f_back_T0(t):
            cl = cells[t % 2]
            pv = op("pe", lambda e, c=cl: e.transpose(
                out=psF[0:1, 128:256], in_=c[:, FH + 7:FH + 8],
                identity=identf),
                waits=[(sV, ms[("fh", t)]), psum_free["F"],
                       (sP, ms.get(("w2", t - 1), 0))], inc=sP)
            va = op("act", lambda e: e.activation(
                out=invs2row, in_=psF[0:1, 128:256], func=AF.Copy),
                waits=[(sP, pv)], inc=sA, nosync=True)
            ms[("i2r", t)] = va
            psum_free["F"] = (sA, va)
            _fT(t, (0,))

        def f_out(t):
            cl = cells[t % 2]
            v = op("dve", lambda e, c=cl, x=x1s(t): e.scalar_tensor_tensor(
                out=outbuf, in0=psC, scalar=c[:, FH + 6:FH + 7], in1=x,
                op0=ALU.mult, op1=ALU.add),
                waits=[(sP, ms[("w2", t)]), (dO, out_ms.get(t - 1, 0)),
                       (sV, ms[("fh", t)])],
                inc=sV, nosync=True)
            ms[("fout", t)] = v
            psum_free["C"] = (sV, v)
            out_ms[t] = dma("pool", y_d[t * T:(t + 1) * T, :], outbuf, dO,
                            waits=[(sV, v)])
            if t + 3 < nt:
                x1_ld[t + 3] = dma("syncq", x1s(t + 3),
                                   x1_d[(t + 3) * T:(t + 4) * T, :], dL,
                                   waits=[(sV, v)])

        def _fT(t, rrs):
            wv = [(sA, ms[("i2r", t)]),
                  (sV, ms.get(("fout", t - 1), 0)), (dW[0], ms["w2qt0"]),
                  (dW[1], ms["w2qt1"]), (dY, ms["bias_ld"])]
            pv = None
            evs = None
            for rr in rrs:
                sp, key = next_stage()
                for j in range(8):
                    pv = op("pe", lambda e, rr_=rr, jj=j, p=sp: e.transpose(
                        out=p[:, jj * 128:(jj + 1) * 128],
                        in_=hq[:, rr_ * 1024 + jj * 128:
                               rr_ * 1024 + (jj + 1) * 128],
                        identity=identb),
                        waits=([(sV, ms[("hqsub", t)][rr]), psum_free[key]]
                               if j == 0 else []),
                        inc=(sP if j == 7 else None))
                if rr == 3:
                    ms[("hqT_mm", t)] = pv
                evs = op("act", lambda e, rr_=rr, p=sp: e.activation(
                    out=hqT[:, rr_ * 1024:(rr_ + 1) * 1024], in_=p,
                    func=AF.Copy),
                    waits=[(sP, pv)] + ([(sP, ms.get(("w2", t - 1), 0))]
                                        if rr == 0 else []), inc=sA,
                    nosync=True)
                psum_free[key] = (sA, evs)
                for n in range(2):
                    if rr == 0:
                        op("pe", lambda e, nn=n: e.matmul(
                            out=psC[:, nn * 512:(nn + 1) * 512],
                            lhsT=invs2row,
                            rhs=b2bf[0:1, nn * 512:(nn + 1) * 512],
                            start=True, stop=False),
                            waits=(wv + [(sA, evs)]) if n == 0 else [],
                            inc=None)
                    for k in range(rr * 8, rr * 8 + 8):
                        pv = op("pe", lambda e, nn=n, kk=k: e.matmul(
                            out=psC[:, nn * 512:(nn + 1) * 512],
                            lhsT=hqT[:, kk * 128:(kk + 1) * 128],
                            rhs=W2qT[:, kk * 1024 + nn * 512:
                                     kk * 1024 + (nn + 1) * 512],
                            start=False, stop=(kk == 31)),
                            waits=([(sA, evs)] if (n == 0 and k == rr * 8)
                                   else []),
                            inc=(sP if (n == 1 and k == 31) else None))
            ms[("hqT", t)] = evs
            ms[("w2", t)] = pv

        f_front_dve(0)
        f_front_pe(0)
        if nt > 1:
            f_front_dve(1)
        emit_gelu(0, 0)
        emit_gelu(0, 1)
        emit_w1_quarter(0, 2)
        emit_gelu(0, 2)
        emit_w1_quarter(0, 3)
        emit_gelu(0, 3)
        f_fh(0)
        for t in range(nt):
            fence(gtile)
            gtile += 1
            f_round0(t)
            if t + 1 < nt:
                f_front_pe(t + 1)
            f_back_T0(t)
            f_rounds123(t)
            if t + 2 < nt:
                f_front_dve(t + 2)
            _fT(t, (1,))
            if t + 1 < nt:
                emit_gelu(t + 1, 0)
                emit_gelu(t + 1, 1)
                emit_w1_quarter(t + 1, 2)
            _fT(t, (2,))
            if t + 1 < nt:
                emit_gelu(t + 1, 2)
                emit_w1_quarter(t + 1, 3)
            _fT(t, (3,))
            if t + 1 < nt:
                emit_gelu(t + 1, 3)
                f_fh(t + 1)
            f_out(t)
            tile_end()

        op("syncq", None, waits=[(dO, dO.n)])
        _emit_all(nc, OPS, _ES, cut)

    return nc


def _emit_all(nc, OPS, _ES, cut=None):
    if cut is not None:
        OPS = OPS[:cut]
        tot = {}
        for (k, waits, fn, inc, amt, is_dma, ns) in OPS:
            if inc is not None:
                tot[inc.name] = tot.get(inc.name, 0) + amt
        OPS = [(k, [(s, min(v, tot.get(s.name, 0))) for (s, v) in waits],
                fn, inc, amt, is_dma, ns)
               for (k, waits, fn, inc, amt, is_dma, ns) in OPS]

    def mk(ek):
        zs = _ES.get(ek)

        def run(eng):
            lastw = {}
            zi = 0
            for (k, waits, fn, inc, amt, is_dma, ns) in OPS:
                if k != ek:
                    continue
                for (s, val) in waits:
                    if val > 0 and lastw.get(s.name, 0) < val:
                        eng.wait_ge(s.h, val)
                        lastw[s.name] = val
                if fn is None:
                    continue
                if zs is not None and zi > 0 and not ns:
                    eng.wait_ge(zs.h, zi)
                inst = fn(eng)
                if zs is not None and not is_dma:
                    zi += 1
                if inc is not None:
                    inst.then_inc(inc.h, amt)
        return run

    with nc.Block() as block:
        block.tensor(mk("pe"))
        block.vector(mk("dve"))
        block.scalar(mk("act"))
        block.gpsimd(mk("pool"))
        block.sync(mk("syncq"))


def _consts():
    identb = np.eye(128, dtype=ml_dtypes.bfloat16)
    identf = np.eye(128, dtype=np.float32)
    a = np.arange(128)[:, None]
    b = np.arange(128)[None, :]
    m_prev = np.where(b > a, 0.0, NEG).astype(np.float32)
    m_cur = np.where(b <= a, 0.0, NEG).astype(np.float32)
    normal = np.concatenate([m_prev, m_cur], axis=1)
    t0 = np.concatenate([np.full((128, 128), NEG, np.float32), m_cur], axis=1)
    masks = np.concatenate([normal, t0], axis=1).astype(ml_dtypes.bfloat16)
    return identb, identf, masks


_NC_CACHE = {}


def kernel(L0_attn_x, L0_attn_w_qkv, L0_attn_w_out, L0_ffn_W1, L0_ffn_b1,
           L0_ffn_W2, L0_ffn_b2):
    x = np.ascontiguousarray(np.asarray(L0_attn_x, dtype=np.float32))
    wqkv = np.ascontiguousarray(np.asarray(L0_attn_w_qkv, dtype=np.float32))
    wout = np.ascontiguousarray(np.asarray(L0_attn_w_out, dtype=np.float32))
    w1 = np.ascontiguousarray(np.asarray(L0_ffn_W1, dtype=np.float32))
    w2 = np.ascontiguousarray(np.asarray(L0_ffn_W2, dtype=np.float32))
    b1 = np.ascontiguousarray(np.asarray(L0_ffn_b1, dtype=np.float32))
    b2 = np.ascontiguousarray(np.asarray(L0_ffn_b2, dtype=np.float32))
    identb, identf, masks = _consts()

    nt = SEQ_PC * TPS
    if nt not in _NC_CACHE:
        _NC_CACHE[nt] = _build(nt)
    nc = _NC_CACHE[nt]

    in_maps = []
    for c in range(NCORES):
        xs = np.ascontiguousarray(
            x[c * SEQ_PC:(c + 1) * SEQ_PC].reshape(SEQ_PC * S, D))
        in_maps.append({
            "xc": xs, "wqkv": wqkv, "wout": wout, "w1": w1, "w2": w2,
            "b1": b1, "b2": b2, "identb": identb, "identf": identf,
            "masks": masks,
        })
    res = run_bass_kernel_spmd(nc, in_maps, core_ids=list(range(NCORES)))
    out = np.concatenate(
        [r["y"].reshape(SEQ_PC, S, D) for r in res.results], axis=0)
    return out


# version 82
# speedup vs baseline: 1.0879x; 1.0093x over previous
import numpy as np
import ml_dtypes
from contextlib import ExitStack

import concourse.bass as bass
import concourse.mybir as mybir
from concourse.bass_utils import run_bass_kernel_spmd

F32 = mybir.dt.float32
BF16 = mybir.dt.bfloat16
I32 = mybir.dt.int32
AF = mybir.ActivationFunctionType
ALU = mybir.AluOpType
AX = mybir.AxisListType

B, S, D = 16, 2048, 1024
NCORES = 8
SEQ_PC = B // NCORES
T = 128
TPS = S // T
DC = D // 128
HN, DH = 16, 64
HF = 4096
NEG = -1.0e9
RK = 640.5

_SB_BASE = 16640
_SB_TOP = 229376


class _Sem:
    __slots__ = ("name", "n", "h")

    def __init__(self, name):
        self.name = name
        self.n = 0
        self.h = None


def _off(ap):
    return ap.tensor.manual_sbuf_range[0]


def _build(nt, mode="full", cut=None):
    ntok = nt * T
    nc = bass.Bass("TRN2", target_bir_lowering=False, debug=False)

    xc = nc.dram_tensor("xc", [ntok, D], F32, kind="ExternalInput").ap()
    wqkv_d = nc.dram_tensor("wqkv", [D, 3 * D], F32, kind="ExternalInput").ap()
    wout_d = nc.dram_tensor("wout", [D, D], F32, kind="ExternalInput").ap()
    w1_d = nc.dram_tensor("w1", [HF, D], F32, kind="ExternalInput").ap()
    w2_d = nc.dram_tensor("w2", [D, HF], F32, kind="ExternalInput").ap()
    b1_d = nc.dram_tensor("b1", [HF], F32, kind="ExternalInput").ap()
    b2_d = nc.dram_tensor("b2", [D], F32, kind="ExternalInput").ap()
    identb_d = nc.dram_tensor("identb", [128, 128], BF16, kind="ExternalInput").ap()
    identf_d = nc.dram_tensor("identf", [128, 128], F32, kind="ExternalInput").ap()
    masks_d = nc.dram_tensor("masks", [128, 512], BF16, kind="ExternalInput").ap()
    y_d = nc.dram_tensor("y", [ntok, D], F32, kind="ExternalOutput").ap()
    x1_d = nc.dram_tensor("x1sp", [ntok, D], F32, kind="Internal").ap()
    w2t_d = nc.dram_tensor("w2td", [128, 32 * 1024], BF16, kind="Internal").ap()
    bbf_d = nc.dram_tensor("bbfd", [HF + D], BF16, kind="Internal").ap()

    cur = [_SB_BASE]

    def sb(name, p, n, dt, at=None):
        size = n * mybir.dt.size(dt)
        o = cur[0] if at is None else at
        o = (o + 31) & ~31
        h = nc._state.alloc_sbuf_tensor_at(name, [p, n], dt, offset=o)
        if at is None:
            cur[0] = o + size
        return h.ap()

    identb = sb("identb", 128, 128, BF16)
    identf = sb("identf", 128, 128, F32)
    masks = sb("masks", 128, 512, BF16)
    b1bf = sb("b1bf", 1, HF, BF16)
    b2bf = sb("b2bf", 1, D, BF16)
    cells = [sb("cells0", 128, 64, F32), sb("cells1", 128, 64, F32)]
    statsN = sb("statsN", 128, nt, F32)
    statsR = sb("statsR", 128, nt, F32)
    _c4 = sb("c4", 128, 5, F32)
    c_eps = _c4[:, 0:1]
    c_nh = _c4[:, 1:2]
    c_ph = _c4[:, 2:3]
    ones_col = _c4[:, 3:4]
    c_rk = _c4[:, 4:5]
    ones_row = sb("ones_row", 1, 128, F32)
    _b4 = sb("b4", 128, 4, F32)
    bc_r1 = _b4[:, 0:1]
    bc_ws1 = _b4[:, 1:2]
    bc_r2 = _b4[:, 2:3]
    bc_ws2 = _b4[:, 3:4]
    accw = sb("accw", 128, 8, F32)
    wsc = sb("wsc", 1, 8, F32)
    x1buf = [sb("x1b0", 128, D, F32), sb("x1b1", 128, D, F32)]
    outbuf = sb("outbuf", 128, D, F32)
    invs1row = sb("i1row", 1, 128, BF16)
    invs2row = sb("i2row", 1, 128, BF16)

    A0 = (cur[0] + 63) & ~63
    cur[0] = A0
    wqkv_sb = sb("wqkv_sb", 128, DC * 3 * D, BF16)
    wout_sb = sb("wout_sb", 128, DC * D, BF16)
    xin = [sb("xin0", 128, D, F32), sb("xin1", 128, D, F32)]
    xln = sb("xln", 128, D, BF16)
    xlnT = sb("xlnT", 128, D, BF16)
    q_sb = sb("q_sb", 128, D, BF16)
    k_sb = sb("k_sb", 128, D, BF16)
    qT = sb("qT", 64, 2048, BF16)
    kT_hist = sb("kT_hist", 64, 2 * 2048, BF16)
    v_hist = sb("v_hist", 128, 2 * D, BF16)
    probs = [sb("probs0", 128, 1024, BF16), sb("probs1", 128, 1024, BF16)]
    probsn = probs
    probsT_b = sb("probsT", 128, 1024, BF16)
    probsT = [probsT_b, probsT_b]
    attnout = sb("attnout", 128, 1024, BF16)
    wstage = [sb("wstage0", 128, 512, F32), sb("wstage1", 128, 512, F32),
              sb("wstage2", 128, 512, F32), sb("wstage3", 128, 512, F32)]
    sgnA = sb("sgnA", 128, 512, BF16)
    sgnB = sb("sgnB", 128, 512, BF16)
    wqch = [sb("wqch0", 128, 512, BF16), sb("wqch1", 128, 512, BF16)]
    w2st = [sb("w2st0", 128, 512, BF16), sb("w2st1", 128, 512, BF16)]
    A_END = cur[0]

    w1t_at = (229368 - DC * HF * 2) & ~63
    W1qT = sb("W1qT", 128, DC * HF, BF16, at=w1t_at)
    assert A_END <= w1t_at, (A_END, w1t_at)

    cur[0] = A0
    W2qT = sb("W2qT", 128, 32 * D, BF16)
    h_sb = sb("h_sb", 128, HF, F32)
    hq = sb("hq", 128, HF, BF16)
    hqT = sb("hqT", 128, HF, BF16)
    t1b = sb("t1a", 128, 1024, F32)
    tib = sb("tib", 128, 1024, I32)
    tfb = sb("tfb", 128, 1024, F32)
    x1c = sb("x1c", 128, D, F32)
    xq = sb("xq", 128, D, BF16)
    xqT = sb("xqT", 128, D, BF16)
    F_END = cur[0]
    assert F_END <= w1t_at, (F_END, w1t_at)

    wstg_big = [
        sb("wstgA", 128, 1536, F32, at=w1t_at),
        sb("wstgB", 128, 1536, F32, at=w1t_at + 1536 * 4),
    ]
    bstage = sb("bstage", 128, 32, F32, at=w1t_at + 1536 * 8)
    bstage_bf = sb("bstageb", 128, 32, BF16, at=w1t_at + 1536 * 8 + 128)

    with ExitStack() as ctx:
        psA = ctx.enter_context(nc.psum_tensor("psA", [128, 1024], F32)).ap()
        psB = ctx.enter_context(nc.psum_tensor("psB", [128, 1024], F32)).ap()
        psC = ctx.enter_context(nc.psum_tensor("psC", [128, 1024], F32)).ap()
        psD = ctx.enter_context(nc.psum_tensor("psD", [128, 1024], BF16)).ap()
        psF = ctx.enter_context(nc.psum_tensor("psF", [128, 512], F32)).ap()

        sP = _Sem("sP"); sV = _Sem("sV"); sA = _Sem("sA"); sG = _Sem("sG")
        dX = _Sem("dX"); dS = _Sem("dS"); dL = _Sem("dL"); dO = _Sem("dO")
        dC = _Sem("dC"); dY = _Sem("dY")
        dW = [_Sem("dW0"), _Sem("dW1")]
        dP = [_Sem("dP0"), _Sem("dP1"), _Sem("dP2"), _Sem("dP3")]
        dB = [_Sem("dB0"), _Sem("dB1")]
        SEMS = [sP, sV, sA, sG, dX, dS, dL, dO, dC, dY] + dW + dP + dB
        for s in SEMS:
            s.h = ctx.enter_context(nc.semaphore(s.name))

        OPS = []
        _ES = {"dve": sV, "act": sA, "pool": sG}

        def op(ek, fn, waits=(), inc=None, amt=1, is_dma=False,
               nosync=False):
            if fn is not None and not is_dma and ek in _ES:
                inc = _ES[ek]
                amt = 1
            OPS.append((ek, [w for w in waits if w is not None and w[1] > 0],
                        fn, inc, amt, is_dma, nosync))
            if inc is not None:
                inc.n += amt
                return inc.n
            return None

        def dma(ek, out_ap, in_ap, sem, waits=()):
            w = [(sem, sem.n)] + list(waits)
            return op(ek, (lambda e, o=out_ap, i=in_ap: e.dma_start(out=o, in_=i)),
                      w, sem, 16, is_dma=True)

        ms = {}

        def tickle(n=2):
            for _ in range(n):
                op("pe", lambda e: e.ldweights(identb))

        def tickle_train(sem, target, step=3, n=4, per=3):
            for k in range(n, 0, -1):
                v = target - k * step
                if v > 0:
                    for _ in range(per):
                        op("pe", lambda e: e.ldweights(identb),
                           waits=[(sem, v)])

        psum_free = {"A": None, "B": None, "C": None, "D": None, "F": None}
        psD2 = psF.bitcast(BF16)
        _stg = {"t": 0}

        def next_stage():
            _stg["t"] ^= 1
            return (psD, "D") if _stg["t"] else (psD2, "F")
        eng_end = {k: [0] for k in ("pe", "dve", "act", "pool")}
        ENG_SEM = {"pe": sP, "dve": sV, "act": sA, "pool": sG}

        def fence(gt):
            if gt < 2:
                return
            for ek in ("pe", "dve", "act", "pool"):
                waits = [(ENG_SEM[o], eng_end[o][gt - 1])
                         for o in ("pe", "dve", "act", "pool") if o != ek]
                op(ek, None, waits=waits)

        def tile_end():
            for ek in ("pe", "dve", "act", "pool"):
                eng_end[ek].append(ENG_SEM[ek].n)

        def _memset(ap_, val):
            return lambda e, a=ap_, v=val: e.memset(a, v)

        op("dve", _memset(c_eps, 1e-5))
        op("dve", _memset(c_nh, -0.5))
        op("dve", _memset(c_ph, 0.5))
        op("dve", _memset(c_rk, RK))
        op("dve", _memset(ones_col, 1.0))
        op("dve", _memset(ones_row, 1.0))
        op("dve", _memset(accw, 0.0))
        op("dve", _memset(kT_hist, 0.0))
        op("dve", _memset(v_hist, 0.0), inc=sV)
        ms["memset"] = sV.n

        dma("act", identb, identb_d, dC)
        dma("act", identf, identf_d, dC)
        dma("act", masks, masks_d, dC)
        ms["consts_ld"] = dC.n

        _w = dma("act", bstage, b1_d.rearrange("(p k) -> p k", p=128), dY)
        _a = op("act", lambda e: e.activation(out=bstage_bf, in_=bstage,
                                              func=AF.Copy),
                waits=[(dY, _w)], inc=sA)
        _b = dma("act", bbf_d[0:HF].rearrange("(p k) -> p k", p=128), bstage_bf,
                 dY, waits=[(sA, _a)])
        dma("act", b1bf, bbf_d[0:HF].rearrange("(a k) -> a k", a=1), dY,
            waits=[(dY, _b)])
        _w = dma("act", bstage[:, 0:8], b2_d.rearrange("(p k) -> p k", p=128),
                 dY, waits=[(dY, _b)])
        _a = op("act", lambda e: e.activation(out=bstage_bf[:, 0:8],
                                              in_=bstage[:, 0:8], func=AF.Copy),
                waits=[(dY, _w)], inc=sA)
        _b = dma("act", bbf_d[HF:HF + D].rearrange("(p k) -> p k", p=128),
                 bstage_bf[:, 0:8], dY, waits=[(sA, _a)])
        dma("act", b2bf, bbf_d[HF:HF + D].rearrange("(a k) -> a k", a=1), dY,
            waits=[(dY, _b)])
        ms["bias_ld"] = dY.n

        xin_ld = {}
        for t0 in range(min(2, nt)):
            xin_ld[t0] = dma("syncq", xin[t0 % 2], xc[t0 * T:(t0 + 1) * T, :], dX)

        wc_v, wc_a = 0, 0
        stg_rd = {0: None, 1: None}
        pieces = []
        for i in range(8):
            pieces.append((wqkv_d[i * 128:(i + 1) * 128, 0:1536],
                           wqkv_sb[:, i * 3 * D:i * 3 * D + 1536], 1536))
        for i in range(8):
            pieces.append((wqkv_d[i * 128:(i + 1) * 128, 1536:3072],
                           wqkv_sb[:, i * 3 * D + 1536:
                                   i * 3 * D + 3072], 1536))
        for i in range(8):
            pieces.append((wout_d[i * 128:(i + 1) * 128, :],
                           wout_sb[:, i * D:(i + 1) * D], 1024))
        for j, (src, dst, n) in enumerate(pieces):
            buf = wstg_big[j % 2]
            ld = dma("act", buf[:, 0:n], src, dW[j % 2], waits=[stg_rd[j % 2]])
            ek = "dve" if j % 2 == 0 else "act"
            sm = sV if ek == "dve" else sA
            if ek == "dve":
                fn = (lambda e, o=dst, i_=buf[:, 0:n]:
                      e.tensor_copy(out=o, in_=i_))
            else:
                fn = (lambda e, o=dst, i_=buf[:, 0:n]: e.copy(out=o, in_=i_))
            v = op(ek, fn, waits=[(dW[j % 2], ld)], inc=sm)
            stg_rd[j % 2] = (sm, v)
            if ek == "dve":
                wc_v = v
            else:
                wc_a = v
            if j == 7:
                ms["wcast_q"] = (wc_v, wc_a)
            elif j == 15:
                ms["wcast_kv"] = (wc_v, wc_a)
        ms["wcast_v"] = wc_v
        ms["wcast_a"] = wc_a

        if mode == "S":
            v = op("dve", lambda e: e.tensor_copy(out=outbuf, in_=wqkv_sb[:, 0:D]),
                   waits=[(sV, wc_v), (sA, wc_a), (dX, dX.n), (dY, ms["bias_ld"]),
                          (dC, ms["consts_ld"])])
            o = dma("pool", y_d[0:T, :], outbuf, dO, waits=[(sV, v)])
            op("syncq", None, waits=[(dO, dO.n)])
            _emit_all(nc, OPS, _ES, cut)
            return nc

        prep_list = ([("a1", i) for i in range(64)] +
                     [("a2", i) for i in range(64)] +
                     [("f1",), ("f2",)] +
                     [("q1", i) for i in range(64)] +
                     [("q2", i) for i in range(64)])
        if mode == "A":
            prep_list = []
        prep_by_tile = [[] for _ in range(nt)]
        for j, task in enumerate(prep_list):
            prep_by_tile[min(nt - 1, j * nt // len(prep_list))].append(task)

        pst = {
            "stg_rd": {0: None, 1: None, 2: None, 3: None},
            "stg_ld": {},
            "acc": {"a1": 0, "a2": 0},
            "sgn_rd": None,
            "wq_rd": {0: None, 1: None},
            "w2st_rd": {0: None, 1: None},
            "ndma": 0,
            "acc_rd": None,
            "cursor": 0,
            "pend": [],
        }

        def prep_src(kind, i):
            if kind in ("a1", "q1"):
                return w1_d[(i // 2) * 128:(i // 2 + 1) * 128,
                            (i % 2) * 512:(i % 2) * 512 + 512]
            return w2_d[(i // 8) * 128:(i // 8 + 1) * 128,
                        (i % 8) * 512:(i % 8) * 512 + 512]

        def prep_issue_next_dma():
            while pst["cursor"] < len(prep_list):
                task = prep_list[pst["cursor"]]
                pst["cursor"] += 1
                if task[0] in ("f1", "f2"):
                    continue
                slot = pst["ndma"] % 4
                pst["ndma"] += 1
                ld = dma("syncq", wstage[slot], prep_src(task[0], task[1]),
                         dP[slot], waits=[pst["stg_rd"][slot]])
                pst["stg_ld"][task] = (slot, ld)
                return

        def prep_fin(grp):
            kind = "a1" if grp == 0 else "a2"
            acc_slot = grp * 2 + pst["acc"][kind] % 2
            nelem = float(HF * D)
            pv = op("pe", lambda e, a=accw[:, acc_slot:acc_slot + 1]:
                    e.matmul(out=psF[0:1, 0:1], lhsT=a, rhs=ones_col,
                             start=True, stop=True),
                    waits=[(sV, ms["abs_%s_last" % kind]), psum_free["F"],
                           (sV, ms["memset"])], inc=sP)
            op("dve", lambda e: e.tensor_copy(out=wsc[0:1, 0:1],
                                              in_=psF[0:1, 0:1]),
               waits=[(sP, pv)])
            op("dve", lambda e, n=nelem: e.tensor_scalar(
                out=wsc[0:1, 1:2], in0=wsc[0:1, 0:1],
                scalar1=1.0 / n, scalar2=1e-5, op0=ALU.mult, op1=ALU.max))
            vr = op("dve", lambda e: e.reciprocal(out=wsc[0:1, 2:3],
                                                  in_=wsc[0:1, 1:2]), inc=sV)
            pv = op("pe", lambda e: e.matmul(out=psF[:, 0:1], lhsT=ones_row,
                                             rhs=wsc[0:1, 2:3],
                                             start=True, stop=True),
                    waits=[(sV, vr)], inc=sP)
            pv = op("pe", lambda e: e.matmul(out=psF[:, 1:2], lhsT=ones_row,
                                             rhs=wsc[0:1, 1:2],
                                             start=True, stop=True), inc=sP)
            dst_r = bc_r1 if grp == 0 else bc_r2
            dst_w = bc_ws1 if grp == 0 else bc_ws2
            op("dve", lambda e, d=dst_r: e.tensor_copy(out=d, in_=psF[:, 0:1]),
               waits=[(sP, pv)])
            v = op("dve", lambda e, d=dst_w: e.tensor_scalar(
                out=d, in0=psF[:, 1:2], scalar1=1.0 / 127.0, scalar2=None,
                op0=ALU.mult), inc=sV)
            psum_free["F"] = (sV, v)
            ms["wsc_%d" % grp] = v

        def prep_emit(task):
            kind = task[0]
            if kind == "f1":
                prep_fin(0); return
            if kind == "f2":
                prep_fin(1); return
            i = task[1]
            slot, ld = pst["stg_ld"][task]
            if kind in ("a1", "a2"):
                cnt = pst["acc"][kind]
                g = 0 if kind == "a1" else 1
                src_s = g * 2 + cnt % 2
                dst_s = g * 2 + (cnt + 1) % 2
                op("dve", lambda e, s=wstage[slot]: e.tensor_reduce(
                    out=accw[:, 4:5], in_=s, axis=AX.X, op=ALU.add,
                    apply_absolute_value=True),
                    waits=[(dP[slot], ld), (sV, ms["memset"]),
                           pst["acc_rd"]], nosync=True)
                v = op("dve", lambda e, a=accw[:, src_s:src_s + 1],
                       b=accw[:, dst_s:dst_s + 1]:
                       e.tensor_tensor(out=b, in0=a, in1=accw[:, 4:5],
                                       op=ALU.add), inc=sV)
                pst["acc_rd"] = (sV, v)
                pst["acc"][kind] = cnt + 1
                pst["stg_rd"][slot] = (sV, v)
                ms["abs_%s_last" % kind] = v
            else:
                g = 0 if kind == "q1" else 1
                wqs = i % 2
                if any(p[2] == wqs for p in pst["pend"]):
                    prep_flushT()
                bc = bc_r1 if g == 0 else bc_r2
                op("dve", lambda e, s=wstage[slot], b=bc: e.tensor_scalar(
                    out=sgnA, in0=s, scalar1=b, scalar2=0.5,
                    op0=ALU.mult, op1=ALU.is_ge),
                    waits=[(dP[slot], ld), (sV, ms["wsc_%d" % g]),
                           pst["sgn_rd"]], nosync=True)
                va = op("dve", lambda e, s=wstage[slot], b=bc: e.tensor_scalar(
                    out=sgnB, in0=s, scalar1=b, scalar2=-0.5,
                    op0=ALU.mult, op1=ALU.is_ge), inc=sV, nosync=True)
                pst["stg_rd"][slot] = (sV, va)
                v = op("dve", lambda e, o=wqch[wqs]: e.scalar_tensor_tensor(
                    out=o, in0=sgnA, scalar=-1.0, in1=sgnB,
                    op0=ALU.add, op1=ALU.add),
                    waits=[pst["wq_rd"][wqs]], inc=sV)
                pst["sgn_rd"] = (sV, v)
                pst["pend"].append((kind, i, wqs, v))
            prep_issue_next_dma()

        def prep_flushT():
            for kind, i, wqs, vq in pst["pend"]:
                sp, key = next_stage()
                pv = None
                for j in range(4):
                    pv = op("pe", lambda e, jj=j, s=wqch[wqs], p=sp: e.transpose(
                        out=p[:, jj * 128:(jj + 1) * 128],
                        in_=s[:, jj * 128:(jj + 1) * 128], identity=identb),
                        waits=([(sV, vq), psum_free[key],
                                (dC, ms["consts_ld"])]
                               if j == 0 else []),
                        inc=(sP if j == 3 else None))
                pst["wq_rd"][wqs] = (sP, pv)
                if kind == "q1":
                    hb, half = i // 2, i % 2
                    dst = W1qT.rearrange("p (c h) -> p c h", c=DC)[
                        :, half * 4:half * 4 + 4, hb * 128:(hb + 1) * 128]
                    v = op("dve", lambda e, d=dst, p=sp: e.tensor_copy(
                        out=d,
                        in_=p[:, 0:512].rearrange("p (c h) -> p c h", c=4)),
                        waits=[(sP, pv), (sV, ms["wcast_v"]),
                               (sA, ms["wcast_a"])], inc=sV, nosync=True)
                    psum_free[key] = (sV, v)
                else:
                    db, s8 = i // 8, i % 8
                    st = i % 2
                    v = op("dve", lambda e, o=w2st[st], p=sp: e.tensor_copy(
                        out=o, in_=p[:, 0:512]),
                        waits=[(sP, pv), pst["w2st_rd"][st]], inc=sV,
                        nosync=True)
                    psum_free[key] = (sV, v)
                    dst = w2t_d.rearrange("p (c d) -> p c d", c=32)[
                        :, s8 * 4:s8 * 4 + 4, db * 128:(db + 1) * 128]
                    b = dma("syncq", dst,
                            w2st[st].rearrange("p (c d) -> p c d", c=4),
                            dB[st], waits=[(sV, v)])
                    pst["w2st_rd"][st] = (dB[st], b)
            pst["pend"] = []

        for _ in range(4):
            prep_issue_next_dma()

        spill_ms = {}
        (CEL_SUM, CEL_SSQ, CEL_M, CEL_MSQ, CEL_VAR, CEL_LNV, CEL_RSTD,
         CEL_NMR) = range(8)
        CEL_NMX, CEL_SUMS, CEL_RCP = 8, 24, 40
        CEL2 = 56

        def a_ln1_stats(t):
            par = t % 2
            cl = cells[t % 2]
            va = op("act", lambda e, x=xin[par], c=cl: e.activation(
                out=outbuf, in_=x, func=AF.Square,
                accum_out=c[:, CEL_SSQ:CEL_SSQ + 1]),
                waits=[(dX, xin_ld[t])], inc=sA)
            op("dve", lambda e, x=xin[par], c=cl: e.tensor_reduce(
                out=c[:, CEL_SUM:CEL_SUM + 1], in_=x, axis=AX.X, op=ALU.add),
                waits=[(dX, xin_ld[t])], nosync=True)
            vm = op("dve", lambda e, c=cl: e.tensor_scalar(
                out=c[:, CEL_M:CEL_M + 1], in0=c[:, CEL_SUM:CEL_SUM + 1],
                scalar1=1.0 / D, scalar2=None, op0=ALU.mult), inc=sV)
            ms[("ln1m", t)] = vm
            op("dve", lambda e, c=cl: e.tensor_tensor(
                out=c[:, CEL_MSQ:CEL_MSQ + 1], in0=c[:, CEL_M:CEL_M + 1],
                in1=c[:, CEL_M:CEL_M + 1], op=ALU.mult))
            v = op("dve", lambda e, c=cl: e.scalar_tensor_tensor(
                out=c[:, CEL_VAR:CEL_VAR + 1], in0=c[:, CEL_SSQ:CEL_SSQ + 1],
                scalar=1.0 / D, in1=c[:, CEL_MSQ:CEL_MSQ + 1],
                op0=ALU.mult, op1=ALU.subtract), waits=[(sA, va)], inc=sV)
            op("act", lambda e, c=cl: e.activation(
                out=c[:, CEL_LNV:CEL_LNV + 1], in_=c[:, CEL_VAR:CEL_VAR + 1],
                func=AF.Ln, bias=c_eps),
                waits=[(sV, v), (sV, ms["memset"])])
            va = op("act", lambda e, c=cl: e.activation(
                out=c[:, CEL_RSTD:CEL_RSTD + 1], in_=c[:, CEL_LNV:CEL_LNV + 1],
                func=AF.Exp, scale=-0.5), inc=sA)
            ms[("rstd", t)] = va

        def a_ln1_apply(t):
            par = t % 2
            cl = cells[t % 2]
            v = op("dve", lambda e, c=cl: e.tensor_scalar(
                out=c[:, CEL_NMR:CEL_NMR + 1], in0=c[:, CEL_M:CEL_M + 1],
                scalar1=c[:, CEL_RSTD:CEL_RSTD + 1], scalar2=-1.0,
                op0=ALU.mult, op1=ALU.mult),
                waits=[(sA, ms[("rstd", t)]), (sV, ms[("ln1m", t)])],
                inc=sV, nosync=True)
            va = op("act", lambda e, x=xin[par], c=cl: e.activation(
                out=xln, in_=x, func=AF.Identity,
                bias=c[:, CEL_NMR:CEL_NMR + 1],
                scale=c[:, CEL_RSTD:CEL_RSTD + 1]),
                waits=[(sV, v), (sA, ms.get(("xlnT", t - 1), 0)),
                       (sA, ms[("rstd", t)])], inc=sA, nosync=True)
            ms[("xln", t)] = va

        def a_xlnT(t):
            sp, key = next_stage()
            pv = None
            for c in range(DC):
                pv = op("pe", lambda e, cc=c, p=sp: e.transpose(
                    out=p[:, cc * 128:(cc + 1) * 128],
                    in_=xln[:, cc * 128:(cc + 1) * 128], identity=identb),
                    waits=([(sA, ms[("xln", t)]), psum_free[key],
                            (dC, ms["consts_ld"])] if c == 0 else []),
                    inc=(sP if c == DC - 1 else None))
            v = op("act", lambda e, p=sp: e.copy(out=xlnT, in_=p),
                   waits=[(sP, pv), (sP, ms.get(("qkv", t - 1, 2), 0))],
                   inc=sA, nosync=True)
            psum_free[key] = (sA, v)
            ms[("xlnT", t)] = v

        def a_qkv(t, grp):
            par = t % 2
            psX, pfk = ((psA, "A"), (psB, "B"), (psC, "C"))[grp]
            wq = ms["wcast_q"] if grp == 0 else ms["wcast_kv"]
            w0 = [(sA, ms[("xlnT", t)]), psum_free[pfk],
                  (sV, wq[0]), (sA, wq[1])]
            pv = None
            for c in range(DC):
                for n in range(2):
                    pv = op("pe", lambda e, cc=c, nn=n, g=grp, p=psX:
                            e.matmul(
                        out=p[:, nn * 512:(nn + 1) * 512],
                        lhsT=xlnT[:, cc * 128:(cc + 1) * 128],
                        rhs=wqkv_sb[:, cc * 3 * D + g * D + nn * 512:
                                    cc * 3 * D + g * D + (nn + 1) * 512],
                        start=(cc == 0), stop=(cc == DC - 1)),
                        waits=(w0 if (c == 0 and n == 0) else []),
                        inc=(sP if (c == DC - 1 and n == 1) else None))
            ms[("qkv", t, grp)] = pv
            if grp == 0:
                va = op("act", lambda e: e.activation(out=q_sb, in_=psA,
                                                      func=AF.Copy,
                                                      scale=0.125),
                        waits=[(sP, pv), (sA, ms.get(("qT", t - 1), 0))],
                        inc=sA, nosync=True)
                ms[("q_sb", t)] = va
                psum_free["A"] = (sA, va)
            elif grp == 1:
                v = op("dve", lambda e: e.tensor_copy(out=k_sb, in_=psB),
                       waits=[(sP, pv), (sA, ms.get(("kT", t - 1), 0))],
                       inc=sV, nosync=True)
                ms[("k_sb", t)] = v
                psum_free["B"] = (sV, v)
            else:
                v = op("dve", lambda e, s=par: e.tensor_copy(
                    out=v_hist[:, s * D:(s + 1) * D], in_=psC),
                    waits=[(sP, pv), (sP, ms.get(("av", t - 1, 3), 0))],
                    inc=sV, nosync=True)
                ms[("v_sb", t)] = v
                psum_free["C"] = (sV, v)

        def a_qkT(t):
            par = t % 2
            for src_name, buf, wkey in (("q", q_sb, ("q_sb", t)),
                                        ("k", k_sb, ("k_sb", t))):
                wsem = sA if src_name == "q" else sV
                for r in range(2):
                    sp, key = next_stage()
                    pv = None
                    for hh in range(8):
                        h = r * 8 + hh
                        pv = op("pe", lambda e, b=buf, h_=h, hh_=hh, p=sp:
                                e.transpose(
                            out=p[0:64, hh_ * 128:(hh_ + 1) * 128],
                            in_=b[:, h_ * 64:(h_ + 1) * 64], identity=identb),
                            waits=([(wsem, ms[wkey]), psum_free[key]]
                                   if hh == 0 else []),
                            inc=(sP if hh == 7 else None))
                    if src_name == "q":
                        dst = qT[0:64, r * 1024:(r + 1) * 1024]
                    else:
                        dst = kT_hist[0:64, par * 2048 + r * 1024:
                                      par * 2048 + (r + 1) * 1024]
                    v = op("act", lambda e, d=dst, p=sp: e.copy(
                        out=d, in_=p[0:64, 0:1024]),
                        waits=[(sP, pv), (sP, ms.get(("scores", t - 1, 3), 0))],
                        inc=sA, nosync=True)
                    psum_free[key] = (sA, v)
                ms[("qT", t) if src_name == "q" else ("kT", t)] = v

        def a_scores(t, g):
            par, prv = t % 2, 1 - (t % 2)
            tvar = 1 if (t % TPS == 0) else 0
            psX = psA if g % 2 == 0 else psB
            pfk = "A" if g % 2 == 0 else "B"
            w0 = [(sA, ms[("qT", t)]), (sA, ms[("kT", t)]),
                  psum_free[pfk], (dC, ms["consts_ld"])]
            pv = None
            first = True
            for half in range(2):
                wh = w0
                for hh in (half * 2, half * 2 + 1):
                    h = 4 * g + hh
                    for kt, slot in ((0, prv), (1, par)):
                        pv = op("pe", lambda e, p=psX, hh_=hh, kt_=kt,
                                h_=h, sl=slot: e.matmul(
                            out=p[:, hh_ * 256 + kt_ * 128:
                                  hh_ * 256 + (kt_ + 1) * 128],
                            lhsT=qT[0:64, h_ * 128:(h_ + 1) * 128],
                            rhs=kT_hist[0:64, sl * 2048 + h_ * 128:
                                        sl * 2048 + (h_ + 1) * 128],
                            start=(hh_ % 2 == 0 and kt_ == 0), stop=False),
                            waits=(wh if first else []), inc=None)
                        first = False
                for hh in (half * 2, half * 2 + 1):
                    last = (hh == half * 2 + 1)
                    pv = op("pe", lambda e, p=psX, hh_=hh, mv=tvar, l=last:
                            e.matmul(
                                out=p[:, hh_ * 256:(hh_ + 1) * 256],
                                lhsT=identb,
                                rhs=masks[:, mv * 256:(mv + 1) * 256],
                                start=False, stop=l),
                            inc=(sP if last else None))
                if half == 0:
                    ms[("sc_h0", t, g)] = pv
            ms[("scores", t, g)] = pv

        def a_sm_max(t, g):
            cl = cells[t % 2]
            psX = psA if g % 2 == 0 else psB
            v0 = op("dve", lambda e, p=psX, c=cl, g_=g: e.tensor_reduce(
                out=c[:, CEL_NMX + 4 * g_:CEL_NMX + 4 * g_ + 2],
                in_=p[:, 0:512].rearrange("p (h k) -> p h k", h=2),
                axis=AX.X, op=ALU.max, negate=True),
                waits=[(sP, ms[("sc_h0", t, g)])], inc=sV, nosync=True)
            v = op("dve", lambda e, p=psX, c=cl, g_=g: e.tensor_reduce(
                out=c[:, CEL_NMX + 4 * g_ + 2:CEL_NMX + 4 * g_ + 4],
                in_=p[:, 512:1024].rearrange("p (h k) -> p h k", h=2),
                axis=AX.X, op=ALU.max, negate=True),
                waits=[(sP, ms[("scores", t, g)])], inc=sV, nosync=True)
            ms[("nmx0", t, g)] = v0
            ms[("nmx", t, g)] = v

        def a_sm_exp(t, g):
            cl = cells[t % 2]
            psX = psA if g % 2 == 0 else psB
            pfk = "A" if g % 2 == 0 else "B"
            va = None
            evl = []
            for hh in range(4):
                wv = [(sV, ms[("nmx0", t, g)] if hh < 2
                       else ms[("nmx", t, g)])]
                if g >= 2:
                    wv.append((sP, ms.get(("pT_mm", t, g - 2), 0)))
                else:
                    wv.append((sP, ms.get(("pT_mm", t - 1, g + 2), 0)))
                va = op("act", lambda e, p=psX, c=cl, g_=g, hh_=hh:
                        e.activation(
                    out=probs[g_ % 2][:, hh_ * 256:(hh_ + 1) * 256],
                    in_=p[:, hh_ * 256:(hh_ + 1) * 256], func=AF.Exp,
                    bias=c[:, CEL_NMX + 4 * g_ + hh_:
                           CEL_NMX + 4 * g_ + hh_ + 1],
                    accum_out=c[:, CEL_SUMS + 4 * g_ + hh_:
                                CEL_SUMS + 4 * g_ + hh_ + 1]),
                    waits=(wv if hh in (0, 2) else []),
                    inc=sA, nosync=(hh > 0))
                evl.append(va)
            ms[("exp", t, g)] = va
            ms[("expv", t, g)] = evl
            psum_free[pfk] = (sA, va)

        def a_sm_norm(t, g):
            cl = cells[t % 2]
            ev = ms[("expv", t, g)]
            r0 = op("dve", lambda e, c=cl, g_=g: e.reciprocal(
                out=c[:, CEL_RCP + 4 * g_:CEL_RCP + 4 * g_ + 2],
                in_=c[:, CEL_SUMS + 4 * g_:CEL_SUMS + 4 * g_ + 2]),
                waits=[(sA, ev[1])], inc=sV, nosync=True)
            rc = r0
            v2 = None
            for hh in range(4):
                if hh == 2:
                    rc = op("dve", lambda e, c=cl, g_=g: e.reciprocal(
                        out=c[:, CEL_RCP + 4 * g_ + 2:CEL_RCP + 4 * g_ + 4],
                        in_=c[:, CEL_SUMS + 4 * g_ + 2:
                               CEL_SUMS + 4 * g_ + 4]),
                        waits=[(sA, ev[3])], inc=sV, nosync=True)
                v2 = op("dve", lambda e, c=cl, g_=g, hh_=hh: e.tensor_scalar(
                    out=probsn[g_ % 2][:, hh_ * 256:(hh_ + 1) * 256],
                    in0=probs[g_ % 2][:, hh_ * 256:(hh_ + 1) * 256],
                    scalar1=c[:, CEL_RCP + 4 * g_ + hh_:
                              CEL_RCP + 4 * g_ + hh_ + 1],
                    scalar2=None, op0=ALU.mult),
                    waits=[(sV, rc)], inc=sV, nosync=True)
                if hh == 1:
                    ms[("pn0", t, g)] = v2
            ms[("pn", t, g)] = v2

        def a_pt(t, g):
            sp, key = next_stage()
            pv = None
            for j in range(8):
                if j == 0:
                    w = [(sV, ms[("pn0", t, g)]), psum_free[key]]
                elif j == 4:
                    w = [(sV, ms[("pn", t, g)])]
                else:
                    w = []
                pv = op("pe", lambda e, jj=j, g_=g, p=sp: e.transpose(
                    out=p[:, jj * 128:(jj + 1) * 128],
                    in_=probsn[g_ % 2][:, jj * 128:(jj + 1) * 128],
                    identity=identb),
                    waits=w, inc=(sP if j == 7 else None))
            ms[("pT_mm", t, g)] = pv
            wv = [(sP, pv)]
            if g >= 1:
                wv.append((sP, ms.get(("av", t, g - 1), 0)))
            else:
                wv.append((sP, ms.get(("av", t - 1, 3), 0)))
            v = op("act", lambda e, g_=g, p=sp: e.activation(
                out=probsT[g_ % 2], in_=p, func=AF.Copy), waits=wv, inc=sA,
                nosync=True)
            psum_free[key] = (sA, v)
            ms[("pT", t, g)] = v

        def a_av(t, g):
            par, prv = t % 2, 1 - (t % 2)
            pv = None
            for hh in range(4):
                h = 4 * g + hh
                bp = 64 * (h % 2)
                p = h // 2
                for kt, slot in ((0, prv), (1, par)):
                    wv = [(sA, ms[("pT", t, g)]), (sV, ms[("v_sb", t)])]
                    if g == 0 and hh == 0 and kt == 0:
                        wv.append(psum_free["C"])
                    pv = op("pe", lambda e, hh_=hh, h_=h, bp_=bp, pp=p,
                            kt_=kt, sl=slot, g_=g: e.matmul(
                        out=psC[bp_:bp_ + 64, pp * 128:(pp + 1) * 128],
                        lhsT=v_hist[:, sl * D + h_ * 64:
                                    sl * D + h_ * 64 + 64],
                        rhs=probsT[g_ % 2][:, (hh_ * 2 + kt_) * 128:
                                           (hh_ * 2 + kt_ + 1) * 128],
                        start=(kt_ == 0), stop=(kt_ == 1),
                        tile_position=(0, bp_)),
                        waits=(wv if kt == 0 else []),
                        inc=(sP if (hh == 3 and kt == 1) else None))
            ms[("av", t, g)] = pv

        def a_outproj(t):
            v = op("dve", lambda e: e.tensor_copy(out=attnout, in_=psC),
                   waits=[(sP, ms[("av", t, 3)]),
                          (sP, ms.get(("outproj", t - 1), 0))], inc=sV,
                   nosync=True)
            ms[("attnout", t)] = v
            psum_free["C"] = (sV, v)
            pv = None
            for n in range(2):
                for c in range(DC):
                    wv = [(sV, ms[("attnout", t)]), psum_free["C"],
                          (sV, ms["wcast_v"]), (sA, ms["wcast_a"])]
                    pv = op("pe", lambda e, cc=c, nn=n: e.matmul(
                        out=psC[:, nn * 512:(nn + 1) * 512],
                        lhsT=attnout[:, cc * 128:(cc + 1) * 128],
                        rhs=wout_sb[:, cc * D + nn * 512:cc * D + (nn + 1) * 512],
                        start=(cc == 0), stop=(cc == DC - 1)),
                        waits=(wv if (n == 0 and c == 0) else []),
                        inc=(sP if (n == 1 and c == DC - 1) else None))
            ms[("outproj", t)] = pv

        def a_res(t):
            par = t % 2
            v = op("dve", lambda e, x=xin[par], o=x1buf[par]: e.tensor_tensor(
                out=o, in0=psC, in1=x, op=ALU.add),
                waits=[(sP, ms[("outproj", t)]), (dS, spill_ms.get(t - 2, 0))],
                inc=sV, nosync=True)
            ms[("res", t)] = v
            psum_free["C"] = (sV, v)
            spill_ms[t] = dma("syncq", x1_d[t * T:(t + 1) * T, :], x1buf[par],
                              dS, waits=[(sV, v)])

        def a_ln2_acc(t):
            par = t % 2
            cl = cells[t % 2]
            va = op("act", lambda e, o=x1buf[par], c=cl: e.activation(
                out=outbuf, in_=o, func=AF.Square,
                accum_out=c[:, CEL2:CEL2 + 1]),
                waits=[(sV, ms[("res", t)])], inc=sA, nosync=True)
            va = op("act", lambda e, o=x1buf[par], c=cl: e.activation(
                out=outbuf, in_=o, func=AF.Identity,
                accum_out=c[:, CEL2 + 1:CEL2 + 2]), inc=sA)
            ms[("ln2a", t)] = va

        def a_ln2_fin(t):
            cl = cells[t % 2]
            op("dve", lambda e, c=cl, t_=t: e.tensor_scalar(
                out=statsN[:, t_:t_ + 1], in0=c[:, CEL2 + 1:CEL2 + 2],
                scalar1=-1.0 / D, scalar2=None, op0=ALU.mult),
                waits=[(sA, ms[("ln2a", t)])], nosync=True)
            op("dve", lambda e, c=cl, t_=t: e.tensor_tensor(
                out=c[:, CEL2 + 2:CEL2 + 3], in0=statsN[:, t_:t_ + 1],
                in1=statsN[:, t_:t_ + 1], op=ALU.mult))
            v = op("dve", lambda e, c=cl: e.scalar_tensor_tensor(
                out=c[:, CEL2 + 3:CEL2 + 4], in0=c[:, CEL2:CEL2 + 1],
                scalar=1.0 / D, in1=c[:, CEL2 + 2:CEL2 + 3],
                op0=ALU.mult, op1=ALU.subtract), inc=sV)
            op("act", lambda e, c=cl: e.activation(
                out=c[:, CEL2 + 4:CEL2 + 5], in_=c[:, CEL2 + 3:CEL2 + 4],
                func=AF.Ln, bias=c_eps), waits=[(sV, v)])
            va = op("act", lambda e, c=cl, t_=t: e.activation(
                out=statsR[:, t_:t_ + 1], in_=c[:, CEL2 + 4:CEL2 + 5],
                func=AF.Exp, scale=-0.5), inc=sA)
            ms[("ln2", t)] = va

        gtile = 0
        a_ln1_stats(0)
        a_ln1_apply(0)
        a_xlnT(0)
        a_qkv(0, 0)
        a_qkv(0, 1)
        a_qkv(0, 2)
        for t in range(nt):
            fence(gtile)
            gtile += 1
            ptasks = list(prep_by_tile[t])

            def prep_drip(k):
                for _ in range(k):
                    if ptasks:
                        prep_emit(ptasks.pop(0))

            if t + 1 < nt:
                a_ln1_stats(t + 1)
            if t == 0:
                a_qkT(0)
            a_scores(t, 0)
            a_sm_max(t, 0)
            a_sm_exp(t, 0)
            if t > 0:
                a_ln2_fin(t - 1)
            a_scores(t, 1)
            a_sm_max(t, 1)
            a_sm_exp(t, 1)
            a_sm_norm(t, 0)
            a_scores(t, 2)
            a_sm_max(t, 2)
            prep_drip(2)
            a_pt(t, 0)
            a_av(t, 0)
            prep_flushT()
            a_sm_exp(t, 2)
            a_sm_norm(t, 1)
            if t + 1 < nt:
                a_ln1_apply(t + 1)
            a_scores(t, 3)
            a_sm_max(t, 3)
            prep_drip(2)
            a_pt(t, 1)
            a_av(t, 1)
            prep_flushT()
            a_sm_exp(t, 3)
            a_sm_norm(t, 2)
            if t + 1 < nt:
                a_xlnT(t + 1)
            prep_drip(2)
            a_pt(t, 2)
            a_av(t, 2)
            prep_flushT()
            if t + 1 < nt:
                a_qkv(t + 1, 0)
            a_sm_norm(t, 3)
            a_pt(t, 3)
            a_av(t, 3)
            prep_drip(2)
            if t + 1 < nt:
                a_qkv(t + 1, 1)
                a_qkT(t + 1)
            a_outproj(t)
            a_res(t)
            if t + 1 < nt:
                a_qkv(t + 1, 2)
            a_ln2_acc(t)
            if t + 2 < nt:
                xin_ld[t + 2] = dma(
                    "syncq", xin[t % 2], xc[(t + 2) * T:(t + 3) * T, :],
                    dX, waits=[(sV, ms[("res", t)]), (sA, ms[("xln", t)])])
            prep_drip(len(ptasks))
            prep_flushT()
            tile_end()
        a_ln2_fin(nt - 1)

        a_end = {ek: ENG_SEM[ek].n for ek in ENG_SEM}
        for ek in ("pe", "dve", "act", "pool", "syncq"):
            waits = [(ENG_SEM[o], a_end[o]) for o in ENG_SEM if o != ek]
            for s in (dS, dB[0], dB[1], dP[0], dP[1], dW[0], dW[1], dC, dY, dX):
                waits.append((s, s.n))
            op(ek, None, waits=waits)

        w2qt_lds = []
        for j in range(4):
            w2qt_lds.append(dma(
                "act", W2qT[:, j * 8192:(j + 1) * 8192],
                w2t_d[:, j * 8192:(j + 1) * 8192], dW[j % 2],
                waits=([(dB[0], dB[0].n), (dB[1], dB[1].n)] if j < 2 else [])))
        ms["w2qt0"] = w2qt_lds[2]
        ms["w2qt1"] = w2qt_lds[3]

        x1_ld = {}
        out_ms = {}
        FQ = 0
        FH = 12
        cbb = tib.bitcast(F32)

        def x1s(t):
            return (x1buf[0], x1buf[1], x1c)[t % 3]

        def emit_round(src_ap, dst_ap, scale_cell, bias_cell, bias_imm, nsub,
                       first_wait, war_wait, r0=0):
            subs = []
            prev_cb = 0
            for r in range(r0, r0 + nsub):
                wv = (list(first_wait) + list(war_wait)) if r == r0 else []
                sc2 = bias_cell if bias_cell is not None else bias_imm
                op("dve", lambda e, rr=r, s=src_ap, sc=scale_cell,
                   b=sc2: e.tensor_scalar(
                    out=t1b, in0=s[:, rr * 1024:(rr + 1) * 1024],
                    scalar1=sc, scalar2=b, op0=ALU.mult, op1=ALU.add),
                    waits=wv + [(sV, prev_cb)], nosync=True)
                op("dve", lambda e: e.tensor_copy(out=tib, in_=t1b))
                op("dve", lambda e: e.tensor_copy(out=tfb, in_=tib))
                prev_cb = op("dve", lambda e: e.tensor_tensor(
                    out=cbb, in0=t1b, in1=tfb, op=ALU.is_ge), inc=sV)
                rl = op("dve", lambda e, rr=r, d=dst_ap: e.scalar_tensor_tensor(
                    out=d[:, rr * 1024:(rr + 1) * 1024], in0=tfb,
                    scalar=-641.0, in1=cbb, op0=ALU.add, op1=ALU.add), inc=sV)
                subs.append(rl)
            return subs

        for t in range(min(3, nt)):
            x1_ld[t] = dma("syncq", x1s(t), x1_d[t * T:(t + 1) * T, :],
                           dL, waits=[(dS, dS.n)])

        def emit_w1_quarter(t, j):
            cl = cells[t % 2]
            psX = psA if j % 2 == 0 else psB
            pfk = "A" if j % 2 == 0 else "B"
            wv = [(sA, ms[("xqT", t)]), (sA, ms[("i1r", t)]),
                  psum_free[pfk], (dY, ms["bias_ld"])]
            pv = None
            for n in range(2):
                pv = op("pe", lambda e, jj=j, nn=n, p=psX: e.matmul(
                    out=p[:, nn * 512:(nn + 1) * 512], lhsT=invs1row,
                    rhs=b1bf[0:1, jj * 1024 + nn * 512:
                             jj * 1024 + (nn + 1) * 512],
                    start=True, stop=False),
                    waits=(wv if n == 0 else []), inc=None)
                for c in range(DC):
                    pv = op("pe", lambda e, jj=j, nn=n, cc=c, p=psX:
                            e.matmul(
                        out=p[:, nn * 512:(nn + 1) * 512],
                        lhsT=xqT[:, cc * 128:(cc + 1) * 128],
                        rhs=W1qT[:, cc * HF + jj * 1024 + nn * 512:
                                 cc * HF + jj * 1024 + (nn + 1) * 512],
                        start=False, stop=(cc == DC - 1)),
                        inc=(sP if (n == 1 and c == DC - 1) else None))
            ms[("w1", t, j)] = pv

        def _hqs(t, j):
            l = ms.get(("hqsub", t))
            return 0 if l is None else l[j]

        def emit_gelu(t, j):
            cl = cells[t % 2]
            psX = psA if j % 2 == 0 else psB
            pfk = "A" if j % 2 == 0 else "B"
            va = op("act", lambda e, jj=j, p=psX, c=cl: e.activation(
                out=h_sb[:, jj * 1024:(jj + 1) * 1024], in_=p,
                func=AF.Gelu, scale=c[:, FQ + 9:FQ + 10]),
                waits=[(sP, ms[("w1", t, j)]), (sV, _hqs(t - 1, j)),
                       (sV, ms[("fq", t)])], inc=sA, nosync=True)
            ms[("gelu", t, j)] = va
            psum_free[pfk] = (sA, va)
            op("dve", lambda e, jj=j, c=cl: e.tensor_reduce(
                out=c[:, FH + jj:FH + jj + 1],
                in_=h_sb[:, jj * 1024:(jj + 1) * 1024], axis=AX.X,
                op=ALU.max, apply_absolute_value=True),
                waits=[(sA, va)], nosync=True)

        def f_front_dve(t):
            cl = cells[t % 2]
            op("dve", lambda e, x=x1s(t), c=cl: e.tensor_reduce(
                out=c[:, FQ:FQ + 1], in_=x, axis=AX.X, op=ALU.max),
                waits=[(dL, x1_ld[t]), (sA, ms.get(("gelu", t - 2, 3), 0)),
                       (sP, ms.get(("sT1", t - 2), 0))], nosync=True)
            op("dve", lambda e, x=x1s(t), c=cl: e.tensor_reduce(
                out=c[:, FQ + 1:FQ + 2], in_=x, axis=AX.X, op=ALU.min),
                waits=[(dL, x1_ld[t])], nosync=True)
            op("dve", lambda e, c=cl, t_=t: e.tensor_tensor(
                out=c[:, FQ + 2:FQ + 3], in0=c[:, FQ:FQ + 1],
                in1=statsN[:, t_:t_ + 1], op=ALU.add))
            op("dve", lambda e, c=cl, t_=t: e.tensor_tensor(
                out=c[:, FQ + 3:FQ + 4], in0=c[:, FQ + 1:FQ + 2],
                in1=statsN[:, t_:t_ + 1], op=ALU.add))
            op("dve", lambda e, c=cl: e.tensor_scalar(
                out=c[:, FQ + 4:FQ + 5], in0=c[:, FQ + 3:FQ + 4],
                scalar1=-1.0, scalar2=None, op0=ALU.mult))
            op("dve", lambda e, c=cl: e.tensor_tensor(
                out=c[:, FQ + 5:FQ + 6], in0=c[:, FQ + 2:FQ + 3],
                in1=c[:, FQ + 4:FQ + 5], op=ALU.max))
            op("dve", lambda e, c=cl: e.reciprocal(
                out=c[:, FQ + 6:FQ + 7], in_=c[:, FQ + 5:FQ + 6]))
            op("dve", lambda e, c=cl: e.tensor_scalar(
                out=c[:, FQ + 7:FQ + 8], in0=c[:, FQ + 6:FQ + 7],
                scalar1=127.0, scalar2=None, op0=ALU.mult))
            op("dve", lambda e, c=cl, t_=t: e.tensor_tensor(
                out=c[:, FQ + 8:FQ + 9], in0=c[:, FQ + 5:FQ + 6],
                in1=statsR[:, t_:t_ + 1], op=ALU.mult))
            op("dve", lambda e, c=cl: e.tensor_scalar(
                out=c[:, FQ + 9:FQ + 10], in0=c[:, FQ + 8:FQ + 9],
                scalar1=bc_ws1, scalar2=None, op0=ALU.mult))
            op("dve", lambda e, c=cl: e.reciprocal(
                out=c[:, FQ + 10:FQ + 11], in_=c[:, FQ + 9:FQ + 10]))
            v = op("dve", lambda e, c=cl, t_=t: e.tensor_scalar(
                out=c[:, FQ + 11:FQ + 12], in0=statsN[:, t_:t_ + 1],
                scalar1=c[:, FQ + 7:FQ + 8], scalar2=RK,
                op0=ALU.mult, op1=ALU.add), inc=sV)
            ms[("fq", t)] = v

            rl = emit_round(x1s(t), xq,
                            cl[:, FQ + 7:FQ + 8], cl[:, FQ + 11:FQ + 12], None,
                            1, [(sV, ms[("fq", t)])],
                            [(sP, ms.get(("xqT_mm", t - 1), 0))])
            ms[("xq", t)] = rl[-1]

        def f_front_pe(t):
            cl = cells[t % 2]
            pv = op("pe", lambda e, c=cl: e.transpose(
                out=psF[0:1, 0:128], in_=c[:, FQ + 10:FQ + 11],
                identity=identf),
                waits=[(sV, ms[("fq", t)]), psum_free["F"],
                       (dC, ms["consts_ld"]),
                       (sP, ms.get(("w1", t - 1, 3), 0))], inc=sP)
            ms[("sT1", t)] = pv
            va = op("act", lambda e: e.activation(
                out=invs1row, in_=psF[0:1, 0:128], func=AF.Copy),
                waits=[(sP, pv)], inc=sA, nosync=True)
            ms[("i1r", t)] = va
            psum_free["F"] = (sA, va)

            sp, key = next_stage()
            pv = None
            for c in range(DC):
                pv = op("pe", lambda e, cc=c, p=sp: e.transpose(
                    out=p[:, cc * 128:(cc + 1) * 128],
                    in_=xq[:, cc * 128:(cc + 1) * 128], identity=identb),
                    waits=([(sV, ms[("xq", t)]), psum_free[key]]
                           if c == 0 else []),
                    inc=(sP if c == DC - 1 else None))
            ms[("xqT_mm", t)] = pv
            v = op("act", lambda e, p=sp: e.activation(
                out=xqT, in_=p, func=AF.Copy),
                waits=[(sP, pv), (sP, ms.get(("w1", t - 1, 3), 0))], inc=sA,
                nosync=True)
            psum_free[key] = (sA, v)
            ms[("xqT", t)] = v

            emit_w1_quarter(t, 0)
            emit_w1_quarter(t, 1)

        def f_fh(t):
            cl = cells[t % 2]
            op("dve", lambda e, c=cl: e.tensor_reduce(
                out=c[:, FH + 4:FH + 5], in_=c[:, FH:FH + 4], axis=AX.X,
                op=ALU.max))
            op("dve", lambda e, c=cl: e.reciprocal(
                out=c[:, FH + 8:FH + 9], in_=c[:, FH + 4:FH + 5]))
            op("dve", lambda e, c=cl: e.tensor_scalar(
                out=c[:, FH + 5:FH + 6], in0=c[:, FH + 8:FH + 9],
                scalar1=127.0, scalar2=None, op0=ALU.mult))
            op("dve", lambda e, c=cl: e.tensor_scalar(
                out=c[:, FH + 6:FH + 7], in0=c[:, FH + 4:FH + 5],
                scalar1=bc_ws2, scalar2=None, op0=ALU.mult))
            v = op("dve", lambda e, c=cl: e.reciprocal(
                out=c[:, FH + 7:FH + 8], in_=c[:, FH + 6:FH + 7]), inc=sV)
            ms[("fh", t)] = v

        def f_round0(t):
            cl = cells[t % 2]
            rl = emit_round(h_sb, hq, cl[:, FH + 5:FH + 6], None, RK, 1,
                            [(sV, ms[("fh", t)])],
                            [(sP, ms.get(("hqT_mm", t - 1), 0))])
            ms[("hqsub", t)] = rl

        def f_rounds123(t):
            cl = cells[t % 2]
            rl = emit_round(h_sb, hq, cl[:, FH + 5:FH + 6], None, RK, 3,
                            [], [], r0=1)
            ms[("hq", t)] = rl[-1]
            ms[("hqsub", t)] = ms[("hqsub", t)] + rl

        def f_back_T0(t):
            cl = cells[t % 2]
            pv = op("pe", lambda e, c=cl: e.transpose(
                out=psF[0:1, 128:256], in_=c[:, FH + 7:FH + 8],
                identity=identf),
                waits=[(sV, ms[("fh", t)]), psum_free["F"],
                       (sP, ms.get(("w2", t - 1), 0))], inc=sP)
            va = op("act", lambda e: e.activation(
                out=invs2row, in_=psF[0:1, 128:256], func=AF.Copy),
                waits=[(sP, pv)], inc=sA, nosync=True)
            ms[("i2r", t)] = va
            psum_free["F"] = (sA, va)
            _fT(t, (0,))

        def f_out(t):
            cl = cells[t % 2]
            v = op("dve", lambda e, c=cl, x=x1s(t): e.scalar_tensor_tensor(
                out=outbuf, in0=psC, scalar=c[:, FH + 6:FH + 7], in1=x,
                op0=ALU.mult, op1=ALU.add),
                waits=[(sP, ms[("w2", t)]), (dO, out_ms.get(t - 1, 0)),
                       (sV, ms[("fh", t)])],
                inc=sV, nosync=True)
            ms[("fout", t)] = v
            psum_free["C"] = (sV, v)
            out_ms[t] = dma("pool", y_d[t * T:(t + 1) * T, :], outbuf, dO,
                            waits=[(sV, v)])
            if t + 3 < nt:
                x1_ld[t + 3] = dma("syncq", x1s(t + 3),
                                   x1_d[(t + 3) * T:(t + 4) * T, :], dL,
                                   waits=[(sV, v)])

        def _fT(t, rrs):
            wv = [(sA, ms[("i2r", t)]),
                  (sV, ms.get(("fout", t - 1), 0)), (dW[0], ms["w2qt0"]),
                  (dW[1], ms["w2qt1"]), (dY, ms["bias_ld"])]
            pv = None
            evs = None
            for rr in rrs:
                sp, key = next_stage()
                for j in range(8):
                    pv = op("pe", lambda e, rr_=rr, jj=j, p=sp: e.transpose(
                        out=p[:, jj * 128:(jj + 1) * 128],
                        in_=hq[:, rr_ * 1024 + jj * 128:
                               rr_ * 1024 + (jj + 1) * 128],
                        identity=identb),
                        waits=([(sV, ms[("hqsub", t)][rr]), psum_free[key]]
                               if j == 0 else []),
                        inc=(sP if j == 7 else None))
                if rr == 3:
                    ms[("hqT_mm", t)] = pv
                evs = op("act", lambda e, rr_=rr, p=sp: e.activation(
                    out=hqT[:, rr_ * 1024:(rr_ + 1) * 1024], in_=p,
                    func=AF.Copy),
                    waits=[(sP, pv)] + ([(sP, ms.get(("w2", t - 1), 0))]
                                        if rr == 0 else []), inc=sA,
                    nosync=True)
                psum_free[key] = (sA, evs)
                for n in range(2):
                    if rr == 0:
                        op("pe", lambda e, nn=n: e.matmul(
                            out=psC[:, nn * 512:(nn + 1) * 512],
                            lhsT=invs2row,
                            rhs=b2bf[0:1, nn * 512:(nn + 1) * 512],
                            start=True, stop=False),
                            waits=(wv + [(sA, evs)]) if n == 0 else [],
                            inc=None)
                    for k in range(rr * 8, rr * 8 + 8):
                        pv = op("pe", lambda e, nn=n, kk=k: e.matmul(
                            out=psC[:, nn * 512:(nn + 1) * 512],
                            lhsT=hqT[:, kk * 128:(kk + 1) * 128],
                            rhs=W2qT[:, kk * 1024 + nn * 512:
                                     kk * 1024 + (nn + 1) * 512],
                            start=False, stop=(kk == 31)),
                            waits=([(sA, evs)] if (n == 0 and k == rr * 8)
                                   else []),
                            inc=(sP if (n == 1 and k == 31) else None))
            ms[("hqT", t)] = evs
            ms[("w2", t)] = pv

        f_front_dve(0)
        f_front_pe(0)
        if nt > 1:
            f_front_dve(1)
        emit_gelu(0, 0)
        emit_gelu(0, 1)
        emit_w1_quarter(0, 2)
        emit_gelu(0, 2)
        emit_w1_quarter(0, 3)
        emit_gelu(0, 3)
        f_fh(0)
        for t in range(nt):
            fence(gtile)
            gtile += 1
            f_round0(t)
            if t + 1 < nt:
                f_front_pe(t + 1)
            f_back_T0(t)
            f_rounds123(t)
            if t + 2 < nt:
                f_front_dve(t + 2)
            _fT(t, (1,))
            if t + 1 < nt:
                emit_gelu(t + 1, 0)
                emit_gelu(t + 1, 1)
                emit_w1_quarter(t + 1, 2)
            _fT(t, (2,))
            if t + 1 < nt:
                emit_gelu(t + 1, 2)
                emit_w1_quarter(t + 1, 3)
            _fT(t, (3,))
            if t + 1 < nt:
                emit_gelu(t + 1, 3)
                f_fh(t + 1)
            f_out(t)
            tile_end()

        op("syncq", None, waits=[(dO, dO.n)])
        _emit_all(nc, OPS, _ES, cut)

    return nc


def _emit_all(nc, OPS, _ES, cut=None):
    if cut is not None:
        OPS = OPS[:cut]
        tot = {}
        for (k, waits, fn, inc, amt, is_dma, ns) in OPS:
            if inc is not None:
                tot[inc.name] = tot.get(inc.name, 0) + amt
        OPS = [(k, [(s, min(v, tot.get(s.name, 0))) for (s, v) in waits],
                fn, inc, amt, is_dma, ns)
               for (k, waits, fn, inc, amt, is_dma, ns) in OPS]

    def mk(ek):
        zs = _ES.get(ek)

        def run(eng):
            lastw = {}
            zi = 0
            for (k, waits, fn, inc, amt, is_dma, ns) in OPS:
                if k != ek:
                    continue
                for (s, val) in waits:
                    if val > 0 and lastw.get(s.name, 0) < val:
                        eng.wait_ge(s.h, val)
                        lastw[s.name] = val
                if fn is None:
                    continue
                if zs is not None and zi > 0 and not ns:
                    eng.wait_ge(zs.h, zi)
                inst = fn(eng)
                if zs is not None and not is_dma:
                    zi += 1
                if inc is not None:
                    inst.then_inc(inc.h, amt)
        return run

    with nc.Block() as block:
        block.tensor(mk("pe"))
        block.vector(mk("dve"))
        block.scalar(mk("act"))
        block.gpsimd(mk("pool"))
        block.sync(mk("syncq"))


def _consts():
    identb = np.eye(128, dtype=ml_dtypes.bfloat16)
    identf = np.eye(128, dtype=np.float32)
    a = np.arange(128)[:, None]
    b = np.arange(128)[None, :]
    m_prev = np.where(b > a, 0.0, NEG).astype(np.float32)
    m_cur = np.where(b <= a, 0.0, NEG).astype(np.float32)
    normal = np.concatenate([m_prev, m_cur], axis=1)
    t0 = np.concatenate([np.full((128, 128), NEG, np.float32), m_cur], axis=1)
    masks = np.concatenate([normal, t0], axis=1).astype(ml_dtypes.bfloat16)
    return identb, identf, masks


_NC_CACHE = {}


def kernel(L0_attn_x, L0_attn_w_qkv, L0_attn_w_out, L0_ffn_W1, L0_ffn_b1,
           L0_ffn_W2, L0_ffn_b2):
    x = np.ascontiguousarray(np.asarray(L0_attn_x, dtype=np.float32))
    wqkv = np.ascontiguousarray(np.asarray(L0_attn_w_qkv, dtype=np.float32))
    wout = np.ascontiguousarray(np.asarray(L0_attn_w_out, dtype=np.float32))
    w1 = np.ascontiguousarray(np.asarray(L0_ffn_W1, dtype=np.float32))
    w2 = np.ascontiguousarray(np.asarray(L0_ffn_W2, dtype=np.float32))
    b1 = np.ascontiguousarray(np.asarray(L0_ffn_b1, dtype=np.float32))
    b2 = np.ascontiguousarray(np.asarray(L0_ffn_b2, dtype=np.float32))
    identb, identf, masks = _consts()

    nt = SEQ_PC * TPS
    if nt not in _NC_CACHE:
        _NC_CACHE[nt] = _build(nt)
    nc = _NC_CACHE[nt]

    in_maps = []
    for c in range(NCORES):
        xs = np.ascontiguousarray(
            x[c * SEQ_PC:(c + 1) * SEQ_PC].reshape(SEQ_PC * S, D))
        in_maps.append({
            "xc": xs, "wqkv": wqkv, "wout": wout, "w1": w1, "w2": w2,
            "b1": b1, "b2": b2, "identb": identb, "identf": identf,
            "masks": masks,
        })
    res = run_bass_kernel_spmd(nc, in_maps, core_ids=list(range(NCORES)))
    out = np.concatenate(
        [r["y"].reshape(SEQ_PC, S, D) for r in res.results], axis=0)
    return out
